# version 14
# baseline (speedup 1.0000x reference)
import numpy as np
import ml_dtypes
from contextlib import ExitStack
import concourse.bass as bass
import concourse.mybir as mybir
from concourse.bass_utils import run_bass_kernel_spmd

F32 = mybir.dt.float32
BF16 = mybir.dt.bfloat16
I32 = mybir.dt.int32
AF = mybir.ActivationFunctionType
ALU = mybir.AluOpType
AX = mybir.AxisListType
BF16NP = ml_dtypes.bfloat16

NC = 8
DM = 2048
DI = 4096
DS = DI // NC
NSS = 16
DTR = 128
KC = 4
TT = 16384
LB = 8192
SL = TT // NC
HID = 5632
NH = 16
DH = 128
NEXP = 16
KTOP = 1024

RMS_EPS = 1e-6
LN_EPS = 1e-5


RACECHECK = False

def _bass():
    return bass.Bass("TRN2", target_bir_lowering=False,
                     detect_race_conditions=RACECHECK)


def build_inproj():
    nc = _bass()
    KT = DM // 128
    MT = (2 * DS) // 128
    NO = 8
    NN = 4

    xT_d = nc.dram_tensor("xT", [DM, TT], F32, kind="ExternalInput").ap()
    win_d = nc.dram_tensor("win", [DM, 2 * DS], F32, kind="ExternalInput").ap()
    hid_d = nc.dram_tensor("hid", [DS, TT], F32, kind="ExternalOutput").ap()
    sg_d = nc.dram_tensor("sg", [DS, TT], F32, kind="ExternalOutput").ap()

    with ExitStack() as es:
        e = es.enter_context
        wst = e(nc.sbuf_tensor("wst", [128, 2 * DS], F32))
        wbf = [e(nc.sbuf_tensor(f"wbf{k}", [128, 2 * DS], BF16)) for k in range(KT)]
        xst = [e(nc.sbuf_tensor(f"xst{i}", [128, 2048], F32)) for i in range(2)]
        xbf = [e(nc.sbuf_tensor(f"xbf{k}", [128, 2048], BF16)) for k in range(KT)]
        ev = [e(nc.sbuf_tensor(f"ev{i}", [128, 512], F32)) for i in range(2)]
        ps = [e(nc.psum_tensor(f"ps{i}", [128, 512], F32)) for i in range(2)]
        wl = e(nc.semaphore("wl"))
        xl = [e(nc.semaphore(f"xl{i}")) for i in range(2)]
        ca = e(nc.semaphore("ca"))
        mm = e(nc.semaphore("mm"))
        evs = e(nc.semaphore("evs"))
        st = [e(nc.semaphore(f"st{i}")) for i in range(2)]
        blk = e(nc.Block())

        jobs = [(no, m, nn) for no in range(NO) for m in range(MT) for nn in range(NN)]

        @blk.sync
        def _(s):
            for k in range(KT):
                s.wait_ge(ca, k)
                s.dma_start(out=wst[:], in_=win_d[k * 128:(k + 1) * 128, :]).then_inc(wl, 16)
            nld = 0
            for no in range(NO):
                for k in range(KT):
                    s.wait_ge(ca, KT + nld - 1)
                    s.dma_start(
                        out=xst[nld % 2][:],
                        in_=xT_d[k * 128:(k + 1) * 128, no * 2048:(no + 1) * 2048],
                    ).then_inc(xl[nld % 2], 16)
                    nld += 1

        @blk.vector
        def _(v):
            for k in range(KT):
                v.wait_ge(wl, 16 * (k + 1))
                v.tensor_copy(out=wbf[k][:], in_=wst[:]).then_inc(ca, 1)
            nld = 0
            for no in range(NO):
                if no >= 1:
                    v.wait_ge(mm, no * MT * NN)
                for k in range(KT):
                    v.wait_ge(xl[nld % 2], 16 * (nld // 2 + 1))
                    v.tensor_copy(out=xbf[k][:], in_=xst[nld % 2][:]).then_inc(ca, 1)
                    nld += 1

        @blk.tensor
        def _(t):
            for j, (no, m, nn) in enumerate(jobs):
                if m == 0 and nn == 0:
                    t.wait_ge(ca, KT + (no + 1) * KT)
                if j >= 2:
                    t.wait_ge(evs, j - 1)
                for k in range(KT):
                    inst = t.matmul(
                        ps[j % 2][:],
                        wbf[k][:, m * 128:(m + 1) * 128],
                        xbf[k][:, nn * 512:(nn + 1) * 512],
                        start=(k == 0), stop=(k == KT - 1),
                    )
                inst.then_inc(mm, 1)

        @blk.scalar
        def _(sc):
            for j, (no, m, nn) in enumerate(jobs):
                sc.wait_ge(mm, j + 1)
                if j >= 2:
                    sc.wait_ge(st[j % 2], 16 * (j // 2))
                func = AF.Copy if m < MT // 2 else AF.Silu
                sc.activation(out=ev[j % 2][:], in_=ps[j % 2][:], func=func).then_inc(evs, 1)

        @blk.gpsimd
        def _(g):
            for j, (no, m, nn) in enumerate(jobs):
                g.wait_ge(evs, j + 1)
                if m < MT // 2:
                    dst = hid_d[m * 128:(m + 1) * 128,
                                no * 2048 + nn * 512: no * 2048 + (nn + 1) * 512]
                else:
                    mg = m - MT // 2
                    dst = sg_d[mg * 128:(mg + 1) * 128,
                               no * 2048 + nn * 512: no * 2048 + (nn + 1) * 512]
                g.dma_start(out=dst, in_=ev[j % 2][:]).then_inc(st[j % 2], 16)
            g.wait_ge(st[0], 16 * (len(jobs) // 2))
            g.wait_ge(st[1], 16 * (len(jobs) // 2))
    return nc


def build_conv():
    nc = _bass()
    DT4 = DS // 128
    CH = 4096
    NHF = LB // CH

    hid_d = nc.dram_tensor("hid", [DS, TT], F32, kind="ExternalInput").ap()
    convw_d = nc.dram_tensor("convw", [128, 4 * DT4], F32, kind="ExternalInput").ap()
    convb_d = nc.dram_tensor("convb", [128, DT4], F32, kind="ExternalInput").ap()
    u_d = nc.dram_tensor("u", [DS, TT], F32, kind="ExternalOutput").ap()
    ubf_d = nc.dram_tensor("ubf", [DS, TT], BF16, kind="ExternalOutput").ap()

    jobs = [(d, b, hf) for d in range(DT4) for b in range(2) for hf in range(NHF)]

    with ExitStack() as es:
        e = es.enter_context
        h = [e(nc.sbuf_tensor(f"h{i}", [128, CH + 4], F32)) for i in range(2)]
        acc = [e(nc.sbuf_tensor(f"acc{i}", [128, CH], F32)) for i in range(2)]
        uo = [e(nc.sbuf_tensor(f"uo{i}", [128, CH], F32)) for i in range(2)]
        ub = [e(nc.sbuf_tensor(f"ub{i}", [128, CH], BF16)) for i in range(2)]
        cw = e(nc.sbuf_tensor("cw", [128, 4 * DT4], F32))
        cb = e(nc.sbuf_tensor("cb", [128, DT4], F32))
        cst = e(nc.semaphore("cst"))
        hl = [e(nc.semaphore(f"hl{i}")) for i in range(2)]
        zi = e(nc.semaphore("zi"))
        vc = e(nc.semaphore("vc"))
        ac = e(nc.semaphore("ac"))
        cc = e(nc.semaphore("cc"))
        su = [e(nc.semaphore(f"su{i}")) for i in range(2)]
        sb = [e(nc.semaphore(f"sb{i}")) for i in range(2)]
        blk = e(nc.Block())

        @blk.sync
        def _(s):
            s.dma_start(out=cw[:], in_=convw_d[:]).then_inc(cst, 16)
            s.dma_start(out=cb[:], in_=convb_d[:]).then_inc(cst, 16)
            for j, (d, b, hf) in enumerate(jobs):
                if j >= 2:
                    s.wait_ge(vc, j - 1)
                t0 = b * LB + hf * CH
                if hf == 0:
                    s.wait_ge(zi, j + 1)
                    s.dma_start(out=h[j % 2][:, 3:3 + CH],
                                in_=hid_d[d * 128:(d + 1) * 128, t0:t0 + CH]).then_inc(hl[j % 2], 16)
                else:
                    s.dma_start(out=h[j % 2][:, 0:3 + CH],
                                in_=hid_d[d * 128:(d + 1) * 128, t0 - 3:t0 + CH]).then_inc(hl[j % 2], 16)
            s.wait_ge(su[0], 16 * (len(jobs) // 2))
            s.wait_ge(su[1], 16 * (len(jobs) // 2))
            s.wait_ge(sb[0], 16 * (len(jobs) // 2))
            s.wait_ge(sb[1], 16 * (len(jobs) // 2))

        @blk.gpsimd
        def _(g):
            for j, (d, b, hf) in enumerate(jobs):
                if hf == 0:
                    if j >= 2:
                        g.wait_ge(vc, j - 1)
                    g.memset(h[j % 2][:, 0:3], 0.0).then_inc(zi, 1)
                else:
                    g.engine_nop().then_inc(zi, 1)
                t0 = b * LB + hf * CH
                g.wait_ge(ac, j + 1)
                g.dma_start(out=u_d[d * 128:(d + 1) * 128, t0:t0 + CH],
                            in_=uo[j % 2][:]).then_inc(su[j % 2], 16)
                g.wait_ge(cc, j + 1)
                g.dma_start(out=ubf_d[d * 128:(d + 1) * 128, t0:t0 + CH],
                            in_=ub[j % 2][:]).then_inc(sb[j % 2], 16)

        @blk.vector
        def _(v):
            v.wait_ge(cst, 32)
            for j, (d, b, hf) in enumerate(jobs):
                v.wait_ge(hl[j % 2], 16 * (j // 2 + 1))
                if j >= 2:
                    v.wait_ge(ac, j - 1)
                hh = h[j % 2]
                aa = acc[j % 2]
                v.tensor_scalar_mul(aa[:], hh[:, 0:CH], cw[:, d * 4 + 0: d * 4 + 1])
                for tap in range(1, KC):
                    inst = v.scalar_tensor_tensor(
                        out=aa[:], in0=hh[:, tap:tap + CH],
                        scalar=cw[:, d * 4 + tap: d * 4 + tap + 1],
                        in1=aa[:], op0=ALU.mult, op1=ALU.add)
                inst.then_inc(vc, 1)
                v.wait_ge(ac, j + 1)
                if j >= 2:
                    v.wait_ge(sb[j % 2], 16 * (j // 2))
                v.tensor_copy(out=ub[j % 2][:], in_=uo[j % 2][:]).then_inc(cc, 1)

        @blk.scalar
        def _(sc):
            for j, (d, b, hf) in enumerate(jobs):
                sc.wait_ge(vc, j + 1)
                if j >= 2:
                    sc.wait_ge(su[j % 2], 16 * (j // 2))
                    sc.wait_ge(cc, j - 1)
                sc.activation(out=uo[j % 2][:], in_=acc[j % 2][:], func=AF.Silu,
                              bias=cb[:, d:d + 1]).then_inc(ac, 1)
    return nc


def build_xproj():
    nc = _bass()
    KT = DS // 128
    NNJ = TT // 512

    ubf_d = nc.dram_tensor("ubf", [DS, TT], BF16, kind="ExternalInput").ap()
    xpw_d = nc.dram_tensor("xpw", [DS, 160], F32, kind="ExternalInput").ap()
    pdt_d = nc.dram_tensor("pdt", [128, TT], F32, kind="ExternalOutput").ap()
    pbc_d = nc.dram_tensor("pbc", [32, TT], F32, kind="ExternalOutput").ap()

    with ExitStack() as es:
        e = es.enter_context
        wst = e(nc.sbuf_tensor("wst", [128, 160], F32))
        wbf = [e(nc.sbuf_tensor(f"wbf{k}", [128, 160], BF16)) for k in range(KT)]
        ust = [e(nc.sbuf_tensor(f"ust{i}", [128, 512], BF16)) for i in range(2 * KT)]
        ev1 = [e(nc.sbuf_tensor(f"ev1{i}", [128, 512], F32)) for i in range(2)]
        ev2 = [e(nc.sbuf_tensor(f"ev2{i}", [32, 512], F32)) for i in range(2)]
        ps1 = [e(nc.psum_tensor(f"ps1{i}", [128, 512], F32)) for i in range(2)]
        ps2 = [e(nc.psum_tensor(f"ps2{i}", [32, 512], F32)) for i in range(2)]
        wl = e(nc.semaphore("wl"))
        ca = e(nc.semaphore("ca"))
        ul = [e(nc.semaphore(f"ul{i}")) for i in range(2)]
        mm = e(nc.semaphore("mm"))
        evs = e(nc.semaphore("evs"))
        st = [e(nc.semaphore(f"st{i}")) for i in range(2)]
        blk = e(nc.Block())

        @blk.sync
        def _(s):
            for k in range(KT):
                s.wait_ge(ca, k)
                s.dma_start(out=wst[:], in_=xpw_d[k * 128:(k + 1) * 128, :]).then_inc(wl, 16)
            for nn in range(NNJ):
                if nn >= 2:
                    s.wait_ge(mm, nn - 1)
                for k in range(KT):
                    bufi = (nn % 2) * KT + k
                    s.dma_start(
                        out=ust[bufi][:],
                        in_=ubf_d[k * 128:(k + 1) * 128, nn * 512:(nn + 1) * 512],
                    ).then_inc(ul[nn % 2], 16)

        @blk.vector
        def _(v):
            for k in range(KT):
                v.wait_ge(wl, 16 * (k + 1))
                v.tensor_copy(out=wbf[k][:], in_=wst[:]).then_inc(ca, 1)
            for nn in range(NNJ):
                v.wait_ge(mm, nn + 1)
                if nn >= 2:
                    v.wait_ge(st[nn % 2], 32 * (nn // 2))
                v.tensor_copy(out=ev1[nn % 2][:], in_=ps1[nn % 2][:])
                v.tensor_copy(out=ev2[nn % 2][:], in_=ps2[nn % 2][:]).then_inc(evs, 1)

        @blk.tensor
        def _(t):
            t.wait_ge(ca, KT)
            for nn in range(NNJ):
                t.wait_ge(ul[nn % 2], 16 * KT * (nn // 2 + 1))
                if nn >= 2:
                    t.wait_ge(evs, nn - 1)
                for k in range(KT):
                    bufi = (nn % 2) * KT + k
                    t.matmul(ps1[nn % 2][:], wbf[k][:, 0:128], ust[bufi][:],
                             start=(k == 0), stop=(k == KT - 1))
                for k in range(KT):
                    bufi = (nn % 2) * KT + k
                    inst = t.matmul(ps2[nn % 2][:], wbf[k][:, 128:160], ust[bufi][:],
                                    start=(k == 0), stop=(k == KT - 1))
                inst.then_inc(mm, 1)

        @blk.scalar
        def _(sc):
            for nn in range(NNJ):
                sc.wait_ge(evs, nn + 1)
                sc.dma_start(out=pdt_d[:, nn * 512:(nn + 1) * 512], in_=ev1[nn % 2][:]).then_inc(st[nn % 2], 16)
                sc.dma_start(out=pbc_d[:, nn * 512:(nn + 1) * 512], in_=ev2[nn % 2][:]).then_inc(st[nn % 2], 16)
            sc.wait_ge(st[0], 32 * (NNJ // 2))
            sc.wait_ge(st[1], 32 * (NNJ // 2))
    return nc


ENG_KEYS = "vstgy"
NSLOT = 8


def run_serial(nc, es, ops):
    comp_sems = {k: es.enter_context(nc.semaphore(f"ser_{k}")) for k in ENG_KEYS}
    dma_sems = {k: [es.enter_context(nc.semaphore(f"dma_{k}{i}")) for i in range(NSLOT)]
                for k in ENG_KEYS}
    blk = es.enter_context(nc.Block())

    vkeys = [("c", k) for k in ENG_KEYS] + [("d", k, i) for k in ENG_KEYS for i in range(NSLOT)]
    tot = {vk: 0 for vk in vkeys}
    cum = []
    kinds = []
    norm_ops = []
    dma_n = {k: 0 for k in ENG_KEYS}
    for o in ops:
        if len(o) == 3:
            ek, inc, fn = o
            selfm, reads, writes = False, None, None
        elif len(o) == 4:
            ek, inc, fn, selfm = o
            reads, writes = None, None
        else:
            ek, inc, fn, reads, writes = o
            selfm = False
        if inc == 16:
            vk = ("d", ek, dma_n[ek] % NSLOT)
            dma_n[ek] += 1
            tot[vk] += 16
        else:
            vk = ("c", ek)
            tot[vk] += inc
        kinds.append(vk)
        cum.append(dict(tot))
        norm_ops.append((ek, inc, fn, selfm, reads, writes))

    n = len(norm_ops)
    last_writer = {}
    readers = {}
    last_barrier = -1
    op_waits = [None] * n
    for i, (ek, inc, fn, selfm, reads, writes) in enumerate(norm_ops):
        if reads is None:
            op_waits[i] = None
            last_barrier = i
            last_writer = {}
            readers = {}
            continue
        deps = set()
        if last_barrier >= 0:
            deps.add(last_barrier)
        for r in reads:
            j = last_writer.get(id(r))
            if j is not None:
                deps.add(j)
        for w in writes:
            j = last_writer.get(id(w))
            if j is not None:
                deps.add(j)
            for j in readers.get(id(w), ()):
                deps.add(j)
        vk = kinds[i]
        waits = {}
        for j in deps:
            if j == i:
                continue
            vkj = kinds[j]
            v = cum[j][vkj]
            if vkj == ("c", ek):
                pass
            if waits.get(vkj, 0) < v:
                waits[vkj] = v
        if vk[0] == "d":
            pv = cum[i][vk] - 16
            if pv > 0 and waits.get(vk, 0) < pv:
                waits[vk] = pv
        op_waits[i] = waits
        for r in reads:
            readers.setdefault(id(r), []).append(i)
        for w in writes:
            last_writer[id(w)] = i
            readers[id(w)] = [i]

    def sem_of(vk):
        return comp_sems[vk[1]] if vk[0] == "c" else dma_sems[vk[1]][vk[2]]

    def section(ek, eng):
        last = {vk: 0 for vk in vkeys}
        for i, (k2, inc, fn, selfm, reads, writes) in enumerate(norm_ops):
            if k2 != ek:
                continue
            vk_self = kinds[i]
            if op_waits[i] is None:
                pre = {vk: cum[i][vk] - (inc if vk == vk_self else 0) for vk in vkeys}
                for vk in vkeys:
                    if pre[vk] > last[vk]:
                        eng.wait_ge(sem_of(vk), pre[vk])
                        last[vk] = pre[vk]
            else:
                for vk, v in op_waits[i].items():
                    if v > last[vk]:
                        eng.wait_ge(sem_of(vk), v)
                        last[vk] = v
            if selfm:
                fn(eng, sem_of(vk_self), inc)
            else:
                inst = fn(eng)
                inst.then_inc(sem_of(vk_self), inc)

    @blk.vector
    def _(v):
        section("v", v)

    @blk.scalar
    def _(s):
        section("s", s)

    @blk.tensor
    def _(t):
        section("t", t)

    @blk.gpsimd
    def _(g):
        section("g", g)

    @blk.sync
    def _(y):
        section("y", y)
        for vk in vkeys:
            if tot[vk] and vk != ("c", "y"):
                y.wait_ge(sem_of(vk), tot[vk])


class LNHelper:

    def __init__(self, nc, es, ops, name, scratch_d, out_d, outbf_d,
                 ones128, ones1, eps_ap, MT=DM // 128, NN=SL // 512):
        e = es.enter_context
        self.nc, self.ops = nc, ops
        self.MT, self.NN = MT, NN
        self.scratch_d, self.out_d, self.outbf_d = scratch_d, out_d, outbf_d
        self.ones128, self.ones1, self.eps_ap = ones128, ones1, eps_ap
        self.sq = e(nc.sbuf_tensor(f"{name}_sq", [128, 512], F32))
        self.statx = e(nc.sbuf_tensor(f"{name}_sx", [1, SL], F32))
        self.statx2 = e(nc.sbuf_tensor(f"{name}_sx2", [1, SL], F32))
        self.arow = e(nc.sbuf_tensor(f"{name}_ar", [1, SL], F32))
        self.brow = e(nc.sbuf_tensor(f"{name}_br", [1, SL], F32))
        self.nt = [e(nc.sbuf_tensor(f"{name}_nt{i}", [128, 512], F32)) for i in range(2)]
        self.obf = e(nc.sbuf_tensor(f"{name}_obf", [128, 512], BF16))
        self.ps_st = e(nc.psum_tensor(f"{name}_pst", [1, 512], F32))
        self.ps_a = e(nc.psum_tensor(f"{name}_psa", [128, 512], F32))
        self.ps_b = e(nc.psum_tensor(f"{name}_psb", [128, 512], F32))
        A = ops.append
        A(("g", 1, lambda g: g.memset(self.statx[:], 0.0)))
        A(("g", 1, lambda g: g.memset(self.statx2[:], 0.0)))

    def accum(self, m, nn, raw_tile):
        A = self.ops.append
        sl = slice(nn * 512, (nn + 1) * 512)
        A(("t", 1, lambda t: t.matmul(self.ps_st[:], self.ones128[:], raw_tile[:],
                                      start=True, stop=True)))
        A(("v", 1, lambda v: v.tensor_add(self.statx[:, sl], self.statx[:, sl], self.ps_st[:])))
        A(("s", 1, lambda s: s.activation(out=self.sq[:], in_=raw_tile[:], func=AF.Square)))
        A(("t", 1, lambda t: t.matmul(self.ps_st[:], self.ones128[:], self.sq[:],
                                      start=True, stop=True)))
        A(("v", 1, lambda v: v.tensor_add(self.statx2[:, sl], self.statx2[:, sl], self.ps_st[:])))
        A(("g", 16, lambda g: g.dma_start(
            out=self.scratch_d[m * 128:(m + 1) * 128, sl], in_=raw_tile[:])))

    def finalize(self):
        A = self.ops.append
        n = 128 * self.MT
        A(("v", 1, lambda v: v.tensor_scalar_mul(self.statx[:], self.statx[:], 1.0 / n)))
        A(("v", 1, lambda v: v.tensor_scalar_mul(self.statx2[:], self.statx2[:], 1.0 / n)))
        A(("v", 1, lambda v: v.tensor_mul(self.arow[:], self.statx[:], self.statx[:])))
        A(("v", 1, lambda v: v.tensor_sub(self.statx2[:], self.statx2[:], self.arow[:])))
        A(("s", 1, lambda s: s.activation(out=self.statx2[:], in_=self.statx2[:],
                                          func=AF.Ln, bias=self.eps_ap)))
        A(("s", 1, lambda s: s.activation(out=self.arow[:], in_=self.statx2[:],
                                          func=AF.Exp, scale=-0.5)))
        A(("v", 1, lambda v: v.tensor_mul(self.brow[:], self.statx[:], self.arow[:])))
        A(("v", 1, lambda v: v.tensor_scalar_mul(self.brow[:], self.brow[:], -1.0)))

    def normalize(self):
        A = self.ops.append
        for nn in range(self.NN):
            sl = slice(nn * 512, (nn + 1) * 512)
            A(("t", 1, lambda t, sl=sl: t.matmul(self.ps_a[:], self.ones1[:],
                                                 self.arow[:, sl], start=True, stop=True)))
            A(("t", 1, lambda t, sl=sl: t.matmul(self.ps_b[:], self.ones1[:],
                                                 self.brow[:, sl], start=True, stop=True)))
            for m in range(self.MT):
                msl = slice(m * 128, (m + 1) * 128)
                A(("y", 16, lambda y, msl=msl, sl=sl: y.dma_start(
                    out=self.nt[0][:], in_=self.scratch_d[msl, sl])))
                A(("v", 1, lambda v: v.tensor_mul(self.nt[1][:], self.nt[0][:], self.ps_a[:])))
                A(("v", 1, lambda v: v.tensor_add(self.nt[1][:], self.nt[1][:], self.ps_b[:])))
                A(("g", 16, lambda g, msl=msl, sl=sl: g.dma_start(
                    out=self.out_d[msl, sl], in_=self.nt[1][:])))
                A(("v", 1, lambda v: v.tensor_copy(out=self.obf[:], in_=self.nt[1][:])))
                A(("g", 16, lambda g, msl=msl, sl=sl: g.dma_start(
                    out=self.outbf_d[msl, sl], in_=self.obf[:])))


def build_outproj():
    nc = _bass()
    KT = DI // 128
    MT = DM // 128
    NN = SL // 512

    y_dd = nc.dram_tensor("yslab", [DI, SL], BF16, kind="ExternalInput").ap()
    opw_d = nc.dram_tensor("opw", [DI, DM], F32, kind="ExternalInput").ap()
    x_dd = nc.dram_tensor("xslab", [DM, SL], F32, kind="ExternalInput").ap()
    ones128_d = nc.dram_tensor("ones128", [128, 1], F32, kind="ExternalInput").ap()
    ones1_d = nc.dram_tensor("ones1", [1, 128], F32, kind="ExternalInput").ap()
    epsv_d = nc.dram_tensor("epsv", [128, 1], F32, kind="ExternalInput").ap()
    x1_d = nc.dram_tensor("x1", [DM, SL], F32, kind="ExternalOutput").ap()
    x1bf_d = nc.dram_tensor("x1bf", [DM, SL], BF16, kind="ExternalOutput").ap()
    scr_d = nc.dram_tensor("scr", [DM, SL], F32, kind="Internal").ap()

    with ExitStack() as es:
        e = es.enter_context
        ones128 = e(nc.sbuf_tensor("ones128_s", [128, 1], F32))
        ones1 = e(nc.sbuf_tensor("ones1_s", [1, 128], F32))
        epsv = e(nc.sbuf_tensor("epsv_s", [128, 1], F32))
        rhs = [e(nc.sbuf_tensor(f"rhs{k}", [128, SL], BF16)) for k in range(KT)]
        wst = e(nc.sbuf_tensor("wst", [128, 128], F32))
        wbf = [e(nc.sbuf_tensor(f"wbf{i}", [128, 128], BF16)) for i in range(KT)]
        xt = e(nc.sbuf_tensor("xt", [128, 512], F32))
        raw = e(nc.sbuf_tensor("raw", [128, 512], F32))
        ps_mm = e(nc.psum_tensor("ps_mm", [128, 512], F32))

        ops = []
        A = ops.append
        A(("y", 16, lambda y: y.dma_start(out=ones128[:], in_=ones128_d[:])))
        A(("y", 16, lambda y: y.dma_start(out=ones1[:], in_=ones1_d[:])))
        A(("y", 16, lambda y: y.dma_start(out=epsv[:], in_=epsv_d[:])))
        for k in range(KT):
            A(("y", 16, lambda y, k=k: y.dma_start(
                out=rhs[k][:], in_=y_dd[k * 128:(k + 1) * 128, :])))
        ln = LNHelper(nc, es, ops, "ln1", scr_d, x1_d, x1bf_d, ones128, ones1,
                      epsv[0:1, 0:1])
        for m in range(MT):
            for k in range(KT):
                A(("y", 16, lambda y, k=k, m=m: y.dma_start(
                    out=wst[:], in_=opw_d[k * 128:(k + 1) * 128, m * 128:(m + 1) * 128])))
                A(("v", 1, lambda v, k=k: v.tensor_copy(out=wbf[k][:], in_=wst[:])))
            for nn in range(NN):
                sl = slice(nn * 512, (nn + 1) * 512)
                for k in range(KT):
                    A(("t", 1, lambda t, k=k, sl=sl: t.matmul(
                        ps_mm[:], wbf[k][:], rhs[k][:, sl],
                        start=(k == 0), stop=(k == KT - 1))))
                A(("y", 16, lambda y, m=m, sl=sl: y.dma_start(
                    out=xt[:], in_=x_dd[m * 128:(m + 1) * 128, sl])))
                A(("v", 1, lambda v: v.tensor_add(raw[:], xt[:], ps_mm[:])))
                ln.accum(m, nn, raw)
        ln.finalize()
        ln.normalize()
        run_serial(nc, es, ops)
    return nc


def build_ffn0():
    nc = _bass()
    KT = DM // 128
    HT = HID // 128
    NN = SL // 512

    x1_dd = nc.dram_tensor("x1", [DM, SL], F32, kind="ExternalInput").ap()
    x1bf_dd = nc.dram_tensor("x1bf", [DM, SL], BF16, kind="ExternalInput").ap()
    wg_d = nc.dram_tensor("wgT", [DM, HID], I32, kind="ExternalInput").ap()
    wu_d = nc.dram_tensor("wuT", [DM, HID], I32, kind="ExternalInput").ap()
    wd_d = nc.dram_tensor("wdT", [HID, DM], I32, kind="ExternalInput").ap()
    sg_d = nc.dram_tensor("sgcol", [128, HT], F32, kind="ExternalInput").ap()
    su_d = nc.dram_tensor("sucol", [128, HT], F32, kind="ExternalInput").ap()
    sd_d = nc.dram_tensor("sdcol", [128, KT], F32, kind="ExternalInput").ap()
    ones128_d = nc.dram_tensor("ones128", [128, 1], F32, kind="ExternalInput").ap()
    ones1_d = nc.dram_tensor("ones1", [1, 128], F32, kind="ExternalInput").ap()
    epsv_d = nc.dram_tensor("epsv", [128, 1], F32, kind="ExternalInput").ap()
    x2_d = nc.dram_tensor("x2", [DM, SL], F32, kind="ExternalOutput").ap()
    x2bf_d = nc.dram_tensor("x2bf", [DM, SL], BF16, kind="ExternalOutput").ap()
    scr_d = nc.dram_tensor("scr", [DM, SL], F32, kind="Internal").ap()

    SLH = SL // 2
    with ExitStack() as es:
        e = es.enter_context
        ones128 = e(nc.sbuf_tensor("ones128_s", [128, 1], F32))
        ones1 = e(nc.sbuf_tensor("ones1_s", [1, 128], F32))
        epsv = e(nc.sbuf_tensor("epsv_s", [128, 1], F32))
        sgc = e(nc.sbuf_tensor("sgc", [128, HT], F32))
        suc = e(nc.sbuf_tensor("suc", [128, HT], F32))
        sdc = e(nc.sbuf_tensor("sdc", [128, KT], F32))
        xk = [e(nc.sbuf_tensor(f"xk{k}", [128, SLH], BF16)) for k in range(KT)]
        hk = [e(nc.sbuf_tensor(f"hk{k}", [128, SLH], BF16)) for k in range(HT)]
        wstage = e(nc.sbuf_tensor("wstage", [128, 128], I32))
        wbf = [e(nc.sbuf_tensor(f"wbf{i}", [128, 128], BF16)) for i in range(HT)]
        upm = e(nc.sbuf_tensor("upm", [128, SLH], F32))
        gt = e(nc.sbuf_tensor("gt", [128, 512], F32))
        xt = e(nc.sbuf_tensor("xt", [128, 512], F32))
        dnt = e(nc.sbuf_tensor("dnt", [128, 512], F32))
        raw = e(nc.sbuf_tensor("raw", [128, 512], F32))
        ps_mm = e(nc.psum_tensor("ps_mm", [128, 512], F32))

        ops = []
        A = ops.append
        A(("y", 16, lambda y: y.dma_start(out=ones128[:], in_=ones128_d[:])))
        A(("y", 16, lambda y: y.dma_start(out=ones1[:], in_=ones1_d[:])))
        A(("y", 16, lambda y: y.dma_start(out=epsv[:], in_=epsv_d[:])))
        A(("y", 16, lambda y: y.dma_start(out=sgc[:], in_=sg_d[:])))
        A(("y", 16, lambda y: y.dma_start(out=suc[:], in_=su_d[:])))
        A(("y", 16, lambda y: y.dma_start(out=sdc[:], in_=sd_d[:])))

        def load_w(dram, k, m):
            A(("y", 16, lambda y, k=k, m=m, dram=dram: y.dma_start(
                out=wstage[:], in_=dram[k * 128:(k + 1) * 128, m * 128:(m + 1) * 128])))
            A(("s", 1, lambda s, k=k: s.activation(
                out=wbf[k][:], in_=wstage[:], func=AF.Copy, bias=-128.0)))

        ln = LNHelper(nc, es, ops, "ln2", scr_d, x2_d, x2bf_d, ones128, ones1,
                      epsv[0:1, 0:1])
        for half in range(2):
            h0 = half * SLH
            for k in range(KT):
                A(("y", 16, lambda y, k=k, h0=h0: y.dma_start(
                    out=xk[k][:], in_=x1bf_dd[k * 128:(k + 1) * 128, h0:h0 + SLH])))
            for m in range(HT):
                for k in range(KT):
                    load_w(wu_d, k, m)
                for nn in range(SLH // 512):
                    sl = slice(nn * 512, (nn + 1) * 512)
                    for k in range(KT):
                        A(("t", 1, lambda t, k=k, sl=sl: t.matmul(
                            ps_mm[:], wbf[k][:], xk[k][:, sl],
                            start=(k == 0), stop=(k == KT - 1))))
                    A(("s", 1, lambda s, m=m, sl=sl: s.activation(
                        out=upm[:, sl], in_=ps_mm[:], func=AF.Copy, scale=suc[:, m:m + 1])))
                for k in range(KT):
                    load_w(wg_d, k, m)
                for nn in range(SLH // 512):
                    sl = slice(nn * 512, (nn + 1) * 512)
                    for k in range(KT):
                        A(("t", 1, lambda t, k=k, sl=sl: t.matmul(
                            ps_mm[:], wbf[k][:], xk[k][:, sl],
                            start=(k == 0), stop=(k == KT - 1))))
                    A(("s", 1, lambda s, m=m, sl=sl: s.activation(
                        out=gt[:], in_=ps_mm[:], func=AF.Silu, scale=sgc[:, m:m + 1])))
                    A(("v", 1, lambda v, m=m, sl=sl: v.tensor_mul(
                        hk[m][:, sl], gt[:], upm[:, sl])))
            for m in range(KT):
                for k in range(HT):
                    load_w(wd_d, k, m)
                for nn in range(SLH // 512):
                    sl = slice(nn * 512, (nn + 1) * 512)
                    gnn = half * (SLH // 512) + nn
                    gsl = slice(h0 + nn * 512, h0 + (nn + 1) * 512)
                    for k in range(HT):
                        A(("t", 1, lambda t, k=k, sl=sl: t.matmul(
                            ps_mm[:], wbf[k][:], hk[k][:, sl],
                            start=(k == 0), stop=(k == HT - 1))))
                    A(("s", 1, lambda s, m=m: s.activation(
                        out=dnt[:], in_=ps_mm[:], func=AF.Copy, scale=sdc[:, m:m + 1])))
                    A(("y", 16, lambda y, m=m, gsl=gsl: y.dma_start(
                        out=xt[:], in_=x1_dd[m * 128:(m + 1) * 128, gsl])))
                    A(("v", 1, lambda v: v.tensor_add(raw[:], xt[:], dnt[:])))
                    ln.accum(m, gnn, raw)
        ln.finalize()
        ln.normalize()
        run_serial(nc, es, ops)
    return nc


def build_qkv():
    nc = _bass()
    KT = DM // 128
    MT = 3 * DM // 128
    NN = SL // 512

    x2bf_dd = nc.dram_tensor("x2bf", [DM, SL], BF16, kind="ExternalInput").ap()
    wqkv_d = nc.dram_tensor("wqkv", [DM, 3 * DM], F32, kind="ExternalInput").ap()
    wg_d = nc.dram_tensor("wg", [DM, NH], F32, kind="ExternalInput").ap()
    qkv_d = nc.dram_tensor("qkv", [3 * DM, SL], BF16, kind="ExternalOutput").ap()
    g_d = nc.dram_tensor("g", [NH, SL], F32, kind="ExternalOutput").ap()

    with ExitStack() as es:
        e = es.enter_context
        xk = [e(nc.sbuf_tensor(f"xk{k}", [128, SL], BF16)) for k in range(KT)]
        wst = e(nc.sbuf_tensor("wst", [128, 128], F32))
        wbf = [e(nc.sbuf_tensor(f"wbf{i}", [128, 128], BF16)) for i in range(KT)]
        wgst = e(nc.sbuf_tensor("wgst", [128, NH], F32))
        wgbf = [e(nc.sbuf_tensor(f"wgbf{i}", [128, NH], BF16)) for i in range(KT)]
        ob = e(nc.sbuf_tensor("ob", [128, 512], BF16))
        gb = e(nc.sbuf_tensor("gb", [NH, 512], F32))
        ps_mm = e(nc.psum_tensor("ps_mm", [128, 512], F32))
        ps_g = e(nc.psum_tensor("ps_g", [NH, 512], F32))

        ops = []
        A = ops.append
        for k in range(KT):
            A(("y", 16, lambda y, k=k: y.dma_start(
                out=xk[k][:], in_=x2bf_dd[k * 128:(k + 1) * 128, :])))
            A(("y", 16, lambda y, k=k: y.dma_start(
                out=wgst[:], in_=wg_d[k * 128:(k + 1) * 128, :])))
            A(("v", 1, lambda v, k=k: v.tensor_copy(out=wgbf[k][:], in_=wgst[:])))
        for nn in range(NN):
            sl = slice(nn * 512, (nn + 1) * 512)
            for k in range(KT):
                A(("t", 1, lambda t, k=k, sl=sl: t.matmul(
                    ps_g[:], wgbf[k][:], xk[k][:, sl],
                    start=(k == 0), stop=(k == KT - 1))))
            A(("s", 1, lambda s: s.activation(out=gb[:], in_=ps_g[:], func=AF.Sigmoid)))
            A(("g", 16, lambda g, sl=sl: g.dma_start(out=g_d[:, sl], in_=gb[:])))
        for m in range(MT):
            for k in range(KT):
                A(("y", 16, lambda y, k=k, m=m: y.dma_start(
                    out=wst[:], in_=wqkv_d[k * 128:(k + 1) * 128, m * 128:(m + 1) * 128])))
                A(("v", 1, lambda v, k=k: v.tensor_copy(out=wbf[k][:], in_=wst[:])))
            for nn in range(NN):
                sl = slice(nn * 512, (nn + 1) * 512)
                for k in range(KT):
                    A(("t", 1, lambda t, k=k, sl=sl: t.matmul(
                        ps_mm[:], wbf[k][:], xk[k][:, sl],
                        start=(k == 0), stop=(k == KT - 1))))
                A(("v", 1, lambda v: v.tensor_copy(out=ob[:], in_=ps_mm[:])))
                A(("g", 16, lambda g, m=m, sl=sl: g.dma_start(
                    out=qkv_d[m * 128:(m + 1) * 128, sl], in_=ob[:])))
        run_serial(nc, es, ops)
    return nc


def build_attn():
    nc = _bass()
    QT = 1024
    NQT = LB // QT
    NKT = LB // 128
    SC = DH ** -0.5

    q_dd = nc.dram_tensor("Q", [512, LB], BF16, kind="ExternalInput").ap()
    k_dd = nc.dram_tensor("Kf", [512, LB], BF16, kind="ExternalInput").ap()
    v_dd = nc.dram_tensor("Vt", [LB, 512], BF16, kind="ExternalInput").ap()
    g_dd = nc.dram_tensor("grows", [4, LB], F32, kind="ExternalInput").ap()
    mask_d = nc.dram_tensor("mask", [128, 2048], F32, kind="ExternalInput").ap()
    ones128b_d = nc.dram_tensor("ones128b", [128, 1], BF16, kind="ExternalInput").ap()
    ones1_d = nc.dram_tensor("ones1", [1, 128], F32, kind="ExternalInput").ap()
    attc_d = nc.dram_tensor("attc", [128, 3], F32, kind="ExternalInput").ap()
    o_d = nc.dram_tensor("O", [512, LB], BF16, kind="ExternalOutput").ap()

    with ExitStack() as es:
        e = es.enter_context
        mask = e(nc.sbuf_tensor("mask_s", [128, 2048], F32))
        ones128b = e(nc.sbuf_tensor("ones128b_s", [128, 1], BF16))
        ones1 = e(nc.sbuf_tensor("ones1_s", [1, 128], F32))
        attc = e(nc.sbuf_tensor("attc_s", [128, 3], F32))
        qh = e(nc.sbuf_tensor("qh", [128, LB], BF16))
        kh = e(nc.sbuf_tensor("kh", [128, LB], BF16))
        vh = e(nc.sbuf_tensor("vh", [128, NKT * 128], BF16))
        gl = e(nc.sbuf_tensor("gl", [1, LB], F32))
        z = e(nc.sbuf_tensor("z", [128, QT], F32))
        pbf = e(nc.sbuf_tensor("pbf", [128, QT], BF16))
        linv = e(nc.sbuf_tensor("linv", [1, QT], F32))
        coef = e(nc.sbuf_tensor("coef", [1, QT], F32))
        cbc = e(nc.sbuf_tensor("cbc", [128, QT], F32))
        obf = e(nc.sbuf_tensor("obf", [128, QT], BF16))
        ps_s = e(nc.psum_tensor("ps_s", [128, QT], F32))
        ps_o = e(nc.psum_tensor("ps_o", [128, QT], F32))
        ps_l = e(nc.psum_tensor("ps_l", [1, QT], F32))
        ps_c = e(nc.psum_tensor("ps_c", [128, QT], F32))

        ops = []
        A = ops.append
        A(("y", 16, lambda y: y.dma_start(out=mask[:], in_=mask_d[:])))
        A(("y", 16, lambda y: y.dma_start(out=ones128b[:], in_=ones128b_d[:])))
        A(("y", 16, lambda y: y.dma_start(out=ones1[:], in_=ones1_d[:])))
        A(("y", 16, lambda y: y.dma_start(out=attc[:], in_=attc_d[:])))
        for h in range(4):
            r0 = h * 128
            A(("y", 16, lambda y, r0=r0: y.dma_start(out=qh[:], in_=q_dd[r0:r0 + 128, :])))
            A(("y", 16, lambda y, r0=r0: y.dma_start(out=kh[:], in_=k_dd[r0:r0 + 128, :])))
            A(("y", 16, lambda y, h=h: y.dma_start(out=gl[:], in_=g_dd[h:h + 1, :])))
            for kt in range(NKT):
                A(("y", 16, lambda y, kt=kt, r0=r0: y.dma_start(
                    out=vh[:, kt * 128:(kt + 1) * 128],
                    in_=v_dd[kt * 128:(kt + 1) * 128, r0:r0 + 128])))
            for qt in range(NQT):
                q0 = qt * QT
                nkt = (qt + 1) * (QT // 128)
                for kt in range(nkt):
                    ksl = slice(kt * 128, (kt + 1) * 128)
                    for hf in range(QT // 512):
                        A(("t", 1, lambda t, ksl=ksl, q0=q0, hf=hf: t.matmul(
                            ps_s[:, hf * 512:(hf + 1) * 512], kh[:, ksl],
                            qh[:, q0 + hf * 512:q0 + (hf + 1) * 512],
                            start=True, stop=True)))
                    A(("s", 1, lambda s: s.activation(
                        out=z[:], in_=ps_s[:], func=AF.Tanh, scale=attc[:, 0:1])))
                    r = kt - (nkt - QT // 128)
                    if r >= 0:
                        msl = slice(QT - 128 * r, 2 * QT - 128 * r)
                        A(("v", 1, lambda v, msl=msl: v.tensor_add(
                            z[:], z[:], mask[:, msl])))
                    A(("s", 1, lambda s: s.activation(
                        out=pbf[:], in_=z[:], func=AF.Exp, scale=attc[:, 1:2], bias=attc[:, 2:3])))
                    for hf in range(QT // 512):
                        A(("t", 1, lambda t, hf=hf, kt=kt, nkt=nkt: t.matmul(
                            ps_l[:, hf * 512:(hf + 1) * 512], ones128b[:],
                            pbf[:, hf * 512:(hf + 1) * 512],
                            start=(kt == 0), stop=(kt == nkt - 1))))
                    for hf in range(QT // 512):
                        A(("t", 1, lambda t, hf=hf, kt=kt, ksl=ksl, nkt=nkt: t.matmul(
                            ps_o[:, hf * 512:(hf + 1) * 512], vh[:, ksl],
                            pbf[:, hf * 512:(hf + 1) * 512],
                            start=(kt == 0), stop=(kt == nkt - 1))))
                A(("v", 1, lambda v: v.reciprocal(linv[:], ps_l[:])))
                A(("v", 1, lambda v, q0=q0: v.tensor_mul(
                    coef[:], linv[:], gl[:, q0:q0 + QT])))
                for hf in range(QT // 512):
                    A(("t", 1, lambda t, hf=hf: t.matmul(
                        ps_c[:, hf * 512:(hf + 1) * 512], ones1[:],
                        coef[:, hf * 512:(hf + 1) * 512], start=True, stop=True)))
                A(("v", 1, lambda v: v.tensor_copy(out=cbc[:], in_=ps_c[:])))
                A(("v", 1, lambda v: v.tensor_mul(obf[:], ps_o[:], cbc[:])))
                A(("g", 16, lambda g, r0=r0, q0=q0: g.dma_start(
                    out=o_d[r0:r0 + 128, q0:q0 + QT], in_=obf[:])))
        run_serial(nc, es, ops)
    return nc


def build_attnout():
    nc = _bass()
    KT = DM // 128
    MT = DM // 128
    NN = SL // 512

    o_dd = nc.dram_tensor("Oslab", [DM, SL], BF16, kind="ExternalInput").ap()
    wout_d = nc.dram_tensor("wout", [DM, DM], F32, kind="ExternalInput").ap()
    x2_dd = nc.dram_tensor("x2", [DM, SL], F32, kind="ExternalInput").ap()
    rw_d = nc.dram_tensor("rw", [DM, NEXP], F32, kind="ExternalInput").ap()
    rb_d = nc.dram_tensor("rb", [1, NEXP], F32, kind="ExternalInput").ap()
    ones128_d = nc.dram_tensor("ones128", [128, 1], F32, kind="ExternalInput").ap()
    ones1_d = nc.dram_tensor("ones1", [1, 128], F32, kind="ExternalInput").ap()
    epsv_d = nc.dram_tensor("epsv", [128, 1], F32, kind="ExternalInput").ap()
    x3_d = nc.dram_tensor("x3", [DM, SL], F32, kind="ExternalOutput").ap()
    x3bf_d = nc.dram_tensor("x3bf", [DM, SL], BF16, kind="ExternalOutput").ap()
    s_d = nc.dram_tensor("S", [SL, NEXP], F32, kind="ExternalOutput").ap()
    scr_d = nc.dram_tensor("scr", [DM, SL], F32, kind="Internal").ap()

    with ExitStack() as es:
        e = es.enter_context
        ones128 = e(nc.sbuf_tensor("ones128_s", [128, 1], F32))
        ones1 = e(nc.sbuf_tensor("ones1_s", [1, 128], F32))
        epsv = e(nc.sbuf_tensor("epsv_s", [128, 1], F32))
        rhs = [e(nc.sbuf_tensor(f"rhs{k}", [128, SL], BF16)) for k in range(KT)]
        wst = e(nc.sbuf_tensor("wst", [128, 128], F32))
        wbf = [e(nc.sbuf_tensor(f"wbf{i}", [128, 128], BF16)) for i in range(KT)]
        rwst = e(nc.sbuf_tensor("rwst", [128, NEXP], F32))
        rwbf = [e(nc.sbuf_tensor(f"rwbf{i}", [128, NEXP], BF16)) for i in range(KT)]
        rbs = e(nc.sbuf_tensor("rbs", [1, NEXP], F32))
        xt = e(nc.sbuf_tensor("xt", [128, 512], F32))
        raw = e(nc.sbuf_tensor("raw", [128, 512], F32))
        x3p = e(nc.sbuf_tensor("x3p", [128, 128], BF16))
        rt = e(nc.sbuf_tensor("rt", [128, NEXP], F32))
        mx = e(nc.sbuf_tensor("mx", [128, 1], F32))
        sm = e(nc.sbuf_tensor("sm", [128, 1], F32))
        smr = e(nc.sbuf_tensor("smr", [128, 1], F32))
        st_ = e(nc.sbuf_tensor("st_", [128, NEXP], F32))
        rbc = e(nc.sbuf_tensor("rbc", [128, NEXP], F32))
        ps_mm = e(nc.psum_tensor("ps_mm", [128, 512], F32))
        ps_r = e(nc.psum_tensor("ps_r", [128, NEXP], F32))
        ps_rb = e(nc.psum_tensor("ps_rb", [128, NEXP], F32))

        ops = []
        A = ops.append
        A(("y", 16, lambda y: y.dma_start(out=ones128[:], in_=ones128_d[:])))
        A(("y", 16, lambda y: y.dma_start(out=ones1[:], in_=ones1_d[:])))
        A(("y", 16, lambda y: y.dma_start(out=epsv[:], in_=epsv_d[:])))
        A(("y", 16, lambda y: y.dma_start(out=rbs[:], in_=rb_d[:])))
        for k in range(KT):
            A(("y", 16, lambda y, k=k: y.dma_start(
                out=rhs[k][:], in_=o_dd[k * 128:(k + 1) * 128, :])))
            A(("y", 16, lambda y, k=k: y.dma_start(
                out=rwst[:], in_=rw_d[k * 128:(k + 1) * 128, :])))
            A(("v", 1, lambda v, k=k: v.tensor_copy(out=rwbf[k][:], in_=rwst[:])))
        ln = LNHelper(nc, es, ops, "ln3", scr_d, x3_d, x3bf_d, ones128, ones1,
                      epsv[0:1, 0:1])
        for m in range(MT):
            for k in range(KT):
                A(("y", 16, lambda y, k=k, m=m: y.dma_start(
                    out=wst[:], in_=wout_d[k * 128:(k + 1) * 128, m * 128:(m + 1) * 128])))
                A(("v", 1, lambda v, k=k: v.tensor_copy(out=wbf[k][:], in_=wst[:])))
            for nn in range(NN):
                sl = slice(nn * 512, (nn + 1) * 512)
                for k in range(KT):
                    A(("t", 1, lambda t, k=k, sl=sl: t.matmul(
                        ps_mm[:], wbf[k][:], rhs[k][:, sl],
                        start=(k == 0), stop=(k == KT - 1))))
                A(("y", 16, lambda y, m=m, sl=sl: y.dma_start(
                    out=xt[:], in_=x2_dd[m * 128:(m + 1) * 128, sl])))
                A(("v", 1, lambda v: v.tensor_add(raw[:], xt[:], ps_mm[:])))
                ln.accum(m, nn, raw)
        ln.finalize()
        ln.normalize()
        A(("t", 1, lambda t: t.matmul(ps_rb[:], ones1[:], rbs[:], start=True, stop=True)))
        A(("v", 1, lambda v: v.tensor_copy(out=rbc[:], in_=ps_rb[:])))
        for tt in range(SL // 128):
            for k in range(KT):
                A(("y", 16, lambda y, tt=tt, k=k: y.dma_start(
                    out=x3p[:], in_=x3bf_d[k * 128:(k + 1) * 128, tt * 128:(tt + 1) * 128])))
                A(("t", 1, lambda t, k=k: t.matmul(
                    ps_r[:], x3p[:], rwbf[k][:],
                    start=(k == 0), stop=(k == KT - 1))))
            A(("v", 1, lambda v: v.tensor_add(rt[:], ps_r[:], rbc[:])))
            A(("v", 1, lambda v: v.reduce_max(mx[:], rt[:].ap() if False else rt[:], axis=AX.X)))
            A(("v", 1, lambda v: v.tensor_sub(rt[:], rt[:], mx.ap().broadcast_to([128, NEXP]))))
            A(("s", 1, lambda s: s.activation(out=rt[:], in_=rt[:], func=AF.Exp)))
            A(("v", 1, lambda v: v.reduce_sum(sm[:], rt[:], axis=AX.X)))
            A(("s", 1, lambda s: s.activation(out=smr[:], in_=sm[:], func=AF.Ln)))
            A(("s", 1, lambda s: s.activation(out=smr[:], in_=smr[:], func=AF.Exp, scale=-1.0)))
            A(("v", 1, lambda v: v.tensor_mul(st_[:], rt[:], smr.ap().broadcast_to([128, NEXP]))))
            A(("g", 16, lambda g, tt=tt: g.dma_start(
                out=s_d[tt * 128:(tt + 1) * 128, :], in_=st_[:])))
        run_serial(nc, es, ops)
    return nc


def build_moe():
    nc = _bass()
    KT = DM // 128
    MT = DM // 128
    NN = SL // 512
    NBIS = 35

    sall_d = nc.dram_tensor("Sall", [128, TT * NEXP // 128], F32, kind="ExternalInput").ap()
    sslab_d = nc.dram_tensor("Sslab", [128, SL * NEXP // 128], F32, kind="ExternalInput").ap()
    x3_dd = nc.dram_tensor("x3", [DM, SL], F32, kind="ExternalInput").ap()
    x3bf_dd = nc.dram_tensor("x3bf", [DM, SL], BF16, kind="ExternalInput").ap()
    ew_d = nc.dram_tensor("ew", [DM, DM], F32, kind="ExternalInput").ap()
    eb_d = nc.dram_tensor("ebcol", [128, KT], F32, kind="ExternalInput").ap()
    kval_d = nc.dram_tensor("kval", [1, NEXP], F32, kind="ExternalInput").ap()
    lohi_d = nc.dram_tensor("lohi", [1, 2 * NEXP], F32, kind="ExternalInput").ap()
    ones128_d = nc.dram_tensor("ones128", [128, 1], F32, kind="ExternalInput").ap()
    ones1_d = nc.dram_tensor("ones1", [1, 128], F32, kind="ExternalInput").ap()
    epsv_d = nc.dram_tensor("epsv", [128, 1], F32, kind="ExternalInput").ap()
    x4_d = nc.dram_tensor("x4", [DM, SL], F32, kind="ExternalOutput").ap()
    x4bf_d = nc.dram_tensor("x4bf", [DM, SL], BF16, kind="ExternalOutput").ap()
    scr_d = nc.dram_tensor("scr", [DM, SL], F32, kind="Internal").ap()
    w_d = nc.dram_tensor("wtok", [1, SL], F32, kind="Internal").ap()

    SW = TT * NEXP // 128
    with ExitStack() as es:
        e = es.enter_context
        ones128 = e(nc.sbuf_tensor("ones128_s", [128, 1], F32))
        ones1 = e(nc.sbuf_tensor("ones1_s", [1, 128], F32))
        epsv = e(nc.sbuf_tensor("epsv_s", [128, 1], F32))
        kval = e(nc.sbuf_tensor("kval_s", [1, NEXP], F32))
        ssb = e(nc.sbuf_tensor("ssb", [128, SW], F32))
        msk = e(nc.sbuf_tensor("msk", [128, SW], F32))
        thr = e(nc.sbuf_tensor("thr", [1, SW], F32))
        cacc = e(nc.sbuf_tensor("cacc", [1, SW], F32))
        lo = e(nc.sbuf_tensor("lo", [1, NEXP], F32))
        hi = e(nc.sbuf_tensor("hi", [1, NEXP], F32))
        mid = e(nc.sbuf_tensor("mid", [1, NEXP], F32))
        cnt = e(nc.sbuf_tensor("cnt", [1, NEXP], F32))
        cge = e(nc.sbuf_tensor("cge", [1, NEXP], F32))
        tmp16 = e(nc.sbuf_tensor("tmp16", [1, NEXP], F32))
        wtm = e(nc.sbuf_tensor("wtm", [128, 16], F32))
        sslab = e(nc.sbuf_tensor("sslab_s", [128, SL * NEXP // 128], F32))
        thr2 = e(nc.sbuf_tensor("thr2", [1, SL * NEXP // 128], F32))
        msk2 = e(nc.sbuf_tensor("msk2", [128, SL * NEXP // 128], F32))
        wrow = e(nc.sbuf_tensor("wrow", [1, SL], F32))
        rhs = [e(nc.sbuf_tensor(f"rhs{k}", [128, SL], BF16)) for k in range(KT)]
        wst = e(nc.sbuf_tensor("wst", [128, 128], F32))
        wbf = [e(nc.sbuf_tensor(f"wbf{i}", [128, 128], BF16)) for i in range(KT)]
        ebc = e(nc.sbuf_tensor("ebc", [128, KT], F32))
        zb = e(nc.sbuf_tensor("zb", [128, 512], F32))
        xt = e(nc.sbuf_tensor("xt", [128, 512], F32))
        raw = e(nc.sbuf_tensor("raw", [128, 512], F32))
        wbcs = e(nc.sbuf_tensor("wbcs", [128, 512], F32))
        ps_mm = e(nc.psum_tensor("ps_mm", [128, 512], F32))
        ps_thr = e(nc.psum_tensor("ps_thr", [128, 512], F32))
        ps_cnt = e(nc.psum_tensor("ps_cnt", [1, 512], F32))
        ps_w = e(nc.psum_tensor("ps_w", [128, 512], F32))
        ps_t2 = e(nc.psum_tensor("ps_t2", [128, 256], F32))

        ops = []
        A = ops.append
        A(("y", 16, lambda y: y.dma_start(out=ones128[:], in_=ones128_d[:])))
        A(("y", 16, lambda y: y.dma_start(out=ones1[:], in_=ones1_d[:])))
        A(("y", 16, lambda y: y.dma_start(out=epsv[:], in_=epsv_d[:])))
        A(("y", 16, lambda y: y.dma_start(out=kval[:], in_=kval_d[:])))
        A(("y", 16, lambda y: y.dma_start(out=ssb[:], in_=sall_d[:])))
        A(("y", 16, lambda y: y.dma_start(out=sslab[:], in_=sslab_d[:])))
        A(("y", 16, lambda y: y.dma_start(out=ebc[:], in_=eb_d[:])))
        for k in range(KT):
            A(("y", 16, lambda y, k=k: y.dma_start(
                out=rhs[k][:], in_=x3bf_dd[k * 128:(k + 1) * 128, :])))
        A(("y", 16, lambda y: y.dma_start(out=lo[:], in_=lohi_d[:, 0:NEXP])))
        A(("y", 16, lambda y: y.dma_start(out=hi[:], in_=lohi_d[:, NEXP:2 * NEXP])))

        def count_pass(thr_src):
            A(("v", 1, lambda v: v.tensor_copy(
                out=thr.ap().rearrange("o (t e) -> o t e", e=NEXP),
                in_=thr_src.ap().unsqueeze(1).broadcast_to([1, 128, NEXP]))))
            for ch in range(SW // 512):
                sl = slice(ch * 512, (ch + 1) * 512)
                A(("t", 1, lambda t, sl=sl: t.matmul(
                    ps_thr[:], ones1[:], thr[:, sl], start=True, stop=True)))
                A(("v", 1, lambda v, sl=sl: v.tensor_tensor(
                    out=msk[:, sl], in0=ssb[:, sl], in1=ps_thr[:], op=ALU.is_ge)))
                A(("t", 1, lambda t, sl=sl: t.matmul(
                    ps_cnt[:], ones128[:], msk[:, sl], start=True, stop=True)))
                A(("v", 1, lambda v, sl=sl: v.tensor_copy(out=cacc[:, sl], in_=ps_cnt[:])))
            A(("v", 1, lambda v: v.reduce_sum(
                cnt[:], cacc.ap().rearrange("o (t e) -> o e t", e=NEXP), axis=AX.X)))

        for it in range(NBIS):
            A(("v", 1, lambda v: v.tensor_add(mid[:], lo[:], hi[:])))
            A(("v", 1, lambda v: v.tensor_scalar_mul(mid[:], mid[:], 0.5)))
            count_pass(mid)
            A(("v", 1, lambda v: v.tensor_tensor(out=cge[:], in0=cnt[:], in1=kval[:], op=ALU.is_ge)))
            A(("v", 1, lambda v: v.tensor_sub(tmp16[:], mid[:], lo[:])))
            A(("v", 1, lambda v: v.tensor_mul(tmp16[:], tmp16[:], cge[:])))
            A(("v", 1, lambda v: v.tensor_add(lo[:], lo[:], tmp16[:])))
            A(("v", 1, lambda v: v.tensor_sub(tmp16[:], hi[:], mid[:])))
            A(("v", 1, lambda v: v.tensor_mul(tmp16[:], tmp16[:], cge[:])))
            A(("v", 1, lambda v: v.tensor_add(hi[:], mid[:], tmp16[:])))
        A(("v", 1, lambda v: v.tensor_copy(
            out=thr2.ap().rearrange("o (t e) -> o t e", e=NEXP),
            in_=lo.ap().unsqueeze(1).broadcast_to([1, 16, NEXP]))))
        A(("t", 1, lambda t: t.matmul(ps_t2[:], ones1[:], thr2[:], start=True, stop=True)))
        A(("v", 1, lambda v: v.tensor_tensor(
            out=msk2[:], in0=sslab[:], in1=ps_t2[:], op=ALU.is_ge)))
        A(("v", 1, lambda v: v.tensor_mul(msk2[:], msk2[:], sslab[:])))
        A(("v", 1, lambda v: v.reduce_sum(
            wtm[:, 0:16], msk2.ap().rearrange("p (t e) -> p t e", e=NEXP), axis=AX.X)))
        A(("g", 16, lambda g: g.dma_start(
            out=w_d.rearrange("o (p t) -> p (o t)", p=128), in_=wtm[:, 0:16])))
        A(("y", 16, lambda y: y.dma_start(out=wrow[:], in_=w_d[0:1, :])))
        ln = LNHelper(nc, es, ops, "ln4", scr_d, x4_d, x4bf_d, ones128, ones1,
                      epsv[0:1, 0:1])
        for m in range(MT):
            for k in range(KT):
                A(("y", 16, lambda y, k=k, m=m: y.dma_start(
                    out=wst[:], in_=ew_d[k * 128:(k + 1) * 128, m * 128:(m + 1) * 128])))
                A(("v", 1, lambda v, k=k: v.tensor_copy(out=wbf[k][:], in_=wst[:])))
            for nn in range(NN):
                sl = slice(nn * 512, (nn + 1) * 512)
                for k in range(KT):
                    A(("t", 1, lambda t, k=k, sl=sl: t.matmul(
                        ps_mm[:], wbf[k][:], rhs[k][:, sl],
                        start=(k == 0), stop=(k == KT - 1))))
                A(("v", 1, lambda v, m=m: v.tensor_scalar_add(
                    zb[:], ps_mm[:], ebc[:, m:m + 1])))
                A(("t", 1, lambda t, sl=sl: t.matmul(
                    ps_w[:], ones1[:], wrow[:, sl], start=True, stop=True)))
                A(("v", 1, lambda v: v.tensor_copy(out=wbcs[:], in_=ps_w[:])))
                A(("v", 1, lambda v: v.tensor_mul(zb[:], zb[:], wbcs[:])))
                A(("y", 16, lambda y, m=m, sl=sl: y.dma_start(
                    out=xt[:], in_=x3_dd[m * 128:(m + 1) * 128, sl])))
                A(("v", 1, lambda v: v.tensor_add(raw[:], xt[:], zb[:])))
                ln.accum(m, nn, raw)
        ln.finalize()
        ln.normalize()
        run_serial(nc, es, ops)
    return nc


class Pool:

    def __init__(self, nc, es, kb_per_partition=200):
        self.words = kb_per_partition * 256
        self.heap = es.enter_context(nc.sbuf_tensor("heap", [128, self.words], F32))
        self.heap_bf = self.heap.ap().bitcast(BF16)
        self.heap_i = self.heap.ap().bitcast(I32)
        self.off = 0
        self.peak = 0

    def tile(self, shape, dtype=F32):
        p, n = shape
        if dtype == F32 or dtype == I32:
            w = n
        elif dtype == BF16:
            w = (n + 1) // 2
        else:
            raise ValueError(dtype)
        o = self.off
        self.off += w
        self.peak = max(self.peak, self.off)
        assert self.off <= self.words, f"SBUF pool overflow: {self.off * 4} B/p"
        if dtype == F32:
            return self.heap.ap()[0:p, o:o + n]
        if dtype == I32:
            return self.heap_i[0:p, o:o + n]
        return self.heap_bf[0:p, 2 * o:2 * o + n]

    def mark(self):
        return self.off

    def reset(self, to=0):
        self.off = to


class Ctx:

    def __init__(self, nc, es, ops, pool, psum):
        self.nc, self.es, self.ops, self.pool, self.psum = nc, es, ops, pool, psum
        self.A = ops.append


def barrier(ctx):
    ctx.A(("g", 1, lambda g: g.engine_nop()))


def emit_mm_group(ctx, psum_ap, pairs, start=True, stop=True, reads=None, writes=None):
    def fn(t):
        inst = None
        n = len(pairs)
        for i, (lf, rf) in enumerate(pairs):
            inst = t.matmul(psum_ap, lf(), rf(),
                            start=(start and i == 0), stop=(stop and i == n - 1))
        return inst
    if reads is None:
        ctx.A(("t", 1, fn))
    else:
        ctx.A(("t", 1, fn, reads, writes))


def _fln_emit(ctx, name, scratch_d, out_d, outbf_d, ones128, ones1, eps_ap, psA, psB, psC):
    pool = ctx.pool
    A = ctx.A
    sq = pool.tile([128, 512])
    statx = pool.tile([1, SL])
    statx2 = pool.tile([1, SL])
    arow = pool.tile([1, SL])
    brow = pool.tile([1, SL])
    nt0 = pool.tile([128, 1024])
    nt1 = pool.tile([128, 1024])
    obf = pool.tile([128, 1024], BF16)
    ps_st = psC[0:1, 0:512]
    MT, NN = DM // 128, SL // 512
    A(("g", 1, lambda g: g.memset(statx, 0.0), [statx], [statx]))
    A(("g", 1, lambda g: g.memset(statx2, 0.0), [statx2], [statx2]))

    def accum(m, nn, raw_tile):
        sl = slice(nn * 512, (nn + 1) * 512)
        A(("t", 1, lambda t: t.matmul(ps_st, ones128, raw_tile, start=True, stop=True),
          [ones128, raw_tile, psC], [psC]))
        A(("v", 1, lambda v: v.tensor_add(statx[:, sl], statx[:, sl], ps_st),
          [psC, statx], [statx]))
        A(("s", 1, lambda s: s.activation(out=sq, in_=raw_tile, func=AF.Square),
          [raw_tile, sq], [sq]))
        A(("t", 1, lambda t: t.matmul(ps_st, ones128, sq, start=True, stop=True),
          [ones128, sq, psC], [psC]))
        A(("v", 1, lambda v: v.tensor_add(statx2[:, sl], statx2[:, sl], ps_st),
          [psC, statx2], [statx2]))
        A(("g", 16, lambda g: g.dma_start(
            out=scratch_d[m * 128:(m + 1) * 128, sl], in_=raw_tile), [raw_tile], []))

    def finalize():
        n = 128 * MT

        def rows1(v):
            v.tensor_scalar_mul(statx, statx, 1.0 / n)
            v.tensor_scalar_mul(statx2, statx2, 1.0 / n)
            v.tensor_mul(arow, statx, statx)
            return v.tensor_sub(statx2, statx2, arow)
        A(("v", 1, rows1, [statx, statx2, arow], [statx, statx2, arow]))

        def rows2(s):
            s.activation(out=statx2, in_=statx2, func=AF.Ln, bias=eps_ap)
            return s.activation(out=arow, in_=statx2, func=AF.Exp, scale=-0.5)
        A(("s", 1, rows2, [statx2, arow], [statx2, arow]))

        def rows3(v):
            v.tensor_mul(brow, statx, arow)
            return v.tensor_scalar_mul(brow, brow, -1.0)
        A(("v", 1, rows3, [statx, arow, brow], [brow]))

    def normalize():
        barrier(ctx)
        for n2 in range(NN // 2):
            sl = slice(n2 * 1024, (n2 + 1) * 1024)

            def mma(t, sl=sl):
                t.matmul(psA[:, 0:512], ones1, arow[:, sl.start:sl.start + 512],
                         start=True, stop=True)
                return t.matmul(psA[:, 512:1024], ones1, arow[:, sl.start + 512:sl.stop],
                                start=True, stop=True)
            A(("t", 1, mma, [ones1, arow, psA], [psA]))

            def mmb(t, sl=sl):
                t.matmul(psB[:, 0:512], ones1, brow[:, sl.start:sl.start + 512],
                         start=True, stop=True)
                return t.matmul(psB[:, 512:1024], ones1, brow[:, sl.start + 512:sl.stop],
                                start=True, stop=True)
            A(("t", 1, mmb, [ones1, brow, psB], [psB]))
            for m in range(MT):
                msl = slice(m * 128, (m + 1) * 128)
                A(("y", 16, lambda y, msl=msl, sl=sl: y.dma_start(
                    out=nt0, in_=scratch_d[msl, sl]), [nt0], [nt0]))

                def norm2(v):
                    v.tensor_mul(nt1, nt0, psA[:, 0:1024])
                    return v.tensor_add(nt1, nt1, psB[:, 0:1024])
                A(("v", 1, norm2, [nt0, psA, psB, nt1], [nt1]))
                A(("g", 16, lambda g, msl=msl, sl=sl: g.dma_start(
                    out=out_d[msl, sl], in_=nt1), [nt1], []))
                if outbf_d is not None:
                    A(("v", 1, lambda v: v.tensor_copy(out=obf, in_=nt1), [nt1, obf], [obf]))
                    A(("g", 16, lambda g, msl=msl, sl=sl: g.dma_start(
                        out=outbf_d[msl, sl], in_=obf), [obf], []))

    return accum, finalize, normalize


def emit_scan(ctx, psA, psB, psC, d, BASE):
    pool, A = ctx.pool, ctx.A
    TC = 2048
    NTC = TT // TC
    NDG = DS // 128
    alogT = pool.tile([128, NDG * NSS])
    acol = pool.tile([128, NDG * NSS])
    dcolT = pool.tile([128, NDG])
    dt4 = [pool.tile([128, TC]) for _ in range(NDG)]
    e4 = [pool.tile([128, TC]) for _ in range(NDG)]
    yacc = [pool.tile([128, TC]) for _ in range(NDG)]
    ut = pool.tile([128, TC])
    sgt = pool.tile([128, TC])
    bbr = pool.tile([1, TC])
    cbr = pool.tile([1, TC])
    brep = pool.tile([128, TC])
    crep = pool.tile([128, TC])
    dA2 = [pool.tile([128, TC]) for _ in range(2)]
    dbu = pool.tile([128, TC])
    St = pool.tile([128, TC])
    z2 = [pool.tile([128, TC]) for _ in range(2)]
    tfin = dbu
    carry = pool.tile([128, NDG * NSS])
    ybf = pool.tile([128, TC], BF16)
    A(("y", 16, lambda y: y.dma_start(out=alogT, in_=d["alog"][:]), [], [alogT]))
    A(("y", 16, lambda y: y.dma_start(out=dcolT, in_=d["dcol"][:]), [], [dcolT]))
    A(("s", 1, lambda s: s.activation(out=acol, in_=alogT, func=AF.Exp),
      [alogT, acol], [acol]))
    A(("v", 1, lambda v: v.tensor_scalar_mul(acol, acol, -1.0), [acol], [acol]))

    for tc in range(NTC):
        t0 = tc * TC
        for dg in range(NDG):
            A(("y", 16, lambda y, dg=dg, t0=t0: y.dma_start(
                out=dt4[dg], in_=d["dt"][dg * 128:(dg + 1) * 128, t0:t0 + TC]),
              [dt4[dg]], [dt4[dg]]))
            A(("y", 16, lambda y, dg=dg, t0=t0: y.dma_start(
                out=ut, in_=d["u"][dg * 128:(dg + 1) * 128, t0:t0 + TC]),
              [ut], [ut]))
            A(("v", 1, lambda v, dg=dg: v.tensor_mul(e4[dg], dt4[dg], ut),
              [dt4[dg], ut, e4[dg]], [e4[dg]]))
        for n in range(NSS):
            A(("y", 16, lambda y, n=n, t0=t0: y.dma_start(
                out=bbr, in_=d["bc"][n:n + 1, t0:t0 + TC]), [bbr], [bbr]))
            A(("y", 16, lambda y, n=n, t0=t0: y.dma_start(
                out=cbr, in_=d["bc"][16 + n:17 + n, t0:t0 + TC]), [cbr], [cbr]))
            ones1s = d["ones1"]
            PSL = [psA, psB, psC, psA]
            for h, (src, dst) in enumerate([(bbr, brep), (cbr, crep)]):
                for c2 in range(2):
                    ps = PSL[h * 2 + c2]
                    o = c2 * 1024

                    def mmrep(t, src=src, ps=ps, o=o):
                        t.matmul(ps[:, 0:512], ones1s, src[:, o:o + 512],
                                 start=True, stop=True)
                        return t.matmul(ps[:, 512:1024], ones1s,
                                        src[:, o + 512:o + 1024],
                                        start=True, stop=True)
                    A(("t", 1, mmrep, [ones1s, src, ps], [ps]))
                    A(("s", 1, lambda s, ps=ps, dst=dst, o=o: s.activation(
                        out=dst[:, o:o + 1024], in_=ps[:, 0:1024], func=AF.Copy),
                      [ps, dst], [dst]))
            for dg in range(NDG):
                p2 = (n * NDG + dg) % 2
                col = dg * NSS + n
                A(("s", 1, lambda s, dg=dg, col=col, p2=p2: s.activation(
                    out=dA2[p2], in_=dt4[dg], func=AF.Exp,
                    scale=acol[:, col:col + 1]), [dt4[dg], acol, dA2[p2]], [dA2[p2]]))
                A(("v", 1, lambda v, dg=dg, p2=p2: v.tensor_mul(dbu, e4[dg], brep),
                  [e4[dg], brep, dbu], [dbu]))

                def scan_op(v, tc=tc, p2=p2, col=col):
                    if tc % 4 == 0:
                        return v.tensor_tensor_scan(
                            out=St, data0=dA2[p2], data1=dbu, initial=0.0,
                            op0=ALU.mult, op1=ALU.add)
                    return v.tensor_tensor_scan(
                        out=St, data0=dA2[p2], data1=dbu,
                        initial=carry[:, col:col + 1], op0=ALU.mult, op1=ALU.add)
                A(("v", 1, scan_op, [dA2[p2], dbu, carry, St], [St]))

                def zc(v, p2=p2, col=col):
                    v.tensor_copy(out=carry[:, col:col + 1], in_=St[:, TC - 1:TC])
                    return v.tensor_mul(z2[p2], St, crep)
                A(("v", 1, zc, [St, crep, carry, z2[p2]], [carry, z2[p2]]))
                if n == 0:
                    A(("g", 1, lambda g, dg=dg, p2=p2: g.tensor_copy(
                        out=yacc[dg], in_=z2[p2]), [z2[p2], yacc[dg]], [yacc[dg]]))
                else:
                    A(("g", 1, lambda g, dg=dg, p2=p2: g.tensor_add(
                        yacc[dg], yacc[dg], z2[p2]), [z2[p2], yacc[dg]], [yacc[dg]]))
        for dg in range(NDG):
            A(("y", 16, lambda y, dg=dg, t0=t0: y.dma_start(
                out=ut, in_=d["u"][dg * 128:(dg + 1) * 128, t0:t0 + TC]),
              [ut], [ut]))
            A(("y", 16, lambda y, dg=dg, t0=t0: y.dma_start(
                out=sgt, in_=d["sg"][dg * 128:(dg + 1) * 128, t0:t0 + TC]),
              [sgt], [sgt]))

            def fin_op(v, dg=dg):
                v.scalar_tensor_tensor(
                    out=tfin, in0=ut, scalar=dcolT[:, dg:dg + 1],
                    in1=yacc[dg], op0=ALU.mult, op1=ALU.add)
                return v.tensor_mul(ybf, tfin, sgt)
            A(("v", 1, fin_op, [ut, dcolT, yacc[dg], sgt, tfin, ybf],
              [tfin, ybf]))
            A(("g", 16, lambda g, dg=dg, t0=t0: g.dma_start(
                out=d["y"][dg * 128:(dg + 1) * 128, t0:t0 + TC], in_=ybf),
              [ybf], []))


def emit_tail(ctx, psA, psB, psC, psT, d, consts, BASE, G8):
    pool, A = ctx.pool, ctx.A
    ones128 = consts["ones128"]
    ones128b = consts["ones128b"]
    ones1 = consts["ones1"]
    eps_ln = consts["eps_ln"]
    P0, P1, P2 = psA[:, 0:512], psA[:, 512:1024], psB[:, 0:512]
    P3, P4, P5 = psB[:, 512:1024], psC[:, 0:512], psC[:, 512:1024]
    PROT6 = [P0, P1, P2, P3, P4, P5]
    PROT4 = [P0, P1, P2, P3]
    KT = DM // 128
    NN = SL // 512

    barrier(ctx)
    wop = pool.tile([128, DM])
    wopbf = [pool.tile([128, DM], BF16) for _ in range(4)]
    yk4 = [pool.tile([128, 4 * 512], BF16) for _ in range(2)]
    yfv = d["yfin"].rearrange("(k p) t -> p k t", p=128)
    ev = [pool.tile([128, 512]) for _ in range(4)]
    for k in range(4):
        A(("y", 16, lambda y, k=k: y.dma_start(
            out=wop, in_=d["opwc"][k * 128:(k + 1) * 128, :]), [wop], [wop]))
        A(("v", 1, lambda v, k=k: v.tensor_copy(out=wopbf[k], in_=wop),
          [wop, wopbf[k]], [wopbf[k]]))
    for r in range(NC):
        for nn in range(NN):
            jp = (r * NN + nn) % 2
            t0r = r * SL + nn * 512
            A(("y", 16, lambda y, t0r=t0r, jp=jp: y.dma_start(
                out=yk4[jp].rearrange("p (k c) -> p k c", k=4),
                in_=yfv[:, :, t0r:t0r + 512]), [yk4[jp]], [yk4[jp]]))
            for m in range(KT):
                mp = m % 4
                PS = PROT6[((r * NN + nn) * KT + m) % 6]
                emit_mm_group(ctx, PS, [
                    (lambda k=k, m=m: wopbf[k][:, m * 128:(m + 1) * 128],
                     lambda k=k, jp=jp: yk4[jp][:, k * 512:(k + 1) * 512])
                    for k in range(4)],
                    reads=wopbf + [yk4[jp], PS], writes=[PS])
                A(("v", 1, lambda v, mp=mp, PS=PS: v.tensor_copy(out=ev[mp], in_=PS),
                  [PS, ev[mp]], [ev[mp]]))
                A(("g", 16, lambda g, r=r, m=m, nn=nn, mp=mp: g.dma_start(
                    out=d["cb6"][r * DM + m * 128:r * DM + (m + 1) * 128,
                                 nn * 512:(nn + 1) * 512], in_=ev[mp]), [ev[mp]], []))
    A(("g", 1, lambda g: g.collective_compute(
        "ReduceScatter", ALU.add, G8, [d["cb6"]], [d["rs6"]])))
    pool.reset(BASE)
    accum, fin, norm = _fln_emit(ctx, "f1", d["lnscr"], d["x1"], d["x1bf"],
                                 ones128, ones1, eps_ln, psA, psB, psC)
    xt = [pool.tile([128, 512]) for _ in range(2)]
    raw = [pool.tile([128, 512]) for _ in range(2)]
    for m in range(KT):
        for nn in range(NN):
            jp = (m * NN + nn) % 2
            sl = slice(nn * 512, (nn + 1) * 512)
            msl = slice(m * 128, (m + 1) * 128)
            A(("y", 16, lambda y, msl=msl, sl=sl, jp=jp: y.dma_start(
                out=xt[jp], in_=d["rs6"][msl, sl]), [xt[jp]], [xt[jp]]))
            A(("y", 16, lambda y, msl=msl, sl=sl, jp=jp: y.dma_start(
                out=raw[jp], in_=d["xslab"][msl, sl]), [raw[jp]], [raw[jp]]))
            A(("v", 1, lambda v, jp=jp: v.tensor_add(raw[jp], raw[jp], xt[jp]),
              [raw[jp], xt[jp]], [raw[jp]]))
            accum(m, nn, raw[jp])
    fin()
    norm()
    pool.reset(BASE)

    barrier(ctx)
    HT = HID // 128
    sgc = pool.tile([128, HT])
    suc = pool.tile([128, HT])
    xk7 = [pool.tile([128, SL], BF16) for _ in range(KT)]
    wstage = [pool.tile([128, 11 * 128], I32) for _ in range(2)]
    wbf7 = [pool.tile([128, HT * 128], BF16) for _ in range(2)]
    upf = pool.tile([128, SL])
    gt = [pool.tile([128, 512]) for _ in range(2)]
    hb = [pool.tile([128, SL], BF16) for _ in range(2)]
    A(("y", 16, lambda y: y.dma_start(out=sgc, in_=d["sgc"][:]), [], [sgc]))
    A(("y", 16, lambda y: y.dma_start(out=suc, in_=d["suc"][:]), [], [suc]))

    wgv = d["wg"].rearrange("(k p) h -> p k h", p=128)
    wuv = d["wu"].rearrange("(k p) h -> p k h", p=128)
    wdv = d["wd"].rearrange("(k p) h -> p k h", p=128)
    wchunk = [0]
    wbuf7 = [0]

    def load_w7(view, nk, m, stages, bufs):
        wb = bufs[wbuf7[0] % 2]
        wbuf7[0] += 1
        k0 = 0
        while k0 < nk:
            kc = min(11, nk - k0)
            ws = stages[wchunk[0] % 2]
            wchunk[0] += 1
            A(("y", 16, lambda y, view=view, k0=k0, kc=kc, m=m, ws=ws: y.dma_start(
                out=ws[:, 0:kc * 128].rearrange("p (k c) -> p k c", k=kc),
                in_=view[:, k0:k0 + kc, m * 128:(m + 1) * 128]), [ws], [ws]))
            A(("s", 1, lambda s, k0=k0, kc=kc, ws=ws, wb=wb: s.activation(
                out=wb[:, k0 * 128:(k0 + kc) * 128], in_=ws[:, 0:kc * 128],
                func=AF.Copy, bias=-128.0), [ws, wb], [wb]))
            k0 += kc
        return wb

    for k in range(KT):
        A(("y", 16, lambda y, k=k: y.dma_start(
            out=xk7[k], in_=d["x1bf"][k * 128:(k + 1) * 128, :]),
          [xk7[k]], [xk7[k]]))
    rot7 = [0]
    for m in range(HT):
        wb = load_w7(wuv, KT, m, wstage, wbf7)
        for nn in range(NN):
            sl = slice(nn * 512, (nn + 1) * 512)
            PS = PROT4[rot7[0] % 4]
            rot7[0] += 1
            emit_mm_group(ctx, PS, [
                (lambda k=k, wb=wb: wb[:, k * 128:(k + 1) * 128],
                 lambda k=k, sl=sl: xk7[k][:, sl])
                for k in range(KT)], reads=[wb, PS] + xk7, writes=[PS])
            A(("s", 1, lambda s, m=m, sl=sl, PS=PS: s.activation(
                out=upf[:, sl], in_=PS, func=AF.Copy, scale=suc[:, m:m + 1]),
              [PS, suc, upf], [upf]))
        wb = load_w7(wgv, KT, m, wstage, wbf7)
        hbm = hb[m % 2]
        for nn in range(NN):
            sl = slice(nn * 512, (nn + 1) * 512)
            PS = PROT4[rot7[0] % 4]
            rot7[0] += 1
            gtb = gt[nn % 2]
            emit_mm_group(ctx, PS, [
                (lambda k=k, wb=wb: wb[:, k * 128:(k + 1) * 128],
                 lambda k=k, sl=sl: xk7[k][:, sl])
                for k in range(KT)], reads=[wb, PS] + xk7, writes=[PS])
            A(("s", 1, lambda s, m=m, PS=PS, gtb=gtb: s.activation(
                out=gtb, in_=PS, func=AF.Silu, scale=sgc[:, m:m + 1]),
              [PS, sgc, gtb], [gtb]))
            A(("v", 1, lambda v, sl=sl, gtb=gtb, hbm=hbm: v.tensor_mul(
                hbm[:, sl], gtb, upf[:, sl]), [gtb, upf, hbm], [hbm]))
        A(("g", 16, lambda g, m=m, hbm=hbm: g.dma_start(
            out=d["h7"][m * 128:(m + 1) * 128, :], in_=hbm), [hbm], []))
    barrier(ctx)
    pool.reset(BASE)
    sdc = pool.tile([128, KT])
    hpan = [pool.tile([128, 1024], BF16) for _ in range(HT)]
    wstage2 = [pool.tile([128, 11 * 128], I32) for _ in range(2)]
    wbf7b = [pool.tile([128, HT * 128], BF16) for _ in range(2)]
    xt7 = [pool.tile([128, 512]) for _ in range(2)]
    dnt = [pool.tile([128, 512]) for _ in range(2)]
    raw7 = [pool.tile([128, 512]) for _ in range(2)]
    A(("y", 16, lambda y: y.dma_start(out=sdc, in_=d["sdc"][:]), [], [sdc]))
    accum, fin, norm = _fln_emit(ctx, "f2", d["lnscr"], d["x2"], d["x2bf"],
                                 ones128, ones1, eps_ln, psA, psB, psC)
    for half in range(2):
        h0 = half * 1024
        for k in range(HT):
            A(("y", 16, lambda y, k=k, h0=h0: y.dma_start(
                out=hpan[k], in_=d["h7"][k * 128:(k + 1) * 128, h0:h0 + 1024]),
              [hpan[k]], [hpan[k]]))
        for m in range(KT):
            wb = load_w7(wdv, HT, m, wstage2, wbf7b)
            for nn in range(2):
                sl = slice(nn * 512, (nn + 1) * 512)
                gnn = half * 2 + nn
                gsl = slice(h0 + nn * 512, h0 + (nn + 1) * 512)
                jp = (m * 2 + nn) % 2
                PS = PROT4[rot7[0] % 4]
                rot7[0] += 1
                dntb = dnt[jp]
                emit_mm_group(ctx, PS, [
                    (lambda k=k, wb=wb: wb[:, k * 128:(k + 1) * 128],
                     lambda k=k, sl=sl: hpan[k][:, sl])
                    for k in range(HT)], reads=[wb, PS] + hpan, writes=[PS])
                A(("s", 1, lambda s, m=m, PS=PS, dntb=dntb: s.activation(
                    out=dntb, in_=PS, func=AF.Copy, scale=sdc[:, m:m + 1]),
                  [PS, sdc, dntb], [dntb]))
                A(("y", 16, lambda y, m=m, gsl=gsl, jp=jp: y.dma_start(
                    out=xt7[jp], in_=d["x1"][m * 128:(m + 1) * 128, gsl]),
                  [xt7[jp]], [xt7[jp]]))
                A(("v", 1, lambda v, jp=jp, dntb=dntb: v.tensor_add(
                    raw7[jp], xt7[jp], dntb), [xt7[jp], dntb, raw7[jp]], [raw7[jp]]))
                accum(m, gnn, raw7[jp])
    fin()
    norm()
    pool.reset(BASE)

    barrier(ctx)
    QB = 770
    xk8 = [pool.tile([128, SL], BF16) for _ in range(KT)]
    wst8 = [pool.tile([128, KT * 128]) for _ in range(2)]
    wbf8 = [pool.tile([128, KT * 128], BF16) for _ in range(2)]
    wqkvv = d["wqkv"].rearrange("(k p) m -> p k m", p=128)
    wgst = pool.tile([128, NH])
    wgbf = [pool.tile([128, NH], BF16) for _ in range(KT)]
    ob8 = [pool.tile([128, 512], BF16) for _ in range(4)]
    gb8 = pool.tile([NH, 512])
    gbbf = pool.tile([NH, 512], BF16)
    for k in range(KT):
        A(("y", 16, lambda y, k=k: y.dma_start(
            out=xk8[k], in_=d["x2bf"][k * 128:(k + 1) * 128, :]), [xk8[k]], [xk8[k]]))
        A(("y", 16, lambda y, k=k: y.dma_start(
            out=wgst, in_=d["wgates"][k * 128:(k + 1) * 128, :]), [wgst], [wgst]))
        A(("v", 1, lambda v, k=k: v.tensor_copy(out=wgbf[k], in_=wgst),
          [wgst, wgbf[k]], [wgbf[k]]))
    for nn in range(NN):
        sl = slice(nn * 512, (nn + 1) * 512)
        emit_mm_group(ctx, P0[0:NH, :], [
            (lambda k=k: wgbf[k], lambda k=k, sl=sl: xk8[k][:, sl])
            for k in range(KT)])
        A(("s", 1, lambda s: s.activation(out=gb8, in_=P0[0:NH, :], func=AF.Sigmoid)))
        A(("v", 1, lambda v: v.tensor_copy(out=gbbf, in_=gb8)))
        for h in range(NH):
            blk = h // 2
            ro = QB * blk + 768 + (h % 2)
            A(("g", 16, lambda g, h=h, ro=ro, sl=sl: g.dma_start(
                out=d["cb3"][ro:ro + 1, sl], in_=gbbf[h:h + 1, :])))
    rot8 = [0]
    for m in range(3 * KT):
        part = m // 16
        h = m % 16
        blk = h // 2
        ro = QB * blk + part * 256 + (h % 2) * 128
        mp = m % 2
        ws8, wb8 = wst8[mp], wbf8[mp]
        A(("y", 16, lambda y, m=m, ws8=ws8: y.dma_start(
            out=ws8.rearrange("p (k c) -> p k c", k=KT),
            in_=wqkvv[:, :, m * 128:(m + 1) * 128]), [ws8], [ws8]))
        A(("s", 1, lambda s, ws8=ws8, wb8=wb8: s.activation(
            out=wb8, in_=ws8, func=AF.Copy), [ws8, wb8], [wb8]))
        for nn in range(NN):
            sl = slice(nn * 512, (nn + 1) * 512)
            PS = PROT6[rot8[0] % 6]
            rot8[0] += 1
            ob = ob8[(m * NN + nn) % 4]
            emit_mm_group(ctx, PS, [
                (lambda k=k, wb8=wb8: wb8[:, k * 128:(k + 1) * 128],
                 lambda k=k, sl=sl: xk8[k][:, sl])
                for k in range(KT)], reads=[wb8, PS] + xk8, writes=[PS])
            A(("v", 1, lambda v, ob=ob, PS=PS: v.tensor_copy(out=ob, in_=PS),
              [PS, ob], [ob]))
            A(("g", 16, lambda g, ro=ro, sl=sl, ob=ob: g.dma_start(
                out=d["cb3"][ro:ro + 128, sl], in_=ob), [ob], []))
    def a2a(g):
        from concourse.aot_env import allow_ici_all_to_all
        with allow_ici_all_to_all():
            return g.collective_compute("AllToAll", ALU.bypass, G8,
                                        [d["cb3"]], [d["rs3"]])
    A(("g", 1, a2a))
    pool.reset(BASE)

    barrier(ctx)
    QB = 770
    QT = 1024
    identb = pool.tile([128, 128], BF16)
    mask = pool.tile([128, 2048])
    attc = pool.tile([128, 3])
    qh = pool.tile([128, LB], BF16)
    kh = pool.tile([128, LB], BF16)
    vh = pool.tile([128, LB], BF16)
    vt = pool.tile([128, LB], BF16)
    gl = pool.tile([1, LB])
    z9 = [pool.tile([128, QT]) for _ in range(2)]
    pbf9 = [pool.tile([128, QT], BF16) for _ in range(2)]
    linv = pool.tile([1, QT])
    coef = pool.tile([1, QT])
    cbc = pool.tile([128, QT])
    obf9 = pool.tile([128, QT], BF16)
    A(("y", 16, lambda y: y.dma_start(out=identb, in_=d["identb"][:]), [], [identb]))
    A(("y", 16, lambda y: y.dma_start(out=mask, in_=d["mask"][:]), [], [mask]))
    A(("y", 16, lambda y: y.dma_start(out=attc, in_=d["attc"][:]), [], [attc]))
    for i in range(2):
        for b in range(2):
            for r in range(4):
                rr = 4 * b + r
                csl = slice(r * SL, (r + 1) * SL)
                A(("y", 16, lambda y, rr=rr, i=i, csl=csl: y.dma_start(
                    out=qh[:, csl], in_=d["rs3"][rr * QB + i * 128:rr * QB + (i + 1) * 128, :]),
                  [qh], [qh]))
                A(("y", 16, lambda y, rr=rr, i=i, csl=csl: y.dma_start(
                    out=kh[:, csl], in_=d["rs3"][rr * QB + 256 + i * 128:rr * QB + 256 + (i + 1) * 128, :]),
                  [kh], [kh]))
                A(("y", 16, lambda y, rr=rr, i=i, csl=csl: y.dma_start(
                    out=vh[:, csl], in_=d["rs3"][rr * QB + 512 + i * 128:rr * QB + 512 + (i + 1) * 128, :]),
                  [vh], [vh]))
                A(("g", 16, lambda g, rr=rr, i=i, csl=csl: g.dma_start(
                    out=gl[:, csl], in_=d["rs3"][rr * QB + 768 + i:rr * QB + 768 + i + 1, :]),
                  [gl], [gl]))
            for kt in range(LB // 128):
                ksl = slice(kt * 128, (kt + 1) * 128)
                A(("t", 1, lambda t, ksl=ksl: t.transpose(
                    psT[:, 0:128], vh[:, ksl], identb), [vh, identb, psT], [psT]))
                A(("v", 1, lambda v, ksl=ksl: v.tensor_copy(
                    out=vt[:, ksl], in_=psT[:, 0:128]), [psT, vt], [vt]))

            def emit_mms(kt, q0):
                ksl = slice(kt * 128, (kt + 1) * 128)
                A(("t", 1, lambda t, ksl=ksl, q0=q0: t.matmul(
                    psA[:, 0:512], kh[:, ksl], qh[:, q0:q0 + 512],
                    start=True, stop=True), [kh, qh, P0], [P0]))
                A(("t", 1, lambda t, ksl=ksl, q0=q0: t.matmul(
                    psA[:, 512:1024], kh[:, ksl], qh[:, q0 + 512:q0 + 1024],
                    start=True, stop=True), [kh, qh, P1], [P1]))

            for qt in range(LB // QT):
                q0 = qt * QT
                nkt = (qt + 1) * (QT // 128)
                emit_mms(0, q0)
                for kt in range(nkt):
                    ksl = slice(kt * 128, (kt + 1) * 128)
                    kp = kt % 2
                    zb9, pb9 = z9[kp], pbf9[kp]
                    rrel = kt - (nkt - QT // 128)
                    A(("s", 1, lambda s, zb9=zb9: s.activation(
                        out=zb9, in_=psA[:, 0:1024], func=AF.Tanh,
                        scale=attc[:, 0:1]), [P0, P1, attc, zb9], [zb9]))
                    if kt + 1 < nkt:
                        emit_mms(kt + 1, q0)
                    if rrel >= 0:
                        msl = slice(QT - 128 * rrel, 2 * QT - 128 * rrel)
                        A(("v", 1, lambda v, msl=msl, zb9=zb9: v.tensor_add(
                            zb9, zb9, mask[:, msl]), [zb9, mask], [zb9]))
                    A(("s", 1, lambda s, zb9=zb9, pb9=pb9: s.activation(
                        out=pb9, in_=zb9, func=AF.Exp, scale=attc[:, 1:2],
                        bias=attc[:, 2:3]), [zb9, attc, pb9], [pb9]))

                    def mmlo(t, kt=kt, ksl=ksl, nkt=nkt, pb9=pb9):
                        t.matmul(psC[0:1, 0:512], ones128b, pb9[:, 0:512],
                                 start=(kt == 0), stop=(kt == nkt - 1))
                        t.matmul(psC[0:1, 512:1024], ones128b, pb9[:, 512:1024],
                                 start=(kt == 0), stop=(kt == nkt - 1))
                        t.matmul(psB[:, 0:512], vt[:, ksl], pb9[:, 0:512],
                                 start=(kt == 0), stop=(kt == nkt - 1))
                        return t.matmul(psB[:, 512:1024], vt[:, ksl], pb9[:, 512:1024],
                                        start=(kt == 0), stop=(kt == nkt - 1))
                    A(("t", 1, mmlo, [ones128b, vt, pb9, psB, psC], [psB, psC]))

                def lchain(s):
                    s.activation(out=linv, in_=psC[0:1, 0:1024], func=AF.Ln)
                    return s.activation(out=linv, in_=linv, func=AF.Exp, scale=-1.0)
                A(("s", 1, lchain, [psC, linv], [linv]))
                A(("v", 1, lambda v, q0=q0: v.tensor_mul(
                    coef, linv, gl[:, q0:q0 + QT]), [linv, gl, coef], [coef]))

                def mmc(t):
                    t.matmul(psC[:, 0:512], ones1, coef[:, 0:512], start=True, stop=True)
                    return t.matmul(psC[:, 512:1024], ones1, coef[:, 512:1024],
                                    start=True, stop=True)
                A(("t", 1, mmc, [ones1, coef, psC], [psC]))

                def ochain(v):
                    v.tensor_copy(out=cbc, in_=psC[:, 0:1024])
                    return v.tensor_mul(obf9, psB[:, 0:1024], cbc)
                A(("v", 1, ochain, [psC, psB, cbc, obf9], [cbc, obf9]))
                A(("g", 16, lambda g, i=i, b=b, q0=q0: g.dma_start(
                    out=d["o"][i * 128:(i + 1) * 128, b * LB + q0:b * LB + q0 + QT],
                    in_=obf9), [obf9], []))
    pool.reset(BASE)

    barrier(ctx)
    wo = pool.tile([128, DM])
    wobf = [pool.tile([128, DM], BF16) for _ in range(2)]
    ok10 = [pool.tile([128, 2 * 512], BF16) for _ in range(2)]
    ofv = d["o"].rearrange("(k p) t -> p k t", p=128)
    ev10 = [pool.tile([128, 512]) for _ in range(2)]
    for k in range(2):
        A(("y", 16, lambda y, k=k: y.dma_start(
            out=wo, in_=d["wouth"][k * 128:(k + 1) * 128, :]), [wo], [wo]))
        A(("v", 1, lambda v, k=k: v.tensor_copy(out=wobf[k], in_=wo),
          [wo, wobf[k]], [wobf[k]]))
    for r in range(NC):
        for nn in range(NN):
            jp = (r * NN + nn) % 2
            t0r = r * SL + nn * 512
            A(("y", 16, lambda y, t0r=t0r, jp=jp: y.dma_start(
                out=ok10[jp].rearrange("p (k c) -> p k c", k=2),
                in_=ofv[:, :, t0r:t0r + 512]), [ok10[jp]], [ok10[jp]]))
            for m in range(KT):
                mp = m % 2
                emit_mm_group(ctx, P0, [
                    (lambda k=k, m=m: wobf[k][:, m * 128:(m + 1) * 128],
                     lambda k=k, jp=jp: ok10[jp][:, k * 512:(k + 1) * 512])
                    for k in range(2)],
                    reads=wobf + [ok10[jp], psA], writes=[psA])
                A(("v", 1, lambda v, mp=mp: v.tensor_copy(out=ev10[mp], in_=P0),
                  [psA, ev10[mp]], [ev10[mp]]))
                A(("g", 16, lambda g, r=r, m=m, nn=nn, mp=mp: g.dma_start(
                    out=d["cb10"][r * DM + m * 128:r * DM + (m + 1) * 128,
                                  nn * 512:(nn + 1) * 512], in_=ev10[mp]), [ev10[mp]], []))
    A(("g", 1, lambda g: g.collective_compute(
        "ReduceScatter", ALU.add, G8, [d["cb10"]], [d["rs10"]])))
    pool.reset(BASE)
    accum, fin, norm = _fln_emit(ctx, "f3", d["lnscr"], d["x3"], d["x3bf"],
                                 ones128, ones1, eps_ln, psA, psB, psC)
    xt10 = [pool.tile([128, 512]) for _ in range(2)]
    raw10 = [pool.tile([128, 512]) for _ in range(2)]
    for m in range(KT):
        for nn in range(NN):
            jp = (m * NN + nn) % 2
            sl = slice(nn * 512, (nn + 1) * 512)
            msl = slice(m * 128, (m + 1) * 128)
            A(("y", 16, lambda y, msl=msl, sl=sl, jp=jp: y.dma_start(
                out=xt10[jp], in_=d["rs10"][msl, sl]), [xt10[jp]], [xt10[jp]]))
            A(("y", 16, lambda y, msl=msl, sl=sl, jp=jp: y.dma_start(
                out=raw10[jp], in_=d["x2"][msl, sl]), [raw10[jp]], [raw10[jp]]))
            A(("v", 1, lambda v, jp=jp: v.tensor_add(raw10[jp], raw10[jp], xt10[jp]),
              [raw10[jp], xt10[jp]], [raw10[jp]]))
            accum(m, nn, raw10[jp])
    fin()
    norm()
    barrier(ctx)
    rwst = pool.tile([128, NEXP])
    rwbf = [pool.tile([128, NEXP], BF16) for _ in range(KT)]
    rbs = pool.tile([1, NEXP])
    rbc = pool.tile([128, NEXP])
    x3p = [pool.tile([128, KT * 128], BF16) for _ in range(2)]
    x3bv = d["x3bf"].rearrange("(k p) t -> p k t", p=128)
    rt10 = [pool.tile([128, NEXP]) for _ in range(2)]
    mx = pool.tile([128, 1])
    sm = pool.tile([128, 1])
    smr = pool.tile([128, 1])
    st10 = [pool.tile([128, NEXP]) for _ in range(2)]
    A(("y", 16, lambda y: y.dma_start(out=rbs, in_=d["rb"][:]), [], [rbs]))
    for k in range(KT):
        A(("y", 16, lambda y, k=k: y.dma_start(
            out=rwst, in_=d["rw"][k * 128:(k + 1) * 128, :]), [rwst], [rwst]))
        A(("v", 1, lambda v, k=k: v.tensor_copy(out=rwbf[k], in_=rwst),
          [rwst, rwbf[k]], [rwbf[k]]))
    A(("t", 1, lambda t: t.matmul(P1[:, 0:NEXP], ones1, rbs, start=True, stop=True),
      [ones1, rbs, psA], [psA]))
    A(("v", 1, lambda v: v.tensor_copy(out=rbc, in_=P1[:, 0:NEXP]), [psA, rbc], [rbc]))
    for tt in range(SL // 128):
        tp10 = tt % 2
        A(("y", 16, lambda y, tt=tt, tp10=tp10: y.dma_start(
            out=x3p[tp10].rearrange("p (k c) -> p k c", k=KT),
            in_=x3bv[:, :, tt * 128:(tt + 1) * 128]), [x3p[tp10]], [x3p[tp10]]))

        def mmr(t, tp10=tp10):
            inst = None
            for k in range(KT):
                inst = t.matmul(P0[:, 0:NEXP], x3p[tp10][:, k * 128:(k + 1) * 128],
                                rwbf[k], start=(k == 0), stop=(k == KT - 1))
            return inst
        A(("t", 1, mmr, [x3p[tp10], psA] + rwbf, [psA]))
        A(("v", 1, lambda v, tp10=tp10: v.tensor_add(rt10[tp10], P0[:, 0:NEXP], rbc),
          [psA, rbc, rt10[tp10]], [rt10[tp10]]))
        A(("v", 1, lambda v, tp10=tp10: v.reduce_max(mx, rt10[tp10], axis=AX.X),
          [rt10[tp10], mx], [mx]))
        A(("v", 1, lambda v, tp10=tp10: v.tensor_sub(
            rt10[tp10], rt10[tp10], mx.broadcast_to([128, NEXP])),
          [rt10[tp10], mx], [rt10[tp10]]))
        A(("s", 1, lambda s, tp10=tp10: s.activation(
            out=rt10[tp10], in_=rt10[tp10], func=AF.Exp), [rt10[tp10]], [rt10[tp10]]))
        A(("v", 1, lambda v, tp10=tp10: v.reduce_sum(sm, rt10[tp10], axis=AX.X),
          [rt10[tp10], sm], [sm]))

        def smchain(s):
            s.activation(out=smr, in_=sm, func=AF.Ln)
            return s.activation(out=smr, in_=smr, func=AF.Exp, scale=-1.0)
        A(("s", 1, smchain, [sm, smr], [smr]))
        A(("v", 1, lambda v, tp10=tp10: v.tensor_mul(
            st10[tp10], rt10[tp10], smr.broadcast_to([128, NEXP])),
          [rt10[tp10], smr, st10[tp10]], [st10[tp10]]))
        A(("g", 16, lambda g, tt=tt, tp10=tp10: g.dma_start(
            out=d["sslab"][tt * 128:(tt + 1) * 128, :], in_=st10[tp10]),
          [st10[tp10]], []))
    A(("g", 1, lambda g: g.collective_compute(
        "AllGather", ALU.bypass, G8, [d["sslab"]], [d["sgath"]])))
    pool.reset(BASE)

    barrier(ctx)
    SW = TT * NEXP // 128
    SWL = SL * NEXP // 128
    NBIS = 35
    kval = pool.tile([1, NEXP])
    ssb = pool.tile([128, SW])
    msk = pool.tile([128, SW])
    thr = pool.tile([1, SW])
    cacc = pool.tile([1, SW])
    lo = pool.tile([1, NEXP])
    hi = pool.tile([1, NEXP])
    mid = pool.tile([1, NEXP])
    cnt = pool.tile([1, NEXP])
    cge = pool.tile([1, NEXP])
    tmp16 = pool.tile([1, NEXP])
    sslab = pool.tile([128, SWL])
    thr2 = pool.tile([1, SWL])
    msk2 = pool.tile([128, SWL])
    wtm = pool.tile([128, 16])
    wrow = pool.tile([1, SL])
    rhs11 = [pool.tile([128, SL], BF16) for _ in range(KT)]
    wst11 = pool.tile([128, KT * 128])
    wbf11v = pool.tile([128, KT * 128], BF16)
    ewv = d["ew"].rearrange("(k p) m -> p k m", p=128)
    ebc = pool.tile([128, KT])
    zb = pool.tile([128, 512])
    xt11 = pool.tile([128, 512])
    raw11 = pool.tile([128, 512])
    wbcs = pool.tile([128, 512])
    A(("y", 16, lambda y: y.dma_start(out=kval, in_=d["kval"][:])))
    A(("y", 16, lambda y: y.dma_start(
        out=ssb, in_=d["sgath"].rearrange("(p w) e -> p (w e)", p=128))))
    A(("y", 16, lambda y: y.dma_start(
        out=sslab, in_=d["sslab"].rearrange("(p w) e -> p (w e)", p=128))))
    A(("y", 16, lambda y: y.dma_start(out=ebc, in_=d["ebc"][:])))
    A(("y", 16, lambda y: y.dma_start(out=lo, in_=d["lohi"][:, 0:NEXP])))
    A(("y", 16, lambda y: y.dma_start(out=hi, in_=d["lohi"][:, NEXP:2 * NEXP])))
    for k in range(KT):
        A(("y", 16, lambda y, k=k: y.dma_start(
            out=rhs11[k], in_=d["x3bf"][k * 128:(k + 1) * 128, :])))

    def count_pass(thr_src):
        A(("v", 1, lambda v: v.tensor_copy(
            out=thr.rearrange("o (t e) -> o t e", e=NEXP),
            in_=thr_src.unsqueeze(1).broadcast_to([1, SW // NEXP, NEXP]))))
        for ch in range(SW // 512):
            sl = slice(ch * 512, (ch + 1) * 512)
            A(("t", 1, lambda t, sl=sl: t.matmul(
                P2, ones1, thr[:, sl], start=True, stop=True)))
            A(("v", 1, lambda v, sl=sl: v.tensor_tensor(
                out=msk[:, sl], in0=ssb[:, sl], in1=P2, op=ALU.is_ge)))
            A(("t", 1, lambda t, sl=sl: t.matmul(
                P3[0:1, :], ones128, msk[:, sl], start=True, stop=True)))
            A(("v", 1, lambda v, sl=sl: v.tensor_copy(out=cacc[:, sl], in_=P3[0:1, :])))
        A(("v", 1, lambda v: v.reduce_sum(
            cnt, cacc.rearrange("o (t e) -> o e t", e=NEXP), axis=AX.X)))

    for it in range(NBIS):
        A(("v", 1, lambda v: v.tensor_add(mid, lo, hi)))
        A(("v", 1, lambda v: v.tensor_scalar_mul(mid, mid, 0.5)))
        count_pass(mid)
        A(("v", 1, lambda v: v.tensor_tensor(out=cge, in0=cnt, in1=kval, op=ALU.is_ge)))
        A(("v", 1, lambda v: v.tensor_sub(tmp16, mid, lo)))
        A(("v", 1, lambda v: v.tensor_mul(tmp16, tmp16, cge)))
        A(("v", 1, lambda v: v.tensor_add(lo, lo, tmp16)))
        A(("v", 1, lambda v: v.tensor_sub(tmp16, hi, mid)))
        A(("v", 1, lambda v: v.tensor_mul(tmp16, tmp16, cge)))
        A(("v", 1, lambda v: v.tensor_add(hi, mid, tmp16)))
    A(("v", 1, lambda v: v.tensor_copy(
        out=thr2.rearrange("o (t e) -> o t e", e=NEXP),
        in_=lo.unsqueeze(1).broadcast_to([1, SWL // NEXP, NEXP]))))
    A(("t", 1, lambda t: t.matmul(P2[:, 0:256], ones1, thr2, start=True, stop=True)))
    A(("v", 1, lambda v: v.tensor_tensor(
        out=msk2, in0=sslab, in1=P2[:, 0:256], op=ALU.is_ge)))
    A(("v", 1, lambda v: v.tensor_mul(msk2, msk2, sslab)))
    A(("v", 1, lambda v: v.reduce_sum(
        wtm, msk2.rearrange("p (t e) -> p t e", e=NEXP), axis=AX.X)))
    A(("g", 16, lambda g: g.dma_start(
        out=d["w"].rearrange("o (p t) -> p (o t)", p=128), in_=wtm)))
    A(("y", 16, lambda y: y.dma_start(out=wrow, in_=d["w"][0:1, :])))
    accum, fin, norm = _fln_emit(ctx, "f4", d["lnscr"], d["x4"], None,
                                 ones128, ones1, eps_ln, psA, psB, psC)
    for m in range(KT):
        A(("y", 16, lambda y, m=m: y.dma_start(
            out=wst11.rearrange("p (k c) -> p k c", k=KT),
            in_=ewv[:, :, m * 128:(m + 1) * 128])))
        A(("v", 1, lambda v: v.tensor_copy(out=wbf11v, in_=wst11)))
        for nn in range(NN):
            sl = slice(nn * 512, (nn + 1) * 512)
            emit_mm_group(ctx, P0, [
                (lambda k=k: wbf11v[:, k * 128:(k + 1) * 128],
                 lambda k=k, sl=sl: rhs11[k][:, sl])
                for k in range(KT)])
            A(("v", 1, lambda v, m=m: v.tensor_scalar_add(zb, P0, ebc[:, m:m + 1])))
            A(("t", 1, lambda t, sl=sl: t.matmul(
                P4, ones1, wrow[:, sl], start=True, stop=True)))
            A(("v", 1, lambda v: v.tensor_copy(out=wbcs, in_=P4)))
            A(("v", 1, lambda v: v.tensor_mul(zb, zb, wbcs)))
            A(("y", 16, lambda y, m=m, sl=sl: y.dma_start(
                out=xt11, in_=d["x3"][m * 128:(m + 1) * 128, sl])))
            A(("v", 1, lambda v: v.tensor_add(raw11, xt11, zb)))
            accum(m, nn, raw11)
    fin()
    norm()


def build_fused():
    nc = _bass()
    f = F32
    def di(name, shape, dt=F32):
        return nc.dram_tensor(name, shape, dt, kind="ExternalInput").ap()

    xT_d = di("xT", [DM, TT])
    win_d = di("win", [DM, 2 * DS])
    convw_d = di("convw", [128, 16])
    convb_d = di("convb", [128, 4])
    xpw_d = di("xpw", [DS, 160])
    dpwc_d = di("dpwc", [128, DS])
    dpbc_d = di("dpbc", [128, DS // 128])
    alog_d = di("alog", [128, 64])
    dcol_d = di("dcol", [128, 4])
    opwc_d = di("opwc", [DS, DM])
    xslab_d = di("xslab", [DM, SL])
    wg_d = di("wgT", [DM, HID], I32)
    wu_d = di("wuT", [DM, HID], I32)
    wd_d = di("wdT", [HID, DM], I32)
    sgc_d = di("sgcol", [128, HID // 128])
    suc_d = di("sucol", [128, HID // 128])
    sdc_d = di("sdcol", [128, DM // 128])
    wqkv_d = di("wqkv", [DM, 3 * DM])
    wgates_d = di("wgates", [DM, NH])
    wouth_d = di("wouth", [256, DM])
    rw_d = di("rw", [DM, NEXP])
    rb_d = di("rb", [1, NEXP])
    ew_d = di("ew", [DM, DM])
    ebc_d = di("ebcol", [128, DM // 128])
    kval_d = di("kval", [1, NEXP])
    lohi_d = di("lohi", [1, 2 * NEXP])
    ones128_d = di("ones128", [128, 1])
    ones128b_d = di("ones128b", [128, 1], BF16)
    ones1_d = di("ones1", [1, 128])
    epsv_d = di("epsv", [128, 2])
    attc_d = di("attc", [128, 3])
    mask_d = di("mask", [128, 2048])
    identb_d = di("identb", [128, 128], BF16)
    onesbc_d = di("onesbc", [32, 2])
    sel2_d2 = di("sel2t", [2, 32])
    x4_d = nc.dram_tensor("x4", [DM, SL], F32, kind="ExternalOutput").ap()

    def dn(name, shape, dt=F32):
        return nc.dram_tensor(name, shape, dt, kind="Internal").ap()

    hid_d = dn("hid", [DS, TT])
    sg_d = dn("sg", [DS, TT])
    u_d = dn("u", [DS, TT])
    ubf_d = dn("ubf", [DS, TT], BF16)
    pssm_d = dn("pssm", [160, TT])
    ssm_d = dn("ssm", [160, TT])
    dt_d = dn("dtf", [DS, TT])
    bcrow_d = dn("bcrow", [32, TT])
    yfin_d = dn("yfinf", [DS, TT], BF16)
    cb6_d = dn("cb6", [NC * DM, SL])
    rs6_d = dn("rs6", [DM, SL])
    lnscr_d = dn("lnscr", [DM, SL])
    x1_d = dn("x1f", [DM, SL])
    x1bf_d = dn("x1bff", [DM, SL], BF16)
    x2_d = dn("x2f", [DM, SL])
    x2bf_d = dn("x2bff", [DM, SL], BF16)
    h7_d = dn("h7f", [HID, SL], BF16)
    QB = 770
    cb3_d = dn("cb3", [8 * QB, SL], BF16)
    rs3_d = dn("rs3", [8 * QB, SL], BF16)
    o_dd = dn("Of", [256, TT], BF16)
    cb10_d = dn("cb10", [8 * DM, SL])
    rs10_d = dn("rs10", [DM, SL])
    x3_d = dn("x3f", [DM, SL])
    x3bf_d = dn("x3bff", [DM, SL], BF16)
    sslab_d = dn("sslabf", [SL, NEXP])
    sgath_d = dn("sgath", [TT, NEXP])
    w_d = dn("wtokf", [1, SL])

    G8 = [list(range(8))]
    GB = [[0, 1, 2, 3], [4, 5, 6, 7]]

    with ExitStack() as es:
        e = es.enter_context
        pool = Pool(nc, es, kb_per_partition=204)
        psA = e(nc.psum_tensor("psA", [128, 1024], F32))
        psB = e(nc.psum_tensor("psB", [128, 1024], F32))
        psC = e(nc.psum_tensor("psC", [128, 1024], F32))
        psT = e(nc.psum_tensor("psT", [128, 1024], BF16))
        P32 = [psA[:, 0:512], psA[:, 512:1024], psB[:, 0:512],
               psB[:, 512:1024], psC[:, 0:512], psC[:, 512:1024]]
        ops = []
        ctx = Ctx(nc, es, ops, pool, None)
        A = ops.append

        ones128 = pool.tile([128, 1])
        ones128b = pool.tile([128, 1], BF16)
        ones1 = pool.tile([1, 128])
        epsv = pool.tile([128, 2])
        A(("y", 16, lambda y: y.dma_start(out=ones128, in_=ones128_d[:])))
        A(("y", 16, lambda y: y.dma_start(out=ones128b, in_=ones128b_d[:])))
        A(("y", 16, lambda y: y.dma_start(out=ones1, in_=ones1_d[:])))
        A(("y", 16, lambda y: y.dma_start(out=epsv, in_=epsv_d[:])))
        BASE = pool.mark()
        eps_rms = epsv[0:1, 0:1]
        eps_ln = epsv[0:1, 1:2]

        KT, MT8, NO, NN4 = DM // 128, 8, 8, 4
        wst = pool.tile([128, 2 * DS])
        wbf = [pool.tile([128, 2 * DS], BF16) for _ in range(KT)]
        xst = pool.tile([128, 2048])
        xbf = [pool.tile([128, 2048], BF16) for _ in range(KT)]
        ev = pool.tile([128, 1024])
        for k in range(KT):
            A(("y", 16, lambda y, k=k: y.dma_start(out=wst, in_=win_d[k * 128:(k + 1) * 128, :])))
            A(("v", 1, lambda v, k=k: v.tensor_copy(out=wbf[k], in_=wst)))
        for no in range(NO):
            for k in range(KT):
                A(("y", 16, lambda y, k=k, no=no: y.dma_start(
                    out=xst, in_=xT_d[k * 128:(k + 1) * 128, no * 2048:(no + 1) * 2048])))
                A(("v", 1, lambda v, k=k: v.tensor_copy(out=xbf[k], in_=xst)))
            for m in range(MT8):
                for n2 in range(2):
                    def mmg(t, m=m, n2=n2):
                        inst = None
                        for half in range(2):
                            for k in range(KT):
                                inst = t.matmul(
                                    psA[:, half * 512:(half + 1) * 512],
                                    wbf[k][:, m * 128:(m + 1) * 128],
                                    xbf[k][:, n2 * 1024 + half * 512:n2 * 1024 + (half + 1) * 512],
                                    start=(k == 0), stop=(k == KT - 1))
                        return inst
                    A(("t", 1, mmg))
                    func = AF.Copy if m < 4 else AF.Silu
                    A(("s", 1, lambda s, func=func: s.activation(
                        out=ev, in_=psA[:, 0:1024], func=func)))
                    if m < 4:
                        dst = hid_d[m * 128:(m + 1) * 128,
                                    no * 2048 + n2 * 1024: no * 2048 + (n2 + 1) * 1024]
                    else:
                        dst = sg_d[(m - 4) * 128:(m - 3) * 128,
                                   no * 2048 + n2 * 1024: no * 2048 + (n2 + 1) * 1024]
                    A(("g", 16, lambda g, dst=dst: g.dma_start(out=dst, in_=ev)))
        pool.reset(BASE)

        CH = 4096
        cjobs = [(d, b, hf) for d in range(4) for b in range(2) for hf in range(2)]
        cw = pool.tile([128, 16])
        cb = pool.tile([128, 4])
        hh = pool.tile([128, CH + 4])
        acc = pool.tile([128, CH])
        uo = pool.tile([128, CH])
        ub = pool.tile([128, CH], BF16)
        A(("y", 16, lambda y: y.dma_start(out=cw, in_=convw_d[:])))
        A(("y", 16, lambda y: y.dma_start(out=cb, in_=convb_d[:])))
        for (d, b, hf) in cjobs:
            t0 = b * LB + hf * CH
            if hf == 0:
                A(("g", 1, lambda g: g.memset(hh[:, 0:4], 0.0)))
                A(("y", 16, lambda y, d=d, t0=t0: y.dma_start(
                    out=hh[:, 4:4 + CH], in_=hid_d[d * 128:(d + 1) * 128, t0:t0 + CH])))
            else:
                A(("y", 16, lambda y, d=d, t0=t0: y.dma_start(
                    out=hh[:, 1:4 + CH], in_=hid_d[d * 128:(d + 1) * 128, t0 - 3:t0 + CH])))
            def conv_fn(v, d=d):
                v.tensor_scalar_mul(acc, hh[:, 1:1 + CH], cw[:, d * 4 + 0: d * 4 + 1])
                inst = None
                for tap in range(1, KC):
                    inst = v.scalar_tensor_tensor(
                        out=acc, in0=hh[:, 1 + tap:1 + tap + CH],
                        scalar=cw[:, d * 4 + tap: d * 4 + tap + 1],
                        in1=acc, op0=ALU.mult, op1=ALU.add)
                return inst
            A(("v", 1, conv_fn))
            A(("s", 1, lambda s, d=d: s.activation(out=uo, in_=acc, func=AF.Silu,
                                                   bias=cb[:, d:d + 1])))
            A(("g", 16, lambda g, d=d, t0=t0: g.dma_start(
                out=u_d[d * 128:(d + 1) * 128, t0:t0 + CH], in_=uo)))
            A(("v", 1, lambda v: v.tensor_copy(out=ub, in_=uo)))
            A(("g", 16, lambda g, d=d, t0=t0: g.dma_start(
                out=ubf_d[d * 128:(d + 1) * 128, t0:t0 + CH], in_=ub)))
        pool.reset(BASE)

        KT4 = 4
        xwst = pool.tile([128, 160])
        xwbf = [pool.tile([128, 160], BF16) for _ in range(KT4)]
        ust = [pool.tile([128, 512], BF16) for _ in range(KT4)]
        ev1 = pool.tile([128, 512])
        ev2 = pool.tile([32, 512])
        for k in range(KT4):
            A(("y", 16, lambda y, k=k: y.dma_start(out=xwst, in_=xpw_d[k * 128:(k + 1) * 128, :])))
            A(("v", 1, lambda v, k=k: v.tensor_copy(out=xwbf[k], in_=xwst)))
        for nn in range(TT // 512):
            for k in range(KT4):
                A(("y", 16, lambda y, k=k, nn=nn: y.dma_start(
                    out=ust[k], in_=ubf_d[k * 128:(k + 1) * 128, nn * 512:(nn + 1) * 512])))
            emit_mm_group(ctx, P32[0], [
                (lambda k=k: xwbf[k][:, 0:128], lambda k=k: ust[k]) for k in range(KT4)])
            emit_mm_group(ctx, P32[1][0:32, :], [
                (lambda k=k: xwbf[k][:, 128:160], lambda k=k: ust[k]) for k in range(KT4)])
            A(("v", 1, lambda v: v.tensor_copy(out=ev1, in_=P32[0])))
            A(("v", 1, lambda v: v.tensor_copy(out=ev2, in_=P32[1][0:32, :])))
            A(("g", 16, lambda g, nn=nn: g.dma_start(
                out=pssm_d[0:128, nn * 512:(nn + 1) * 512], in_=ev1)))
            A(("g", 16, lambda g, nn=nn: g.dma_start(
                out=pssm_d[128:160, nn * 512:(nn + 1) * 512], in_=ev2)))
        A(("g", 1, lambda g: g.collective_compute(
            "AllReduce", ALU.add, G8, [pssm_d], [ssm_d])))
        pool.reset(BASE)

        dtnbf = pool.tile([128, TT], BF16)
        bcn = pool.tile([32, 2048])
        sdt = pool.tile([128, 2048])
        sbc = pool.tile([32, 2048])
        sq4 = pool.tile([128, 2048])
        sqb = pool.tile([32, 2048])
        mrow = pool.tile([1, 2048])
        rrow = pool.tile([1, 2048])
        m2b = pool.tile([2, 2048])
        onesbc = pool.tile([32, 2])
        sel2t = pool.tile([2, 32])
        A(("y", 16, lambda y: y.dma_start(out=onesbc, in_=onesbc_d[:])))
        A(("y", 16, lambda y: y.dma_start(out=sel2t, in_=sel2_d2[:])))
        for tc in range(TT // 2048):
            t0 = tc * 2048
            A(("y", 16, lambda y, t0=t0: y.dma_start(out=sdt, in_=ssm_d[0:128, t0:t0 + 2048])))
            A(("y", 16, lambda y, t0=t0: y.dma_start(out=sbc, in_=ssm_d[128:160, t0:t0 + 2048])))
            A(("s", 1, lambda s: s.activation(out=sq4, in_=sdt, func=AF.Square)))
            for ch in range(4):
                sl = slice(ch * 512, (ch + 1) * 512)
                A(("t", 1, lambda t, sl=sl: t.matmul(P32[0][0:1, :], ones128, sq4[:, sl], start=True, stop=True)))
                A(("v", 1, lambda v, sl=sl: v.tensor_copy(out=mrow[:, sl], in_=P32[0][0:1, :])))
            A(("s", 1, lambda s: s.activation(out=mrow, in_=mrow, func=AF.Ln, scale=1.0 / 128, bias=eps_rms)))
            A(("s", 1, lambda s: s.activation(out=rrow, in_=mrow, func=AF.Exp, scale=-0.5)))
            for ch in range(4):
                sl = slice(ch * 512, (ch + 1) * 512)
                A(("t", 1, lambda t, sl=sl: t.matmul(P32[1], ones1, rrow[:, sl], start=True, stop=True)))
                A(("v", 1, lambda v, sl=sl, t0=t0: v.tensor_tensor(
                    out=dtnbf[:, t0 + sl.start:t0 + sl.stop], in0=sdt[:, sl], in1=P32[1], op=ALU.mult)))
            A(("s", 1, lambda s: s.activation(out=sqb, in_=sbc, func=AF.Square)))
            for ch in range(4):
                sl = slice(ch * 512, (ch + 1) * 512)
                A(("t", 1, lambda t, sl=sl: t.matmul(P32[2][0:2, :], onesbc, sqb[:, sl], start=True, stop=True)))
                A(("v", 1, lambda v, sl=sl: v.tensor_copy(out=m2b[:, sl], in_=P32[2][0:2, :])))
            A(("s", 1, lambda s: s.activation(out=m2b, in_=m2b, func=AF.Ln, scale=1.0 / NSS, bias=epsv[0:2, 0:1])))
            A(("s", 1, lambda s: s.activation(out=m2b, in_=m2b, func=AF.Exp, scale=-0.5)))
            for ch in range(4):
                sl = slice(ch * 512, (ch + 1) * 512)
                A(("t", 1, lambda t, sl=sl: t.matmul(P32[3][0:32, :], sel2t, m2b[:, sl], start=True, stop=True)))
                A(("v", 1, lambda v, sl=sl: v.tensor_mul(bcn[:, sl], sbc[:, sl], P32[3][0:32, :])))
            A(("g", 16, lambda g, t0=t0: g.dma_start(out=bcrow_d[:, t0:t0 + 2048], in_=bcn)))
        dwbf = pool.tile([128, DS], BF16)
        dwst = pool.tile([128, DS])
        dpb4 = pool.tile([128, DS // 128])
        zr = pool.tile([128, 1024])
        za = pool.tile([128, 1024])
        dstg = pool.tile([128, 1024])
        A(("y", 16, lambda y: y.dma_start(out=dwst, in_=dpwc_d[:])))
        A(("v", 1, lambda v: v.tensor_copy(out=dwbf, in_=dwst)))
        A(("y", 16, lambda y: y.dma_start(out=dpb4, in_=dpbc_d[:])))
        for m in range(DS // 128):
            for nn in range(TT // 1024):
                sl = slice(nn * 1024, (nn + 1) * 1024)

                def mmdp(t, m=m, sl=sl):
                    t.matmul(psA[:, 0:512], dwbf[:, m * 128:(m + 1) * 128],
                             dtnbf[:, sl.start:sl.start + 512], start=True, stop=True)
                    return t.matmul(psA[:, 512:1024], dwbf[:, m * 128:(m + 1) * 128],
                                    dtnbf[:, sl.start + 512:sl.stop], start=True, stop=True)
                A(("t", 1, mmdp))
                A(("s", 1, lambda s, m=m: s.activation(
                    out=zr, in_=psA[:, 0:1024], func=AF.Relu, bias=dpb4[:, m:m + 1])))
                A(("s", 1, lambda s, m=m: s.activation(
                    out=za, in_=psA[:, 0:1024], func=AF.Abs, bias=dpb4[:, m:m + 1])))
                A(("s", 1, lambda s: s.activation(out=za, in_=za, func=AF.Exp, scale=-1.0)))
                A(("s", 1, lambda s: s.activation(out=za, in_=za, func=AF.Ln, bias=1.0)))
                A(("v", 1, lambda v: v.tensor_add(dstg, zr, za)))
                A(("g", 16, lambda g, m=m, sl=sl: g.dma_start(
                    out=dt_d[m * 128:(m + 1) * 128, sl], in_=dstg)))
        pool.reset(BASE)
        emit_scan(ctx, psA, psB, psC, dict(
            dt=dt_d, u=u_d, sg=sg_d, bc=bcrow_d, alog=alog_d, dcol=dcol_d,
            y=yfin_d, ones1=ones1), BASE)
        pool.reset(BASE)
        emit_tail(ctx, psA, psB, psC, psT, dict(
            yfin=yfin_d, opwc=opwc_d, cb6=cb6_d, rs6=rs6_d, xslab=xslab_d,
            lnscr=lnscr_d, x1=x1_d, x1bf=x1bf_d,
            wg=wg_d, wu=wu_d, wd=wd_d, sgc=sgc_d, suc=suc_d, sdc=sdc_d, h7=h7_d,
            x2=x2_d, x2bf=x2bf_d, wqkv=wqkv_d, wgates=wgates_d,
            cb3=cb3_d, rs3=rs3_d, o=o_dd, identb=identb_d, attc=attc_d,
            mask=mask_d, wouth=wouth_d, cb10=cb10_d, rs10=rs10_d,
            x3=x3_d, x3bf=x3bf_d, rw=rw_d, rb=rb_d, sslab=sslab_d,
            sgath=sgath_d, ew=ew_d, ebc=ebc_d, kval=kval_d, lohi=lohi_d,
            w=w_d, x4=x4_d),
            dict(ones128=ones128, ones128b=ones128b, ones1=ones1,
                 eps_ln=eps_ln), BASE, G8)
        run_serial(nc, es, ops)
    return nc


def scan_host_consts(c, A_log, D_param):
    ds = c * DS
    alog = np.zeros((128, 64), np.float32)
    for dg in range(4):
        alog[:, dg * 16:(dg + 1) * 16] = A_log[ds + dg * 128: ds + (dg + 1) * 128, :]
    dcol = np.zeros((128, 4), np.float32)
    for dg in range(4):
        dcol[:, dg] = D_param[ds + dg * 128: ds + (dg + 1) * 128]
    return {"alog": alog, "dcol": dcol}


def _run(nc, in_maps, names):
    res = run_bass_kernel_spmd(nc, in_maps, core_ids=list(range(NC)))
    return [[np.asarray(res.results[c][n]) for n in names] for c in range(NC)]


def kernel_unfused(**inp):
    f32 = np.float32
    x = np.ascontiguousarray(inp["L0_attn_x"].astype(f32, copy=False))
    xT = np.ascontiguousarray(x.reshape(TT, DM).T)
    ones128 = np.ones((128, 1), f32)
    ones1 = np.ones((1, 128), f32)
    ones128b = np.ones((128, 1), BF16NP)
    epsr = np.full((128, 1), RMS_EPS, f32)
    epsl = np.full((128, 1), LN_EPS, f32)

    def colarr(s, nt):
        a = np.zeros((128, nt), f32)
        for m in range(nt):
            a[:, m] = s[m * 128:(m + 1) * 128]
        return a

    W = inp["L0_attn_in_proj_w"].astype(f32, copy=False)
    maps = []
    for c in range(NC):
        ds = c * DS
        win = np.ascontiguousarray(
            np.concatenate([W[:, ds:ds + DS], W[:, DI + ds:DI + ds + DS]], axis=1))
        maps.append({"xT": xT, "win": win})
    r1 = _run(build_inproj(), maps, ["hid", "sg"])

    cw = inp["L0_attn_conv_w"].astype(f32, copy=False).reshape(DI, KC)
    cb = inp["L0_attn_conv_b"].astype(f32, copy=False)
    maps = []
    for c in range(NC):
        ds = c * DS
        convw = np.zeros((128, 16), f32)
        convb = np.zeros((128, 4), f32)
        for d in range(4):
            convw[:, d * 4:(d + 1) * 4] = cw[ds + d * 128: ds + (d + 1) * 128]
            convb[:, d] = cb[ds + d * 128: ds + (d + 1) * 128]
        maps.append({"hid": r1[c][0], "convw": convw, "convb": convb})
    r2 = _run(build_conv(), maps, ["u", "ubf"])

    xpw = inp["L0_attn_x_proj_w"].astype(f32, copy=False)
    maps = [{"ubf": r2[c][1], "xpw": np.ascontiguousarray(xpw[c * DS:(c + 1) * DS])}
            for c in range(NC)]
    r3 = _run(build_xproj(), maps, ["pdt", "pbc"])

    dpw = inp["L0_attn_dt_proj_w"].astype(f32, copy=False)
    dpb = colarr(inp["L0_attn_dt_proj_b"].astype(f32, copy=False), DI // 128)
    onesbc = np.zeros((32, 2), f32); onesbc[:16, 0] = 1; onesbc[16:, 1] = 1
    sel2 = np.zeros((2, 32), f32); sel2[0, :16] = 1; sel2[1, 16:] = 1
    maps = []
    for c in range(NC):
        t0 = c * SL
        pa = np.stack([r3[i][0][:, t0:t0 + SL] for i in range(NC)])
        pb = np.stack([r3[i][1][:, t0:t0 + SL] for i in range(NC)])
        maps.append({"pdt_all": np.ascontiguousarray(pa), "pbc_all": np.ascontiguousarray(pb),
                     "dpw": dpw, "dpb": dpb, "ones128": ones128, "ones1": ones1,
                     "onesbc": onesbc, "sel2": sel2, "epsv": epsr})
    r4 = _run(build_ssmmid(), maps, ["dt", "bc"])
    del r3

    dt_full = np.concatenate([r4[c][0] for c in range(NC)], axis=1)
    bc_full = np.concatenate([r4[c][1] for c in range(NC)], axis=1)
    del r4
    A_log = inp["L0_attn_A_log"].astype(f32, copy=False)
    D_param = inp["L0_attn_D_param"].astype(f32, copy=False)
    brow = np.ascontiguousarray(bc_full[:16])
    crow = np.ascontiguousarray(bc_full[16:])
    maps = []
    for c in range(NC):
        ds = c * DS
        m = scan_host_consts(c, A_log, D_param)
        m.update({"dt_c": np.ascontiguousarray(dt_full[ds:ds + DS]),
                  "u_c": r2[c][0], "sg_c": r1[c][1],
                  "brow": brow, "crow": crow})
        maps.append(m)
    r5 = _run(build_scan(), maps, ["yfin"])
    del dt_full, bc_full, r1, r2

    yfin = np.concatenate([r5[c][0] for c in range(NC)], axis=0)
    del r5
    opw = inp["L0_attn_out_proj_w"].astype(f32, copy=False)
    maps = []
    for c in range(NC):
        t0 = c * SL
        maps.append({"yslab": np.ascontiguousarray(yfin[:, t0:t0 + SL]),
                     "opw": opw, "xslab": np.ascontiguousarray(xT[:, t0:t0 + SL]),
                     "ones128": ones128, "ones1": ones1, "epsv": epsl})
    r6 = _run(build_outproj(), maps, ["x1", "x1bf"])
    del yfin

    wgT = np.ascontiguousarray(inp["L0_ffn_wq_gate"].T)
    wuT = np.ascontiguousarray(inp["L0_ffn_wq_up"].T)
    wdT = np.ascontiguousarray(inp["L0_ffn_wq_down"].T)
    maps = []
    for c in range(NC):
        maps.append({"x1": r6[c][0], "x1bf": r6[c][1],
                     "wgT": wgT, "wuT": wuT, "wdT": wdT,
                     "sgcol": colarr(inp["L0_ffn_s_gate"].astype(f32, copy=False), HID // 128),
                     "sucol": colarr(inp["L0_ffn_s_up"].astype(f32, copy=False), HID // 128),
                     "sdcol": colarr(inp["L0_ffn_s_down"].astype(f32, copy=False), DM // 128),
                     "ones128": ones128, "ones1": ones1, "epsv": epsl})
    r7 = _run(build_ffn0(), maps, ["x2", "x2bf"])
    del r6, wgT, wuT, wdT

    maps = [{"x2bf": r7[c][1],
             "wqkv": inp["L1_attn_w_qkv"].astype(f32, copy=False),
             "wg": inp["L1_attn_w_gates"].astype(f32, copy=False)}
            for c in range(NC)]
    r8 = _run(build_qkv(), maps, ["qkv", "g"])

    qkv = np.concatenate([r8[c][0] for c in range(NC)], axis=1)
    g = np.concatenate([r8[c][1] for c in range(NC)], axis=1)
    del r8
    mask = np.zeros((128, 2048), f32)
    jj = np.arange(-1024, 1024)
    kk = np.arange(128)
    mask[kk[:, None] > jj[None, :]] = -1e9
    attc = np.tile(np.array([[(DH ** -0.5) / 50.0, 50.0, -50.0]], f32), (128, 1))
    maps = []
    for c in range(NC):
        b, hg = c // 4, c % 4
        t0 = b * LB
        hs = [4 * hg + i for i in range(4)]
        Q = np.concatenate([qkv[h * 128:(h + 1) * 128, t0:t0 + LB] for h in hs], axis=0)
        Kf = np.concatenate([qkv[DM + h * 128:DM + (h + 1) * 128, t0:t0 + LB] for h in hs], axis=0)
        V = np.concatenate([qkv[2 * DM + h * 128:2 * DM + (h + 1) * 128, t0:t0 + LB] for h in hs], axis=0)
        maps.append({"Q": np.ascontiguousarray(Q), "Kf": np.ascontiguousarray(Kf),
                     "Vt": np.ascontiguousarray(V.T),
                     "grows": np.ascontiguousarray(g[4 * hg:4 * hg + 4, t0:t0 + LB]),
                     "mask": mask, "ones128b": ones128b, "ones1": ones1, "attc": attc})
    r9 = _run(build_attn(), maps, ["O"])
    del qkv, g

    O_full = np.zeros((DM, TT), BF16NP)
    for c in range(NC):
        b, hg = c // 4, c % 4
        t0 = b * LB
        for i in range(4):
            h = 4 * hg + i
            O_full[h * 128:(h + 1) * 128, t0:t0 + LB] = r9[c][0][i * 128:(i + 1) * 128]
    del r9
    maps = []
    for c in range(NC):
        t0 = c * SL
        maps.append({"Oslab": np.ascontiguousarray(O_full[:, t0:t0 + SL]),
                     "wout": inp["L1_attn_w_out"].astype(f32, copy=False),
                     "x2": r7[c][0],
                     "rw": inp["L1_ffn_router_w"].astype(f32, copy=False),
                     "rb": inp["L1_ffn_router_b"].astype(f32, copy=False).reshape(1, NEXP),
                     "ones128": ones128, "ones1": ones1, "epsv": epsl})
    r10 = _run(build_attnout(), maps, ["x3", "x3bf", "S"])
    del O_full, r7

    S_full = np.concatenate([r10[c][2] for c in range(NC)], axis=0)
    ebcol = colarr(inp["L1_ffn_expert_b"].astype(f32, copy=False), DM // 128)
    lohi = np.concatenate([np.zeros((1, NEXP), f32), np.ones((1, NEXP), f32)], axis=1)
    Sall = np.ascontiguousarray(S_full.reshape(128, TT * NEXP // 128))
    maps = []
    for c in range(NC):
        t0 = c * SL
        maps.append({"Sall": Sall,
                     "Sslab": np.ascontiguousarray(S_full[t0:t0 + SL].reshape(128, SL * NEXP // 128)),
                     "x3": r10[c][0], "x3bf": r10[c][1],
                     "ew": inp["L1_ffn_expert_w"].astype(f32, copy=False),
                     "ebcol": ebcol,
                     "kval": np.full((1, NEXP), float(KTOP), f32),
                     "lohi": lohi,
                     "ones128": ones128, "ones1": ones1, "epsv": epsl})
    r11 = _run(build_moe(), maps, ["x4"])
    del r10

    x4 = np.concatenate([r11[c][0] for c in range(NC)], axis=1)
    return np.ascontiguousarray(x4.T.reshape(2, LB, DM)).astype(np.float32)


def kernel(**inp):
    f32 = np.float32
    x = np.ascontiguousarray(inp["L0_attn_x"].astype(f32, copy=False))
    xT = np.ascontiguousarray(x.reshape(TT, DM).T)

    def colarr(s, nt):
        a = np.zeros((128, nt), f32)
        for m in range(nt):
            a[:, m] = s[m * 128:(m + 1) * 128]
        return a

    W = inp["L0_attn_in_proj_w"].astype(f32, copy=False)
    cw = inp["L0_attn_conv_w"].astype(f32, copy=False).reshape(DI, KC)
    cbv = inp["L0_attn_conv_b"].astype(f32, copy=False)
    xpw = inp["L0_attn_x_proj_w"].astype(f32, copy=False)
    dpw = inp["L0_attn_dt_proj_w"].astype(f32, copy=False)
    dpb = inp["L0_attn_dt_proj_b"].astype(f32, copy=False)
    opw = inp["L0_attn_out_proj_w"].astype(f32, copy=False)
    A_log = inp["L0_attn_A_log"].astype(f32, copy=False)
    D_param = inp["L0_attn_D_param"].astype(f32, copy=False)
    wgT = np.ascontiguousarray(inp["L0_ffn_wq_gate"].T)
    wuT = np.ascontiguousarray(inp["L0_ffn_wq_up"].T)
    wdT = np.ascontiguousarray(inp["L0_ffn_wq_down"].T)
    wqkv = inp["L1_attn_w_qkv"].astype(f32, copy=False)
    wgates = inp["L1_attn_w_gates"].astype(f32, copy=False)
    wout = inp["L1_attn_w_out"].astype(f32, copy=False)
    rw = inp["L1_ffn_router_w"].astype(f32, copy=False)
    rb = inp["L1_ffn_router_b"].astype(f32, copy=False).reshape(1, NEXP)
    ew = inp["L1_ffn_expert_w"].astype(f32, copy=False)
    ebcol = colarr(inp["L1_ffn_expert_b"].astype(f32, copy=False), DM // 128)

    epsv = np.zeros((128, 2), f32)
    epsv[:, 0] = RMS_EPS
    epsv[:, 1] = LN_EPS
    mask = np.zeros((128, 2048), f32)
    jj = np.arange(-1024, 1024)
    kk = np.arange(128)
    mask[kk[:, None] > jj[None, :]] = -1e9
    attc = np.tile(np.array([[(DH ** -0.5) / 50.0, 50.0, -50.0]], f32), (128, 1))
    onesbc = np.zeros((32, 2), f32); onesbc[:16, 0] = 1; onesbc[16:, 1] = 1
    sel2t = np.zeros((2, 32), f32); sel2t[0, :16] = 1; sel2t[1, 16:] = 1
    lohi = np.concatenate([np.zeros((1, NEXP), f32), np.ones((1, NEXP), f32)], axis=1)

    nc = build_fused()
    maps = []
    for c in range(NC):
        ds = c * DS
        t0 = c * SL
        m = scan_host_consts(c, A_log, D_param)
        win = np.ascontiguousarray(
            np.concatenate([W[:, ds:ds + DS], W[:, DI + ds:DI + ds + DS]], axis=1))
        convw = np.zeros((128, 16), f32)
        convb = np.zeros((128, 4), f32)
        for dd in range(4):
            convw[:, dd * 4:(dd + 1) * 4] = cw[ds + dd * 128: ds + (dd + 1) * 128]
            convb[:, dd] = cbv[ds + dd * 128: ds + (dd + 1) * 128]
        m.update({
            "xT": xT, "win": win, "convw": convw, "convb": convb,
            "xpw": np.ascontiguousarray(xpw[ds:ds + DS]),
            "dpwc": np.ascontiguousarray(dpw[:, ds:ds + DS]),
            "dpbc": colarr(dpb[ds:ds + DS], DS // 128),
            "opwc": np.ascontiguousarray(opw[ds:ds + DS]),
            "xslab": np.ascontiguousarray(xT[:, t0:t0 + SL]),
            "wgT": wgT, "wuT": wuT, "wdT": wdT,
            "sgcol": colarr(inp["L0_ffn_s_gate"].astype(f32, copy=False), HID // 128),
            "sucol": colarr(inp["L0_ffn_s_up"].astype(f32, copy=False), HID // 128),
            "sdcol": colarr(inp["L0_ffn_s_down"].astype(f32, copy=False), DM // 128),
            "wqkv": wqkv, "wgates": wgates,
            "wouth": np.ascontiguousarray(wout[2 * c * 128:2 * c * 128 + 256, :]),
            "rw": rw, "rb": rb, "ew": ew, "ebcol": ebcol,
            "kval": np.full((1, NEXP), float(KTOP), f32), "lohi": lohi,
            "ones128": np.ones((128, 1), f32),
            "ones128b": np.ones((128, 1), BF16NP),
            "ones1": np.ones((1, 128), f32),
            "epsv": epsv, "attc": attc, "mask": mask,
            "identb": np.eye(128).astype(BF16NP),
            "onesbc": onesbc, "sel2t": sel2t,
        })
        maps.append(m)
    res = run_bass_kernel_spmd(nc, maps, core_ids=list(range(NC)))
    x4 = np.concatenate([np.asarray(res.results[c]["x4"]) for c in range(NC)], axis=1)
    return np.ascontiguousarray(x4.T.reshape(2, LB, DM)).astype(np.float32)


# version 15
# speedup vs baseline: 1.1744x; 1.1744x over previous
import numpy as np
import ml_dtypes
from contextlib import ExitStack
import concourse.bass as bass
import concourse.mybir as mybir
from concourse.bass_utils import run_bass_kernel_spmd

F32 = mybir.dt.float32
BF16 = mybir.dt.bfloat16
I32 = mybir.dt.int32
AF = mybir.ActivationFunctionType
ALU = mybir.AluOpType
AX = mybir.AxisListType
BF16NP = ml_dtypes.bfloat16

NC = 8
DM = 2048
DI = 4096
DS = DI // NC
NSS = 16
DTR = 128
KC = 4
TT = 16384
LB = 8192
SL = TT // NC
HID = 5632
NH = 16
DH = 128
NEXP = 16
KTOP = 1024

RMS_EPS = 1e-6
LN_EPS = 1e-5


RACECHECK = False

def _bass():
    return bass.Bass("TRN2", target_bir_lowering=False,
                     detect_race_conditions=RACECHECK)


def build_inproj():
    nc = _bass()
    KT = DM // 128
    MT = (2 * DS) // 128
    NO = 8
    NN = 4

    xT_d = nc.dram_tensor("xT", [DM, TT], F32, kind="ExternalInput").ap()
    win_d = nc.dram_tensor("win", [DM, 2 * DS], F32, kind="ExternalInput").ap()
    hid_d = nc.dram_tensor("hid", [DS, TT], F32, kind="ExternalOutput").ap()
    sg_d = nc.dram_tensor("sg", [DS, TT], F32, kind="ExternalOutput").ap()

    with ExitStack() as es:
        e = es.enter_context
        wst = e(nc.sbuf_tensor("wst", [128, 2 * DS], F32))
        wbf = [e(nc.sbuf_tensor(f"wbf{k}", [128, 2 * DS], BF16)) for k in range(KT)]
        xst = [e(nc.sbuf_tensor(f"xst{i}", [128, 2048], F32)) for i in range(2)]
        xbf = [e(nc.sbuf_tensor(f"xbf{k}", [128, 2048], BF16)) for k in range(KT)]
        ev = [e(nc.sbuf_tensor(f"ev{i}", [128, 512], F32)) for i in range(2)]
        ps = [e(nc.psum_tensor(f"ps{i}", [128, 512], F32)) for i in range(2)]
        wl = e(nc.semaphore("wl"))
        xl = [e(nc.semaphore(f"xl{i}")) for i in range(2)]
        ca = e(nc.semaphore("ca"))
        mm = e(nc.semaphore("mm"))
        evs = e(nc.semaphore("evs"))
        st = [e(nc.semaphore(f"st{i}")) for i in range(2)]
        blk = e(nc.Block())

        jobs = [(no, m, nn) for no in range(NO) for m in range(MT) for nn in range(NN)]

        @blk.sync
        def _(s):
            for k in range(KT):
                s.wait_ge(ca, k)
                s.dma_start(out=wst[:], in_=win_d[k * 128:(k + 1) * 128, :]).then_inc(wl, 16)
            nld = 0
            for no in range(NO):
                for k in range(KT):
                    s.wait_ge(ca, KT + nld - 1)
                    s.dma_start(
                        out=xst[nld % 2][:],
                        in_=xT_d[k * 128:(k + 1) * 128, no * 2048:(no + 1) * 2048],
                    ).then_inc(xl[nld % 2], 16)
                    nld += 1

        @blk.vector
        def _(v):
            for k in range(KT):
                v.wait_ge(wl, 16 * (k + 1))
                v.tensor_copy(out=wbf[k][:], in_=wst[:]).then_inc(ca, 1)
            nld = 0
            for no in range(NO):
                if no >= 1:
                    v.wait_ge(mm, no * MT * NN)
                for k in range(KT):
                    v.wait_ge(xl[nld % 2], 16 * (nld // 2 + 1))
                    v.tensor_copy(out=xbf[k][:], in_=xst[nld % 2][:]).then_inc(ca, 1)
                    nld += 1

        @blk.tensor
        def _(t):
            for j, (no, m, nn) in enumerate(jobs):
                if m == 0 and nn == 0:
                    t.wait_ge(ca, KT + (no + 1) * KT)
                if j >= 2:
                    t.wait_ge(evs, j - 1)
                for k in range(KT):
                    inst = t.matmul(
                        ps[j % 2][:],
                        wbf[k][:, m * 128:(m + 1) * 128],
                        xbf[k][:, nn * 512:(nn + 1) * 512],
                        start=(k == 0), stop=(k == KT - 1),
                    )
                inst.then_inc(mm, 1)

        @blk.scalar
        def _(sc):
            for j, (no, m, nn) in enumerate(jobs):
                sc.wait_ge(mm, j + 1)
                if j >= 2:
                    sc.wait_ge(st[j % 2], 16 * (j // 2))
                func = AF.Copy if m < MT // 2 else AF.Silu
                sc.activation(out=ev[j % 2][:], in_=ps[j % 2][:], func=func).then_inc(evs, 1)

        @blk.gpsimd
        def _(g):
            for j, (no, m, nn) in enumerate(jobs):
                g.wait_ge(evs, j + 1)
                if m < MT // 2:
                    dst = hid_d[m * 128:(m + 1) * 128,
                                no * 2048 + nn * 512: no * 2048 + (nn + 1) * 512]
                else:
                    mg = m - MT // 2
                    dst = sg_d[mg * 128:(mg + 1) * 128,
                               no * 2048 + nn * 512: no * 2048 + (nn + 1) * 512]
                g.dma_start(out=dst, in_=ev[j % 2][:]).then_inc(st[j % 2], 16)
            g.wait_ge(st[0], 16 * (len(jobs) // 2))
            g.wait_ge(st[1], 16 * (len(jobs) // 2))
    return nc


def build_conv():
    nc = _bass()
    DT4 = DS // 128
    CH = 4096
    NHF = LB // CH

    hid_d = nc.dram_tensor("hid", [DS, TT], F32, kind="ExternalInput").ap()
    convw_d = nc.dram_tensor("convw", [128, 4 * DT4], F32, kind="ExternalInput").ap()
    convb_d = nc.dram_tensor("convb", [128, DT4], F32, kind="ExternalInput").ap()
    u_d = nc.dram_tensor("u", [DS, TT], F32, kind="ExternalOutput").ap()
    ubf_d = nc.dram_tensor("ubf", [DS, TT], BF16, kind="ExternalOutput").ap()

    jobs = [(d, b, hf) for d in range(DT4) for b in range(2) for hf in range(NHF)]

    with ExitStack() as es:
        e = es.enter_context
        h = [e(nc.sbuf_tensor(f"h{i}", [128, CH + 4], F32)) for i in range(2)]
        acc = [e(nc.sbuf_tensor(f"acc{i}", [128, CH], F32)) for i in range(2)]
        uo = [e(nc.sbuf_tensor(f"uo{i}", [128, CH], F32)) for i in range(2)]
        ub = [e(nc.sbuf_tensor(f"ub{i}", [128, CH], BF16)) for i in range(2)]
        cw = e(nc.sbuf_tensor("cw", [128, 4 * DT4], F32))
        cb = e(nc.sbuf_tensor("cb", [128, DT4], F32))
        cst = e(nc.semaphore("cst"))
        hl = [e(nc.semaphore(f"hl{i}")) for i in range(2)]
        zi = e(nc.semaphore("zi"))
        vc = e(nc.semaphore("vc"))
        ac = e(nc.semaphore("ac"))
        cc = e(nc.semaphore("cc"))
        su = [e(nc.semaphore(f"su{i}")) for i in range(2)]
        sb = [e(nc.semaphore(f"sb{i}")) for i in range(2)]
        blk = e(nc.Block())

        @blk.sync
        def _(s):
            s.dma_start(out=cw[:], in_=convw_d[:]).then_inc(cst, 16)
            s.dma_start(out=cb[:], in_=convb_d[:]).then_inc(cst, 16)
            for j, (d, b, hf) in enumerate(jobs):
                if j >= 2:
                    s.wait_ge(vc, j - 1)
                t0 = b * LB + hf * CH
                if hf == 0:
                    s.wait_ge(zi, j + 1)
                    s.dma_start(out=h[j % 2][:, 3:3 + CH],
                                in_=hid_d[d * 128:(d + 1) * 128, t0:t0 + CH]).then_inc(hl[j % 2], 16)
                else:
                    s.dma_start(out=h[j % 2][:, 0:3 + CH],
                                in_=hid_d[d * 128:(d + 1) * 128, t0 - 3:t0 + CH]).then_inc(hl[j % 2], 16)
            s.wait_ge(su[0], 16 * (len(jobs) // 2))
            s.wait_ge(su[1], 16 * (len(jobs) // 2))
            s.wait_ge(sb[0], 16 * (len(jobs) // 2))
            s.wait_ge(sb[1], 16 * (len(jobs) // 2))

        @blk.gpsimd
        def _(g):
            for j, (d, b, hf) in enumerate(jobs):
                if hf == 0:
                    if j >= 2:
                        g.wait_ge(vc, j - 1)
                    g.memset(h[j % 2][:, 0:3], 0.0).then_inc(zi, 1)
                else:
                    g.engine_nop().then_inc(zi, 1)
                t0 = b * LB + hf * CH
                g.wait_ge(ac, j + 1)
                g.dma_start(out=u_d[d * 128:(d + 1) * 128, t0:t0 + CH],
                            in_=uo[j % 2][:]).then_inc(su[j % 2], 16)
                g.wait_ge(cc, j + 1)
                g.dma_start(out=ubf_d[d * 128:(d + 1) * 128, t0:t0 + CH],
                            in_=ub[j % 2][:]).then_inc(sb[j % 2], 16)

        @blk.vector
        def _(v):
            v.wait_ge(cst, 32)
            for j, (d, b, hf) in enumerate(jobs):
                v.wait_ge(hl[j % 2], 16 * (j // 2 + 1))
                if j >= 2:
                    v.wait_ge(ac, j - 1)
                hh = h[j % 2]
                aa = acc[j % 2]
                v.tensor_scalar_mul(aa[:], hh[:, 0:CH], cw[:, d * 4 + 0: d * 4 + 1])
                for tap in range(1, KC):
                    inst = v.scalar_tensor_tensor(
                        out=aa[:], in0=hh[:, tap:tap + CH],
                        scalar=cw[:, d * 4 + tap: d * 4 + tap + 1],
                        in1=aa[:], op0=ALU.mult, op1=ALU.add)
                inst.then_inc(vc, 1)
                v.wait_ge(ac, j + 1)
                if j >= 2:
                    v.wait_ge(sb[j % 2], 16 * (j // 2))
                v.tensor_copy(out=ub[j % 2][:], in_=uo[j % 2][:]).then_inc(cc, 1)

        @blk.scalar
        def _(sc):
            for j, (d, b, hf) in enumerate(jobs):
                sc.wait_ge(vc, j + 1)
                if j >= 2:
                    sc.wait_ge(su[j % 2], 16 * (j // 2))
                    sc.wait_ge(cc, j - 1)
                sc.activation(out=uo[j % 2][:], in_=acc[j % 2][:], func=AF.Silu,
                              bias=cb[:, d:d + 1]).then_inc(ac, 1)
    return nc


def build_xproj():
    nc = _bass()
    KT = DS // 128
    NNJ = TT // 512

    ubf_d = nc.dram_tensor("ubf", [DS, TT], BF16, kind="ExternalInput").ap()
    xpw_d = nc.dram_tensor("xpw", [DS, 160], F32, kind="ExternalInput").ap()
    pdt_d = nc.dram_tensor("pdt", [128, TT], F32, kind="ExternalOutput").ap()
    pbc_d = nc.dram_tensor("pbc", [32, TT], F32, kind="ExternalOutput").ap()

    with ExitStack() as es:
        e = es.enter_context
        wst = e(nc.sbuf_tensor("wst", [128, 160], F32))
        wbf = [e(nc.sbuf_tensor(f"wbf{k}", [128, 160], BF16)) for k in range(KT)]
        ust = [e(nc.sbuf_tensor(f"ust{i}", [128, 512], BF16)) for i in range(2 * KT)]
        ev1 = [e(nc.sbuf_tensor(f"ev1{i}", [128, 512], F32)) for i in range(2)]
        ev2 = [e(nc.sbuf_tensor(f"ev2{i}", [32, 512], F32)) for i in range(2)]
        ps1 = [e(nc.psum_tensor(f"ps1{i}", [128, 512], F32)) for i in range(2)]
        ps2 = [e(nc.psum_tensor(f"ps2{i}", [32, 512], F32)) for i in range(2)]
        wl = e(nc.semaphore("wl"))
        ca = e(nc.semaphore("ca"))
        ul = [e(nc.semaphore(f"ul{i}")) for i in range(2)]
        mm = e(nc.semaphore("mm"))
        evs = e(nc.semaphore("evs"))
        st = [e(nc.semaphore(f"st{i}")) for i in range(2)]
        blk = e(nc.Block())

        @blk.sync
        def _(s):
            for k in range(KT):
                s.wait_ge(ca, k)
                s.dma_start(out=wst[:], in_=xpw_d[k * 128:(k + 1) * 128, :]).then_inc(wl, 16)
            for nn in range(NNJ):
                if nn >= 2:
                    s.wait_ge(mm, nn - 1)
                for k in range(KT):
                    bufi = (nn % 2) * KT + k
                    s.dma_start(
                        out=ust[bufi][:],
                        in_=ubf_d[k * 128:(k + 1) * 128, nn * 512:(nn + 1) * 512],
                    ).then_inc(ul[nn % 2], 16)

        @blk.vector
        def _(v):
            for k in range(KT):
                v.wait_ge(wl, 16 * (k + 1))
                v.tensor_copy(out=wbf[k][:], in_=wst[:]).then_inc(ca, 1)
            for nn in range(NNJ):
                v.wait_ge(mm, nn + 1)
                if nn >= 2:
                    v.wait_ge(st[nn % 2], 32 * (nn // 2))
                v.tensor_copy(out=ev1[nn % 2][:], in_=ps1[nn % 2][:])
                v.tensor_copy(out=ev2[nn % 2][:], in_=ps2[nn % 2][:]).then_inc(evs, 1)

        @blk.tensor
        def _(t):
            t.wait_ge(ca, KT)
            for nn in range(NNJ):
                t.wait_ge(ul[nn % 2], 16 * KT * (nn // 2 + 1))
                if nn >= 2:
                    t.wait_ge(evs, nn - 1)
                for k in range(KT):
                    bufi = (nn % 2) * KT + k
                    t.matmul(ps1[nn % 2][:], wbf[k][:, 0:128], ust[bufi][:],
                             start=(k == 0), stop=(k == KT - 1))
                for k in range(KT):
                    bufi = (nn % 2) * KT + k
                    inst = t.matmul(ps2[nn % 2][:], wbf[k][:, 128:160], ust[bufi][:],
                                    start=(k == 0), stop=(k == KT - 1))
                inst.then_inc(mm, 1)

        @blk.scalar
        def _(sc):
            for nn in range(NNJ):
                sc.wait_ge(evs, nn + 1)
                sc.dma_start(out=pdt_d[:, nn * 512:(nn + 1) * 512], in_=ev1[nn % 2][:]).then_inc(st[nn % 2], 16)
                sc.dma_start(out=pbc_d[:, nn * 512:(nn + 1) * 512], in_=ev2[nn % 2][:]).then_inc(st[nn % 2], 16)
            sc.wait_ge(st[0], 32 * (NNJ // 2))
            sc.wait_ge(st[1], 32 * (NNJ // 2))
    return nc


ENG_KEYS = "vstgy"
NSLOT = 8


def run_serial(nc, es, ops):
    comp_sems = {k: es.enter_context(nc.semaphore(f"ser_{k}")) for k in ENG_KEYS}
    dma_sems = {k: [es.enter_context(nc.semaphore(f"dma_{k}{i}")) for i in range(NSLOT)]
                for k in ENG_KEYS}
    blk = es.enter_context(nc.Block())

    vkeys = [("c", k) for k in ENG_KEYS] + [("d", k, i) for k in ENG_KEYS for i in range(NSLOT)]
    tot = {vk: 0 for vk in vkeys}
    cum = []
    kinds = []
    norm_ops = []
    dma_n = {k: 0 for k in ENG_KEYS}
    for o in ops:
        if len(o) == 3:
            ek, inc, fn = o
            selfm, reads, writes = False, None, None
        elif len(o) == 4:
            ek, inc, fn, selfm = o
            reads, writes = None, None
        else:
            ek, inc, fn, reads, writes = o
            selfm = False
        if inc == 16:
            vk = ("d", ek, dma_n[ek] % NSLOT)
            dma_n[ek] += 1
            tot[vk] += 16
        else:
            vk = ("c", ek)
            tot[vk] += inc
        kinds.append(vk)
        cum.append(dict(tot))
        norm_ops.append((ek, inc, fn, selfm, reads, writes))

    n = len(norm_ops)
    last_writer = {}
    readers = {}
    last_barrier = -1
    op_waits = [None] * n
    for i, (ek, inc, fn, selfm, reads, writes) in enumerate(norm_ops):
        if reads is None:
            op_waits[i] = None
            last_barrier = i
            last_writer = {}
            readers = {}
            continue
        deps = set()
        if last_barrier >= 0:
            deps.add(last_barrier)
        for r in reads:
            j = last_writer.get(id(r))
            if j is not None:
                deps.add(j)
        for w in writes:
            j = last_writer.get(id(w))
            if j is not None:
                deps.add(j)
            for j in readers.get(id(w), ()):
                deps.add(j)
        vk = kinds[i]
        waits = {}
        for j in deps:
            if j == i:
                continue
            vkj = kinds[j]
            v = cum[j][vkj]
            if vkj == ("c", ek):
                pass
            if waits.get(vkj, 0) < v:
                waits[vkj] = v
        if vk[0] == "d":
            pv = cum[i][vk] - 16
            if pv > 0 and waits.get(vk, 0) < pv:
                waits[vk] = pv
        op_waits[i] = waits
        for r in reads:
            readers.setdefault(id(r), []).append(i)
        for w in writes:
            last_writer[id(w)] = i
            readers[id(w)] = [i]

    def sem_of(vk):
        return comp_sems[vk[1]] if vk[0] == "c" else dma_sems[vk[1]][vk[2]]

    def section(ek, eng):
        last = {vk: 0 for vk in vkeys}
        for i, (k2, inc, fn, selfm, reads, writes) in enumerate(norm_ops):
            if k2 != ek:
                continue
            vk_self = kinds[i]
            if op_waits[i] is None:
                pre = {vk: cum[i][vk] - (inc if vk == vk_self else 0) for vk in vkeys}
                for vk in vkeys:
                    if pre[vk] > last[vk]:
                        eng.wait_ge(sem_of(vk), pre[vk])
                        last[vk] = pre[vk]
            else:
                for vk, v in op_waits[i].items():
                    if v > last[vk]:
                        eng.wait_ge(sem_of(vk), v)
                        last[vk] = v
            if selfm:
                fn(eng, sem_of(vk_self), inc)
            else:
                inst = fn(eng)
                inst.then_inc(sem_of(vk_self), inc)

    @blk.vector
    def _(v):
        section("v", v)

    @blk.scalar
    def _(s):
        section("s", s)

    @blk.tensor
    def _(t):
        section("t", t)

    @blk.gpsimd
    def _(g):
        section("g", g)

    @blk.sync
    def _(y):
        section("y", y)
        for vk in vkeys:
            if tot[vk] and vk != ("c", "y"):
                y.wait_ge(sem_of(vk), tot[vk])


class LNHelper:

    def __init__(self, nc, es, ops, name, scratch_d, out_d, outbf_d,
                 ones128, ones1, eps_ap, MT=DM // 128, NN=SL // 512):
        e = es.enter_context
        self.nc, self.ops = nc, ops
        self.MT, self.NN = MT, NN
        self.scratch_d, self.out_d, self.outbf_d = scratch_d, out_d, outbf_d
        self.ones128, self.ones1, self.eps_ap = ones128, ones1, eps_ap
        self.sq = e(nc.sbuf_tensor(f"{name}_sq", [128, 512], F32))
        self.statx = e(nc.sbuf_tensor(f"{name}_sx", [1, SL], F32))
        self.statx2 = e(nc.sbuf_tensor(f"{name}_sx2", [1, SL], F32))
        self.arow = e(nc.sbuf_tensor(f"{name}_ar", [1, SL], F32))
        self.brow = e(nc.sbuf_tensor(f"{name}_br", [1, SL], F32))
        self.nt = [e(nc.sbuf_tensor(f"{name}_nt{i}", [128, 512], F32)) for i in range(2)]
        self.obf = e(nc.sbuf_tensor(f"{name}_obf", [128, 512], BF16))
        self.ps_st = e(nc.psum_tensor(f"{name}_pst", [1, 512], F32))
        self.ps_a = e(nc.psum_tensor(f"{name}_psa", [128, 512], F32))
        self.ps_b = e(nc.psum_tensor(f"{name}_psb", [128, 512], F32))
        A = ops.append
        A(("g", 1, lambda g: g.memset(self.statx[:], 0.0)))
        A(("g", 1, lambda g: g.memset(self.statx2[:], 0.0)))

    def accum(self, m, nn, raw_tile):
        A = self.ops.append
        sl = slice(nn * 512, (nn + 1) * 512)
        A(("t", 1, lambda t: t.matmul(self.ps_st[:], self.ones128[:], raw_tile[:],
                                      start=True, stop=True)))
        A(("v", 1, lambda v: v.tensor_add(self.statx[:, sl], self.statx[:, sl], self.ps_st[:])))
        A(("s", 1, lambda s: s.activation(out=self.sq[:], in_=raw_tile[:], func=AF.Square)))
        A(("t", 1, lambda t: t.matmul(self.ps_st[:], self.ones128[:], self.sq[:],
                                      start=True, stop=True)))
        A(("v", 1, lambda v: v.tensor_add(self.statx2[:, sl], self.statx2[:, sl], self.ps_st[:])))
        A(("g", 16, lambda g: g.dma_start(
            out=self.scratch_d[m * 128:(m + 1) * 128, sl], in_=raw_tile[:])))

    def finalize(self):
        A = self.ops.append
        n = 128 * self.MT
        A(("v", 1, lambda v: v.tensor_scalar_mul(self.statx[:], self.statx[:], 1.0 / n)))
        A(("v", 1, lambda v: v.tensor_scalar_mul(self.statx2[:], self.statx2[:], 1.0 / n)))
        A(("v", 1, lambda v: v.tensor_mul(self.arow[:], self.statx[:], self.statx[:])))
        A(("v", 1, lambda v: v.tensor_sub(self.statx2[:], self.statx2[:], self.arow[:])))
        A(("s", 1, lambda s: s.activation(out=self.statx2[:], in_=self.statx2[:],
                                          func=AF.Ln, bias=self.eps_ap)))
        A(("s", 1, lambda s: s.activation(out=self.arow[:], in_=self.statx2[:],
                                          func=AF.Exp, scale=-0.5)))
        A(("v", 1, lambda v: v.tensor_mul(self.brow[:], self.statx[:], self.arow[:])))
        A(("v", 1, lambda v: v.tensor_scalar_mul(self.brow[:], self.brow[:], -1.0)))

    def normalize(self):
        A = self.ops.append
        for nn in range(self.NN):
            sl = slice(nn * 512, (nn + 1) * 512)
            A(("t", 1, lambda t, sl=sl: t.matmul(self.ps_a[:], self.ones1[:],
                                                 self.arow[:, sl], start=True, stop=True)))
            A(("t", 1, lambda t, sl=sl: t.matmul(self.ps_b[:], self.ones1[:],
                                                 self.brow[:, sl], start=True, stop=True)))
            for m in range(self.MT):
                msl = slice(m * 128, (m + 1) * 128)
                A(("y", 16, lambda y, msl=msl, sl=sl: y.dma_start(
                    out=self.nt[0][:], in_=self.scratch_d[msl, sl])))
                A(("v", 1, lambda v: v.tensor_mul(self.nt[1][:], self.nt[0][:], self.ps_a[:])))
                A(("v", 1, lambda v: v.tensor_add(self.nt[1][:], self.nt[1][:], self.ps_b[:])))
                A(("g", 16, lambda g, msl=msl, sl=sl: g.dma_start(
                    out=self.out_d[msl, sl], in_=self.nt[1][:])))
                A(("v", 1, lambda v: v.tensor_copy(out=self.obf[:], in_=self.nt[1][:])))
                A(("g", 16, lambda g, msl=msl, sl=sl: g.dma_start(
                    out=self.outbf_d[msl, sl], in_=self.obf[:])))


def build_outproj():
    nc = _bass()
    KT = DI // 128
    MT = DM // 128
    NN = SL // 512

    y_dd = nc.dram_tensor("yslab", [DI, SL], BF16, kind="ExternalInput").ap()
    opw_d = nc.dram_tensor("opw", [DI, DM], F32, kind="ExternalInput").ap()
    x_dd = nc.dram_tensor("xslab", [DM, SL], F32, kind="ExternalInput").ap()
    ones128_d = nc.dram_tensor("ones128", [128, 1], F32, kind="ExternalInput").ap()
    ones1_d = nc.dram_tensor("ones1", [1, 128], F32, kind="ExternalInput").ap()
    epsv_d = nc.dram_tensor("epsv", [128, 1], F32, kind="ExternalInput").ap()
    x1_d = nc.dram_tensor("x1", [DM, SL], F32, kind="ExternalOutput").ap()
    x1bf_d = nc.dram_tensor("x1bf", [DM, SL], BF16, kind="ExternalOutput").ap()
    scr_d = nc.dram_tensor("scr", [DM, SL], F32, kind="Internal").ap()

    with ExitStack() as es:
        e = es.enter_context
        ones128 = e(nc.sbuf_tensor("ones128_s", [128, 1], F32))
        ones1 = e(nc.sbuf_tensor("ones1_s", [1, 128], F32))
        epsv = e(nc.sbuf_tensor("epsv_s", [128, 1], F32))
        rhs = [e(nc.sbuf_tensor(f"rhs{k}", [128, SL], BF16)) for k in range(KT)]
        wst = e(nc.sbuf_tensor("wst", [128, 128], F32))
        wbf = [e(nc.sbuf_tensor(f"wbf{i}", [128, 128], BF16)) for i in range(KT)]
        xt = e(nc.sbuf_tensor("xt", [128, 512], F32))
        raw = e(nc.sbuf_tensor("raw", [128, 512], F32))
        ps_mm = e(nc.psum_tensor("ps_mm", [128, 512], F32))

        ops = []
        A = ops.append
        A(("y", 16, lambda y: y.dma_start(out=ones128[:], in_=ones128_d[:])))
        A(("y", 16, lambda y: y.dma_start(out=ones1[:], in_=ones1_d[:])))
        A(("y", 16, lambda y: y.dma_start(out=epsv[:], in_=epsv_d[:])))
        for k in range(KT):
            A(("y", 16, lambda y, k=k: y.dma_start(
                out=rhs[k][:], in_=y_dd[k * 128:(k + 1) * 128, :])))
        ln = LNHelper(nc, es, ops, "ln1", scr_d, x1_d, x1bf_d, ones128, ones1,
                      epsv[0:1, 0:1])
        for m in range(MT):
            for k in range(KT):
                A(("y", 16, lambda y, k=k, m=m: y.dma_start(
                    out=wst[:], in_=opw_d[k * 128:(k + 1) * 128, m * 128:(m + 1) * 128])))
                A(("v", 1, lambda v, k=k: v.tensor_copy(out=wbf[k][:], in_=wst[:])))
            for nn in range(NN):
                sl = slice(nn * 512, (nn + 1) * 512)
                for k in range(KT):
                    A(("t", 1, lambda t, k=k, sl=sl: t.matmul(
                        ps_mm[:], wbf[k][:], rhs[k][:, sl],
                        start=(k == 0), stop=(k == KT - 1))))
                A(("y", 16, lambda y, m=m, sl=sl: y.dma_start(
                    out=xt[:], in_=x_dd[m * 128:(m + 1) * 128, sl])))
                A(("v", 1, lambda v: v.tensor_add(raw[:], xt[:], ps_mm[:])))
                ln.accum(m, nn, raw)
        ln.finalize()
        ln.normalize()
        run_serial(nc, es, ops)
    return nc


def build_ffn0():
    nc = _bass()
    KT = DM // 128
    HT = HID // 128
    NN = SL // 512

    x1_dd = nc.dram_tensor("x1", [DM, SL], F32, kind="ExternalInput").ap()
    x1bf_dd = nc.dram_tensor("x1bf", [DM, SL], BF16, kind="ExternalInput").ap()
    wg_d = nc.dram_tensor("wgT", [DM, HID], I32, kind="ExternalInput").ap()
    wu_d = nc.dram_tensor("wuT", [DM, HID], I32, kind="ExternalInput").ap()
    wd_d = nc.dram_tensor("wdT", [HID, DM], I32, kind="ExternalInput").ap()
    sg_d = nc.dram_tensor("sgcol", [128, HT], F32, kind="ExternalInput").ap()
    su_d = nc.dram_tensor("sucol", [128, HT], F32, kind="ExternalInput").ap()
    sd_d = nc.dram_tensor("sdcol", [128, KT], F32, kind="ExternalInput").ap()
    ones128_d = nc.dram_tensor("ones128", [128, 1], F32, kind="ExternalInput").ap()
    ones1_d = nc.dram_tensor("ones1", [1, 128], F32, kind="ExternalInput").ap()
    epsv_d = nc.dram_tensor("epsv", [128, 1], F32, kind="ExternalInput").ap()
    x2_d = nc.dram_tensor("x2", [DM, SL], F32, kind="ExternalOutput").ap()
    x2bf_d = nc.dram_tensor("x2bf", [DM, SL], BF16, kind="ExternalOutput").ap()
    scr_d = nc.dram_tensor("scr", [DM, SL], F32, kind="Internal").ap()

    SLH = SL // 2
    with ExitStack() as es:
        e = es.enter_context
        ones128 = e(nc.sbuf_tensor("ones128_s", [128, 1], F32))
        ones1 = e(nc.sbuf_tensor("ones1_s", [1, 128], F32))
        epsv = e(nc.sbuf_tensor("epsv_s", [128, 1], F32))
        sgc = e(nc.sbuf_tensor("sgc", [128, HT], F32))
        suc = e(nc.sbuf_tensor("suc", [128, HT], F32))
        sdc = e(nc.sbuf_tensor("sdc", [128, KT], F32))
        xk = [e(nc.sbuf_tensor(f"xk{k}", [128, SLH], BF16)) for k in range(KT)]
        hk = [e(nc.sbuf_tensor(f"hk{k}", [128, SLH], BF16)) for k in range(HT)]
        wstage = e(nc.sbuf_tensor("wstage", [128, 128], I32))
        wbf = [e(nc.sbuf_tensor(f"wbf{i}", [128, 128], BF16)) for i in range(HT)]
        upm = e(nc.sbuf_tensor("upm", [128, SLH], F32))
        gt = e(nc.sbuf_tensor("gt", [128, 512], F32))
        xt = e(nc.sbuf_tensor("xt", [128, 512], F32))
        dnt = e(nc.sbuf_tensor("dnt", [128, 512], F32))
        raw = e(nc.sbuf_tensor("raw", [128, 512], F32))
        ps_mm = e(nc.psum_tensor("ps_mm", [128, 512], F32))

        ops = []
        A = ops.append
        A(("y", 16, lambda y: y.dma_start(out=ones128[:], in_=ones128_d[:])))
        A(("y", 16, lambda y: y.dma_start(out=ones1[:], in_=ones1_d[:])))
        A(("y", 16, lambda y: y.dma_start(out=epsv[:], in_=epsv_d[:])))
        A(("y", 16, lambda y: y.dma_start(out=sgc[:], in_=sg_d[:])))
        A(("y", 16, lambda y: y.dma_start(out=suc[:], in_=su_d[:])))
        A(("y", 16, lambda y: y.dma_start(out=sdc[:], in_=sd_d[:])))

        def load_w(dram, k, m):
            A(("y", 16, lambda y, k=k, m=m, dram=dram: y.dma_start(
                out=wstage[:], in_=dram[k * 128:(k + 1) * 128, m * 128:(m + 1) * 128])))
            A(("s", 1, lambda s, k=k: s.activation(
                out=wbf[k][:], in_=wstage[:], func=AF.Copy, bias=-128.0)))

        ln = LNHelper(nc, es, ops, "ln2", scr_d, x2_d, x2bf_d, ones128, ones1,
                      epsv[0:1, 0:1])
        for half in range(2):
            h0 = half * SLH
            for k in range(KT):
                A(("y", 16, lambda y, k=k, h0=h0: y.dma_start(
                    out=xk[k][:], in_=x1bf_dd[k * 128:(k + 1) * 128, h0:h0 + SLH])))
            for m in range(HT):
                for k in range(KT):
                    load_w(wu_d, k, m)
                for nn in range(SLH // 512):
                    sl = slice(nn * 512, (nn + 1) * 512)
                    for k in range(KT):
                        A(("t", 1, lambda t, k=k, sl=sl: t.matmul(
                            ps_mm[:], wbf[k][:], xk[k][:, sl],
                            start=(k == 0), stop=(k == KT - 1))))
                    A(("s", 1, lambda s, m=m, sl=sl: s.activation(
                        out=upm[:, sl], in_=ps_mm[:], func=AF.Copy, scale=suc[:, m:m + 1])))
                for k in range(KT):
                    load_w(wg_d, k, m)
                for nn in range(SLH // 512):
                    sl = slice(nn * 512, (nn + 1) * 512)
                    for k in range(KT):
                        A(("t", 1, lambda t, k=k, sl=sl: t.matmul(
                            ps_mm[:], wbf[k][:], xk[k][:, sl],
                            start=(k == 0), stop=(k == KT - 1))))
                    A(("s", 1, lambda s, m=m, sl=sl: s.activation(
                        out=gt[:], in_=ps_mm[:], func=AF.Silu, scale=sgc[:, m:m + 1])))
                    A(("v", 1, lambda v, m=m, sl=sl: v.tensor_mul(
                        hk[m][:, sl], gt[:], upm[:, sl])))
            for m in range(KT):
                for k in range(HT):
                    load_w(wd_d, k, m)
                for nn in range(SLH // 512):
                    sl = slice(nn * 512, (nn + 1) * 512)
                    gnn = half * (SLH // 512) + nn
                    gsl = slice(h0 + nn * 512, h0 + (nn + 1) * 512)
                    for k in range(HT):
                        A(("t", 1, lambda t, k=k, sl=sl: t.matmul(
                            ps_mm[:], wbf[k][:], hk[k][:, sl],
                            start=(k == 0), stop=(k == HT - 1))))
                    A(("s", 1, lambda s, m=m: s.activation(
                        out=dnt[:], in_=ps_mm[:], func=AF.Copy, scale=sdc[:, m:m + 1])))
                    A(("y", 16, lambda y, m=m, gsl=gsl: y.dma_start(
                        out=xt[:], in_=x1_dd[m * 128:(m + 1) * 128, gsl])))
                    A(("v", 1, lambda v: v.tensor_add(raw[:], xt[:], dnt[:])))
                    ln.accum(m, gnn, raw)
        ln.finalize()
        ln.normalize()
        run_serial(nc, es, ops)
    return nc


def build_qkv():
    nc = _bass()
    KT = DM // 128
    MT = 3 * DM // 128
    NN = SL // 512

    x2bf_dd = nc.dram_tensor("x2bf", [DM, SL], BF16, kind="ExternalInput").ap()
    wqkv_d = nc.dram_tensor("wqkv", [DM, 3 * DM], F32, kind="ExternalInput").ap()
    wg_d = nc.dram_tensor("wg", [DM, NH], F32, kind="ExternalInput").ap()
    qkv_d = nc.dram_tensor("qkv", [3 * DM, SL], BF16, kind="ExternalOutput").ap()
    g_d = nc.dram_tensor("g", [NH, SL], F32, kind="ExternalOutput").ap()

    with ExitStack() as es:
        e = es.enter_context
        xk = [e(nc.sbuf_tensor(f"xk{k}", [128, SL], BF16)) for k in range(KT)]
        wst = e(nc.sbuf_tensor("wst", [128, 128], F32))
        wbf = [e(nc.sbuf_tensor(f"wbf{i}", [128, 128], BF16)) for i in range(KT)]
        wgst = e(nc.sbuf_tensor("wgst", [128, NH], F32))
        wgbf = [e(nc.sbuf_tensor(f"wgbf{i}", [128, NH], BF16)) for i in range(KT)]
        ob = e(nc.sbuf_tensor("ob", [128, 512], BF16))
        gb = e(nc.sbuf_tensor("gb", [NH, 512], F32))
        ps_mm = e(nc.psum_tensor("ps_mm", [128, 512], F32))
        ps_g = e(nc.psum_tensor("ps_g", [NH, 512], F32))

        ops = []
        A = ops.append
        for k in range(KT):
            A(("y", 16, lambda y, k=k: y.dma_start(
                out=xk[k][:], in_=x2bf_dd[k * 128:(k + 1) * 128, :])))
            A(("y", 16, lambda y, k=k: y.dma_start(
                out=wgst[:], in_=wg_d[k * 128:(k + 1) * 128, :])))
            A(("v", 1, lambda v, k=k: v.tensor_copy(out=wgbf[k][:], in_=wgst[:])))
        for nn in range(NN):
            sl = slice(nn * 512, (nn + 1) * 512)
            for k in range(KT):
                A(("t", 1, lambda t, k=k, sl=sl: t.matmul(
                    ps_g[:], wgbf[k][:], xk[k][:, sl],
                    start=(k == 0), stop=(k == KT - 1))))
            A(("s", 1, lambda s: s.activation(out=gb[:], in_=ps_g[:], func=AF.Sigmoid)))
            A(("g", 16, lambda g, sl=sl: g.dma_start(out=g_d[:, sl], in_=gb[:])))
        for m in range(MT):
            for k in range(KT):
                A(("y", 16, lambda y, k=k, m=m: y.dma_start(
                    out=wst[:], in_=wqkv_d[k * 128:(k + 1) * 128, m * 128:(m + 1) * 128])))
                A(("v", 1, lambda v, k=k: v.tensor_copy(out=wbf[k][:], in_=wst[:])))
            for nn in range(NN):
                sl = slice(nn * 512, (nn + 1) * 512)
                for k in range(KT):
                    A(("t", 1, lambda t, k=k, sl=sl: t.matmul(
                        ps_mm[:], wbf[k][:], xk[k][:, sl],
                        start=(k == 0), stop=(k == KT - 1))))
                A(("v", 1, lambda v: v.tensor_copy(out=ob[:], in_=ps_mm[:])))
                A(("g", 16, lambda g, m=m, sl=sl: g.dma_start(
                    out=qkv_d[m * 128:(m + 1) * 128, sl], in_=ob[:])))
        run_serial(nc, es, ops)
    return nc


def build_attn():
    nc = _bass()
    QT = 1024
    NQT = LB // QT
    NKT = LB // 128
    SC = DH ** -0.5

    q_dd = nc.dram_tensor("Q", [512, LB], BF16, kind="ExternalInput").ap()
    k_dd = nc.dram_tensor("Kf", [512, LB], BF16, kind="ExternalInput").ap()
    v_dd = nc.dram_tensor("Vt", [LB, 512], BF16, kind="ExternalInput").ap()
    g_dd = nc.dram_tensor("grows", [4, LB], F32, kind="ExternalInput").ap()
    mask_d = nc.dram_tensor("mask", [128, 2048], F32, kind="ExternalInput").ap()
    ones128b_d = nc.dram_tensor("ones128b", [128, 1], BF16, kind="ExternalInput").ap()
    ones1_d = nc.dram_tensor("ones1", [1, 128], F32, kind="ExternalInput").ap()
    attc_d = nc.dram_tensor("attc", [128, 3], F32, kind="ExternalInput").ap()
    o_d = nc.dram_tensor("O", [512, LB], BF16, kind="ExternalOutput").ap()

    with ExitStack() as es:
        e = es.enter_context
        mask = e(nc.sbuf_tensor("mask_s", [128, 2048], F32))
        ones128b = e(nc.sbuf_tensor("ones128b_s", [128, 1], BF16))
        ones1 = e(nc.sbuf_tensor("ones1_s", [1, 128], F32))
        attc = e(nc.sbuf_tensor("attc_s", [128, 3], F32))
        qh = e(nc.sbuf_tensor("qh", [128, LB], BF16))
        kh = e(nc.sbuf_tensor("kh", [128, LB], BF16))
        vh = e(nc.sbuf_tensor("vh", [128, NKT * 128], BF16))
        gl = e(nc.sbuf_tensor("gl", [1, LB], F32))
        z = e(nc.sbuf_tensor("z", [128, QT], F32))
        pbf = e(nc.sbuf_tensor("pbf", [128, QT], BF16))
        linv = e(nc.sbuf_tensor("linv", [1, QT], F32))
        coef = e(nc.sbuf_tensor("coef", [1, QT], F32))
        cbc = e(nc.sbuf_tensor("cbc", [128, QT], F32))
        obf = e(nc.sbuf_tensor("obf", [128, QT], BF16))
        ps_s = e(nc.psum_tensor("ps_s", [128, QT], F32))
        ps_o = e(nc.psum_tensor("ps_o", [128, QT], F32))
        ps_l = e(nc.psum_tensor("ps_l", [1, QT], F32))
        ps_c = e(nc.psum_tensor("ps_c", [128, QT], F32))

        ops = []
        A = ops.append
        A(("y", 16, lambda y: y.dma_start(out=mask[:], in_=mask_d[:])))
        A(("y", 16, lambda y: y.dma_start(out=ones128b[:], in_=ones128b_d[:])))
        A(("y", 16, lambda y: y.dma_start(out=ones1[:], in_=ones1_d[:])))
        A(("y", 16, lambda y: y.dma_start(out=attc[:], in_=attc_d[:])))
        for h in range(4):
            r0 = h * 128
            A(("y", 16, lambda y, r0=r0: y.dma_start(out=qh[:], in_=q_dd[r0:r0 + 128, :])))
            A(("y", 16, lambda y, r0=r0: y.dma_start(out=kh[:], in_=k_dd[r0:r0 + 128, :])))
            A(("y", 16, lambda y, h=h: y.dma_start(out=gl[:], in_=g_dd[h:h + 1, :])))
            for kt in range(NKT):
                A(("y", 16, lambda y, kt=kt, r0=r0: y.dma_start(
                    out=vh[:, kt * 128:(kt + 1) * 128],
                    in_=v_dd[kt * 128:(kt + 1) * 128, r0:r0 + 128])))
            for qt in range(NQT):
                q0 = qt * QT
                nkt = (qt + 1) * (QT // 128)
                for kt in range(nkt):
                    ksl = slice(kt * 128, (kt + 1) * 128)
                    for hf in range(QT // 512):
                        A(("t", 1, lambda t, ksl=ksl, q0=q0, hf=hf: t.matmul(
                            ps_s[:, hf * 512:(hf + 1) * 512], kh[:, ksl],
                            qh[:, q0 + hf * 512:q0 + (hf + 1) * 512],
                            start=True, stop=True)))
                    A(("s", 1, lambda s: s.activation(
                        out=z[:], in_=ps_s[:], func=AF.Tanh, scale=attc[:, 0:1])))
                    r = kt - (nkt - QT // 128)
                    if r >= 0:
                        msl = slice(QT - 128 * r, 2 * QT - 128 * r)
                        A(("v", 1, lambda v, msl=msl: v.tensor_add(
                            z[:], z[:], mask[:, msl])))
                    A(("s", 1, lambda s: s.activation(
                        out=pbf[:], in_=z[:], func=AF.Exp, scale=attc[:, 1:2], bias=attc[:, 2:3])))
                    for hf in range(QT // 512):
                        A(("t", 1, lambda t, hf=hf, kt=kt, nkt=nkt: t.matmul(
                            ps_l[:, hf * 512:(hf + 1) * 512], ones128b[:],
                            pbf[:, hf * 512:(hf + 1) * 512],
                            start=(kt == 0), stop=(kt == nkt - 1))))
                    for hf in range(QT // 512):
                        A(("t", 1, lambda t, hf=hf, kt=kt, ksl=ksl, nkt=nkt: t.matmul(
                            ps_o[:, hf * 512:(hf + 1) * 512], vh[:, ksl],
                            pbf[:, hf * 512:(hf + 1) * 512],
                            start=(kt == 0), stop=(kt == nkt - 1))))
                A(("v", 1, lambda v: v.reciprocal(linv[:], ps_l[:])))
                A(("v", 1, lambda v, q0=q0: v.tensor_mul(
                    coef[:], linv[:], gl[:, q0:q0 + QT])))
                for hf in range(QT // 512):
                    A(("t", 1, lambda t, hf=hf: t.matmul(
                        ps_c[:, hf * 512:(hf + 1) * 512], ones1[:],
                        coef[:, hf * 512:(hf + 1) * 512], start=True, stop=True)))
                A(("v", 1, lambda v: v.tensor_copy(out=cbc[:], in_=ps_c[:])))
                A(("v", 1, lambda v: v.tensor_mul(obf[:], ps_o[:], cbc[:])))
                A(("g", 16, lambda g, r0=r0, q0=q0: g.dma_start(
                    out=o_d[r0:r0 + 128, q0:q0 + QT], in_=obf[:])))
        run_serial(nc, es, ops)
    return nc


def build_attnout():
    nc = _bass()
    KT = DM // 128
    MT = DM // 128
    NN = SL // 512

    o_dd = nc.dram_tensor("Oslab", [DM, SL], BF16, kind="ExternalInput").ap()
    wout_d = nc.dram_tensor("wout", [DM, DM], F32, kind="ExternalInput").ap()
    x2_dd = nc.dram_tensor("x2", [DM, SL], F32, kind="ExternalInput").ap()
    rw_d = nc.dram_tensor("rw", [DM, NEXP], F32, kind="ExternalInput").ap()
    rb_d = nc.dram_tensor("rb", [1, NEXP], F32, kind="ExternalInput").ap()
    ones128_d = nc.dram_tensor("ones128", [128, 1], F32, kind="ExternalInput").ap()
    ones1_d = nc.dram_tensor("ones1", [1, 128], F32, kind="ExternalInput").ap()
    epsv_d = nc.dram_tensor("epsv", [128, 1], F32, kind="ExternalInput").ap()
    x3_d = nc.dram_tensor("x3", [DM, SL], F32, kind="ExternalOutput").ap()
    x3bf_d = nc.dram_tensor("x3bf", [DM, SL], BF16, kind="ExternalOutput").ap()
    s_d = nc.dram_tensor("S", [SL, NEXP], F32, kind="ExternalOutput").ap()
    scr_d = nc.dram_tensor("scr", [DM, SL], F32, kind="Internal").ap()

    with ExitStack() as es:
        e = es.enter_context
        ones128 = e(nc.sbuf_tensor("ones128_s", [128, 1], F32))
        ones1 = e(nc.sbuf_tensor("ones1_s", [1, 128], F32))
        epsv = e(nc.sbuf_tensor("epsv_s", [128, 1], F32))
        rhs = [e(nc.sbuf_tensor(f"rhs{k}", [128, SL], BF16)) for k in range(KT)]
        wst = e(nc.sbuf_tensor("wst", [128, 128], F32))
        wbf = [e(nc.sbuf_tensor(f"wbf{i}", [128, 128], BF16)) for i in range(KT)]
        rwst = e(nc.sbuf_tensor("rwst", [128, NEXP], F32))
        rwbf = [e(nc.sbuf_tensor(f"rwbf{i}", [128, NEXP], BF16)) for i in range(KT)]
        rbs = e(nc.sbuf_tensor("rbs", [1, NEXP], F32))
        xt = e(nc.sbuf_tensor("xt", [128, 512], F32))
        raw = e(nc.sbuf_tensor("raw", [128, 512], F32))
        x3p = e(nc.sbuf_tensor("x3p", [128, 128], BF16))
        rt = e(nc.sbuf_tensor("rt", [128, NEXP], F32))
        mx = e(nc.sbuf_tensor("mx", [128, 1], F32))
        sm = e(nc.sbuf_tensor("sm", [128, 1], F32))
        smr = e(nc.sbuf_tensor("smr", [128, 1], F32))
        st_ = e(nc.sbuf_tensor("st_", [128, NEXP], F32))
        rbc = e(nc.sbuf_tensor("rbc", [128, NEXP], F32))
        ps_mm = e(nc.psum_tensor("ps_mm", [128, 512], F32))
        ps_r = e(nc.psum_tensor("ps_r", [128, NEXP], F32))
        ps_rb = e(nc.psum_tensor("ps_rb", [128, NEXP], F32))

        ops = []
        A = ops.append
        A(("y", 16, lambda y: y.dma_start(out=ones128[:], in_=ones128_d[:])))
        A(("y", 16, lambda y: y.dma_start(out=ones1[:], in_=ones1_d[:])))
        A(("y", 16, lambda y: y.dma_start(out=epsv[:], in_=epsv_d[:])))
        A(("y", 16, lambda y: y.dma_start(out=rbs[:], in_=rb_d[:])))
        for k in range(KT):
            A(("y", 16, lambda y, k=k: y.dma_start(
                out=rhs[k][:], in_=o_dd[k * 128:(k + 1) * 128, :])))
            A(("y", 16, lambda y, k=k: y.dma_start(
                out=rwst[:], in_=rw_d[k * 128:(k + 1) * 128, :])))
            A(("v", 1, lambda v, k=k: v.tensor_copy(out=rwbf[k][:], in_=rwst[:])))
        ln = LNHelper(nc, es, ops, "ln3", scr_d, x3_d, x3bf_d, ones128, ones1,
                      epsv[0:1, 0:1])
        for m in range(MT):
            for k in range(KT):
                A(("y", 16, lambda y, k=k, m=m: y.dma_start(
                    out=wst[:], in_=wout_d[k * 128:(k + 1) * 128, m * 128:(m + 1) * 128])))
                A(("v", 1, lambda v, k=k: v.tensor_copy(out=wbf[k][:], in_=wst[:])))
            for nn in range(NN):
                sl = slice(nn * 512, (nn + 1) * 512)
                for k in range(KT):
                    A(("t", 1, lambda t, k=k, sl=sl: t.matmul(
                        ps_mm[:], wbf[k][:], rhs[k][:, sl],
                        start=(k == 0), stop=(k == KT - 1))))
                A(("y", 16, lambda y, m=m, sl=sl: y.dma_start(
                    out=xt[:], in_=x2_dd[m * 128:(m + 1) * 128, sl])))
                A(("v", 1, lambda v: v.tensor_add(raw[:], xt[:], ps_mm[:])))
                ln.accum(m, nn, raw)
        ln.finalize()
        ln.normalize()
        A(("t", 1, lambda t: t.matmul(ps_rb[:], ones1[:], rbs[:], start=True, stop=True)))
        A(("v", 1, lambda v: v.tensor_copy(out=rbc[:], in_=ps_rb[:])))
        for tt in range(SL // 128):
            for k in range(KT):
                A(("y", 16, lambda y, tt=tt, k=k: y.dma_start(
                    out=x3p[:], in_=x3bf_d[k * 128:(k + 1) * 128, tt * 128:(tt + 1) * 128])))
                A(("t", 1, lambda t, k=k: t.matmul(
                    ps_r[:], x3p[:], rwbf[k][:],
                    start=(k == 0), stop=(k == KT - 1))))
            A(("v", 1, lambda v: v.tensor_add(rt[:], ps_r[:], rbc[:])))
            A(("v", 1, lambda v: v.reduce_max(mx[:], rt[:].ap() if False else rt[:], axis=AX.X)))
            A(("v", 1, lambda v: v.tensor_sub(rt[:], rt[:], mx.ap().broadcast_to([128, NEXP]))))
            A(("s", 1, lambda s: s.activation(out=rt[:], in_=rt[:], func=AF.Exp)))
            A(("v", 1, lambda v: v.reduce_sum(sm[:], rt[:], axis=AX.X)))
            A(("s", 1, lambda s: s.activation(out=smr[:], in_=sm[:], func=AF.Ln)))
            A(("s", 1, lambda s: s.activation(out=smr[:], in_=smr[:], func=AF.Exp, scale=-1.0)))
            A(("v", 1, lambda v: v.tensor_mul(st_[:], rt[:], smr.ap().broadcast_to([128, NEXP]))))
            A(("g", 16, lambda g, tt=tt: g.dma_start(
                out=s_d[tt * 128:(tt + 1) * 128, :], in_=st_[:])))
        run_serial(nc, es, ops)
    return nc


def build_moe():
    nc = _bass()
    KT = DM // 128
    MT = DM // 128
    NN = SL // 512
    NBIS = 35

    sall_d = nc.dram_tensor("Sall", [128, TT * NEXP // 128], F32, kind="ExternalInput").ap()
    sslab_d = nc.dram_tensor("Sslab", [128, SL * NEXP // 128], F32, kind="ExternalInput").ap()
    x3_dd = nc.dram_tensor("x3", [DM, SL], F32, kind="ExternalInput").ap()
    x3bf_dd = nc.dram_tensor("x3bf", [DM, SL], BF16, kind="ExternalInput").ap()
    ew_d = nc.dram_tensor("ew", [DM, DM], F32, kind="ExternalInput").ap()
    eb_d = nc.dram_tensor("ebcol", [128, KT], F32, kind="ExternalInput").ap()
    kval_d = nc.dram_tensor("kval", [1, NEXP], F32, kind="ExternalInput").ap()
    lohi_d = nc.dram_tensor("lohi", [1, 2 * NEXP], F32, kind="ExternalInput").ap()
    ones128_d = nc.dram_tensor("ones128", [128, 1], F32, kind="ExternalInput").ap()
    ones1_d = nc.dram_tensor("ones1", [1, 128], F32, kind="ExternalInput").ap()
    epsv_d = nc.dram_tensor("epsv", [128, 1], F32, kind="ExternalInput").ap()
    x4_d = nc.dram_tensor("x4", [DM, SL], F32, kind="ExternalOutput").ap()
    x4bf_d = nc.dram_tensor("x4bf", [DM, SL], BF16, kind="ExternalOutput").ap()
    scr_d = nc.dram_tensor("scr", [DM, SL], F32, kind="Internal").ap()
    w_d = nc.dram_tensor("wtok", [1, SL], F32, kind="Internal").ap()

    SW = TT * NEXP // 128
    with ExitStack() as es:
        e = es.enter_context
        ones128 = e(nc.sbuf_tensor("ones128_s", [128, 1], F32))
        ones1 = e(nc.sbuf_tensor("ones1_s", [1, 128], F32))
        epsv = e(nc.sbuf_tensor("epsv_s", [128, 1], F32))
        kval = e(nc.sbuf_tensor("kval_s", [1, NEXP], F32))
        ssb = e(nc.sbuf_tensor("ssb", [128, SW], F32))
        msk = e(nc.sbuf_tensor("msk", [128, SW], F32))
        thr = e(nc.sbuf_tensor("thr", [1, SW], F32))
        cacc = e(nc.sbuf_tensor("cacc", [1, SW], F32))
        lo = e(nc.sbuf_tensor("lo", [1, NEXP], F32))
        hi = e(nc.sbuf_tensor("hi", [1, NEXP], F32))
        mid = e(nc.sbuf_tensor("mid", [1, NEXP], F32))
        cnt = e(nc.sbuf_tensor("cnt", [1, NEXP], F32))
        cge = e(nc.sbuf_tensor("cge", [1, NEXP], F32))
        tmp16 = e(nc.sbuf_tensor("tmp16", [1, NEXP], F32))
        wtm = e(nc.sbuf_tensor("wtm", [128, 16], F32))
        sslab = e(nc.sbuf_tensor("sslab_s", [128, SL * NEXP // 128], F32))
        thr2 = e(nc.sbuf_tensor("thr2", [1, SL * NEXP // 128], F32))
        msk2 = e(nc.sbuf_tensor("msk2", [128, SL * NEXP // 128], F32))
        wrow = e(nc.sbuf_tensor("wrow", [1, SL], F32))
        rhs = [e(nc.sbuf_tensor(f"rhs{k}", [128, SL], BF16)) for k in range(KT)]
        wst = e(nc.sbuf_tensor("wst", [128, 128], F32))
        wbf = [e(nc.sbuf_tensor(f"wbf{i}", [128, 128], BF16)) for i in range(KT)]
        ebc = e(nc.sbuf_tensor("ebc", [128, KT], F32))
        zb = e(nc.sbuf_tensor("zb", [128, 512], F32))
        xt = e(nc.sbuf_tensor("xt", [128, 512], F32))
        raw = e(nc.sbuf_tensor("raw", [128, 512], F32))
        wbcs = e(nc.sbuf_tensor("wbcs", [128, 512], F32))
        ps_mm = e(nc.psum_tensor("ps_mm", [128, 512], F32))
        ps_thr = e(nc.psum_tensor("ps_thr", [128, 512], F32))
        ps_cnt = e(nc.psum_tensor("ps_cnt", [1, 512], F32))
        ps_w = e(nc.psum_tensor("ps_w", [128, 512], F32))
        ps_t2 = e(nc.psum_tensor("ps_t2", [128, 256], F32))

        ops = []
        A = ops.append
        A(("y", 16, lambda y: y.dma_start(out=ones128[:], in_=ones128_d[:])))
        A(("y", 16, lambda y: y.dma_start(out=ones1[:], in_=ones1_d[:])))
        A(("y", 16, lambda y: y.dma_start(out=epsv[:], in_=epsv_d[:])))
        A(("y", 16, lambda y: y.dma_start(out=kval[:], in_=kval_d[:])))
        A(("y", 16, lambda y: y.dma_start(out=ssb[:], in_=sall_d[:])))
        A(("y", 16, lambda y: y.dma_start(out=sslab[:], in_=sslab_d[:])))
        A(("y", 16, lambda y: y.dma_start(out=ebc[:], in_=eb_d[:])))
        for k in range(KT):
            A(("y", 16, lambda y, k=k: y.dma_start(
                out=rhs[k][:], in_=x3bf_dd[k * 128:(k + 1) * 128, :])))
        A(("y", 16, lambda y: y.dma_start(out=lo[:], in_=lohi_d[:, 0:NEXP])))
        A(("y", 16, lambda y: y.dma_start(out=hi[:], in_=lohi_d[:, NEXP:2 * NEXP])))

        def count_pass(thr_src):
            A(("v", 1, lambda v: v.tensor_copy(
                out=thr.ap().rearrange("o (t e) -> o t e", e=NEXP),
                in_=thr_src.ap().unsqueeze(1).broadcast_to([1, 128, NEXP]))))
            for ch in range(SW // 512):
                sl = slice(ch * 512, (ch + 1) * 512)
                A(("t", 1, lambda t, sl=sl: t.matmul(
                    ps_thr[:], ones1[:], thr[:, sl], start=True, stop=True)))
                A(("v", 1, lambda v, sl=sl: v.tensor_tensor(
                    out=msk[:, sl], in0=ssb[:, sl], in1=ps_thr[:], op=ALU.is_ge)))
                A(("t", 1, lambda t, sl=sl: t.matmul(
                    ps_cnt[:], ones128[:], msk[:, sl], start=True, stop=True)))
                A(("v", 1, lambda v, sl=sl: v.tensor_copy(out=cacc[:, sl], in_=ps_cnt[:])))
            A(("v", 1, lambda v: v.reduce_sum(
                cnt[:], cacc.ap().rearrange("o (t e) -> o e t", e=NEXP), axis=AX.X)))

        for it in range(NBIS):
            A(("v", 1, lambda v: v.tensor_add(mid[:], lo[:], hi[:])))
            A(("v", 1, lambda v: v.tensor_scalar_mul(mid[:], mid[:], 0.5)))
            count_pass(mid)
            A(("v", 1, lambda v: v.tensor_tensor(out=cge[:], in0=cnt[:], in1=kval[:], op=ALU.is_ge)))
            A(("v", 1, lambda v: v.tensor_sub(tmp16[:], mid[:], lo[:])))
            A(("v", 1, lambda v: v.tensor_mul(tmp16[:], tmp16[:], cge[:])))
            A(("v", 1, lambda v: v.tensor_add(lo[:], lo[:], tmp16[:])))
            A(("v", 1, lambda v: v.tensor_sub(tmp16[:], hi[:], mid[:])))
            A(("v", 1, lambda v: v.tensor_mul(tmp16[:], tmp16[:], cge[:])))
            A(("v", 1, lambda v: v.tensor_add(hi[:], mid[:], tmp16[:])))
        A(("v", 1, lambda v: v.tensor_copy(
            out=thr2.ap().rearrange("o (t e) -> o t e", e=NEXP),
            in_=lo.ap().unsqueeze(1).broadcast_to([1, 16, NEXP]))))
        A(("t", 1, lambda t: t.matmul(ps_t2[:], ones1[:], thr2[:], start=True, stop=True)))
        A(("v", 1, lambda v: v.tensor_tensor(
            out=msk2[:], in0=sslab[:], in1=ps_t2[:], op=ALU.is_ge)))
        A(("v", 1, lambda v: v.tensor_mul(msk2[:], msk2[:], sslab[:])))
        A(("v", 1, lambda v: v.reduce_sum(
            wtm[:, 0:16], msk2.ap().rearrange("p (t e) -> p t e", e=NEXP), axis=AX.X)))
        A(("g", 16, lambda g: g.dma_start(
            out=w_d.rearrange("o (p t) -> p (o t)", p=128), in_=wtm[:, 0:16])))
        A(("y", 16, lambda y: y.dma_start(out=wrow[:], in_=w_d[0:1, :])))
        ln = LNHelper(nc, es, ops, "ln4", scr_d, x4_d, x4bf_d, ones128, ones1,
                      epsv[0:1, 0:1])
        for m in range(MT):
            for k in range(KT):
                A(("y", 16, lambda y, k=k, m=m: y.dma_start(
                    out=wst[:], in_=ew_d[k * 128:(k + 1) * 128, m * 128:(m + 1) * 128])))
                A(("v", 1, lambda v, k=k: v.tensor_copy(out=wbf[k][:], in_=wst[:])))
            for nn in range(NN):
                sl = slice(nn * 512, (nn + 1) * 512)
                for k in range(KT):
                    A(("t", 1, lambda t, k=k, sl=sl: t.matmul(
                        ps_mm[:], wbf[k][:], rhs[k][:, sl],
                        start=(k == 0), stop=(k == KT - 1))))
                A(("v", 1, lambda v, m=m: v.tensor_scalar_add(
                    zb[:], ps_mm[:], ebc[:, m:m + 1])))
                A(("t", 1, lambda t, sl=sl: t.matmul(
                    ps_w[:], ones1[:], wrow[:, sl], start=True, stop=True)))
                A(("v", 1, lambda v: v.tensor_copy(out=wbcs[:], in_=ps_w[:])))
                A(("v", 1, lambda v: v.tensor_mul(zb[:], zb[:], wbcs[:])))
                A(("y", 16, lambda y, m=m, sl=sl: y.dma_start(
                    out=xt[:], in_=x3_dd[m * 128:(m + 1) * 128, sl])))
                A(("v", 1, lambda v: v.tensor_add(raw[:], xt[:], zb[:])))
                ln.accum(m, nn, raw)
        ln.finalize()
        ln.normalize()
        run_serial(nc, es, ops)
    return nc


class Pool:

    def __init__(self, nc, es, kb_per_partition=200):
        self.words = kb_per_partition * 256
        self.heap = es.enter_context(nc.sbuf_tensor("heap", [128, self.words], F32))
        self.heap_bf = self.heap.ap().bitcast(BF16)
        self.heap_i = self.heap.ap().bitcast(I32)
        self.off = 0
        self.peak = 0

    def tile(self, shape, dtype=F32):
        p, n = shape
        if dtype == F32 or dtype == I32:
            w = n
        elif dtype == BF16:
            w = (n + 1) // 2
        else:
            raise ValueError(dtype)
        o = self.off
        self.off += w
        self.peak = max(self.peak, self.off)
        assert self.off <= self.words, f"SBUF pool overflow: {self.off * 4} B/p"
        if dtype == F32:
            return self.heap.ap()[0:p, o:o + n]
        if dtype == I32:
            return self.heap_i[0:p, o:o + n]
        return self.heap_bf[0:p, 2 * o:2 * o + n]

    def mark(self):
        return self.off

    def reset(self, to=0):
        self.off = to


class Ctx:

    def __init__(self, nc, es, ops, pool, psum):
        self.nc, self.es, self.ops, self.pool, self.psum = nc, es, ops, pool, psum
        self.A = ops.append


def barrier(ctx):
    ctx.A(("g", 1, lambda g: g.engine_nop()))


def emit_mm_group(ctx, psum_ap, pairs, start=True, stop=True, reads=None, writes=None):
    def fn(t):
        inst = None
        n = len(pairs)
        for i, (lf, rf) in enumerate(pairs):
            inst = t.matmul(psum_ap, lf(), rf(),
                            start=(start and i == 0), stop=(stop and i == n - 1))
        return inst
    if reads is None:
        ctx.A(("t", 1, fn))
    else:
        ctx.A(("t", 1, fn, reads, writes))


def _fln_emit(ctx, name, scratch_d, out_d, outbf_d, ones128, ones1, eps_ap, psA, psB, psC):
    pool = ctx.pool
    A = ctx.A
    sq = pool.tile([128, 512])
    statx = pool.tile([1, SL])
    statx2 = pool.tile([1, SL])
    arow = pool.tile([1, SL])
    brow = pool.tile([1, SL])
    nt0 = pool.tile([128, 1024])
    nt1 = pool.tile([128, 1024])
    obf = pool.tile([128, 1024], BF16)
    ps_st = psC[0:1, 0:512]
    MT, NN = DM // 128, SL // 512
    A(("g", 1, lambda g: g.memset(statx, 0.0), [statx], [statx]))
    A(("g", 1, lambda g: g.memset(statx2, 0.0), [statx2], [statx2]))

    def accum(m, nn, raw_tile):
        sl = slice(nn * 512, (nn + 1) * 512)
        A(("t", 1, lambda t: t.matmul(ps_st, ones128, raw_tile, start=True, stop=True),
          [ones128, raw_tile, psC], [psC]))
        A(("v", 1, lambda v: v.tensor_add(statx[:, sl], statx[:, sl], ps_st),
          [psC, statx], [statx]))
        A(("s", 1, lambda s: s.activation(out=sq, in_=raw_tile, func=AF.Square),
          [raw_tile, sq], [sq]))
        A(("t", 1, lambda t: t.matmul(ps_st, ones128, sq, start=True, stop=True),
          [ones128, sq, psC], [psC]))
        A(("v", 1, lambda v: v.tensor_add(statx2[:, sl], statx2[:, sl], ps_st),
          [psC, statx2], [statx2]))
        A(("g", 16, lambda g: g.dma_start(
            out=scratch_d[m * 128:(m + 1) * 128, sl], in_=raw_tile), [raw_tile], []))

    def finalize():
        n = 128 * MT

        def rows1(v):
            v.tensor_scalar_mul(statx, statx, 1.0 / n)
            v.tensor_scalar_mul(statx2, statx2, 1.0 / n)
            v.tensor_mul(arow, statx, statx)
            return v.tensor_sub(statx2, statx2, arow)
        A(("v", 1, rows1, [statx, statx2, arow], [statx, statx2, arow]))

        def rows2(s):
            s.activation(out=statx2, in_=statx2, func=AF.Ln, bias=eps_ap)
            return s.activation(out=arow, in_=statx2, func=AF.Exp, scale=-0.5)
        A(("s", 1, rows2, [statx2, arow], [statx2, arow]))

        def rows3(v):
            v.tensor_mul(brow, statx, arow)
            return v.tensor_scalar_mul(brow, brow, -1.0)
        A(("v", 1, rows3, [statx, arow, brow], [brow]))

    def normalize():
        barrier(ctx)
        for n2 in range(NN // 2):
            sl = slice(n2 * 1024, (n2 + 1) * 1024)

            def mma(t, sl=sl):
                t.matmul(psA[:, 0:512], ones1, arow[:, sl.start:sl.start + 512],
                         start=True, stop=True)
                return t.matmul(psA[:, 512:1024], ones1, arow[:, sl.start + 512:sl.stop],
                                start=True, stop=True)
            A(("t", 1, mma, [ones1, arow, psA], [psA]))

            def mmb(t, sl=sl):
                t.matmul(psB[:, 0:512], ones1, brow[:, sl.start:sl.start + 512],
                         start=True, stop=True)
                return t.matmul(psB[:, 512:1024], ones1, brow[:, sl.start + 512:sl.stop],
                                start=True, stop=True)
            A(("t", 1, mmb, [ones1, brow, psB], [psB]))
            for m in range(MT):
                msl = slice(m * 128, (m + 1) * 128)
                A(("y", 16, lambda y, msl=msl, sl=sl: y.dma_start(
                    out=nt0, in_=scratch_d[msl, sl]), [nt0], [nt0]))

                def norm2(v):
                    v.tensor_mul(nt1, nt0, psA[:, 0:1024])
                    return v.tensor_add(nt1, nt1, psB[:, 0:1024])
                A(("v", 1, norm2, [nt0, psA, psB, nt1], [nt1]))
                A(("g", 16, lambda g, msl=msl, sl=sl: g.dma_start(
                    out=out_d[msl, sl], in_=nt1), [nt1], []))
                if outbf_d is not None:
                    A(("v", 1, lambda v: v.tensor_copy(out=obf, in_=nt1), [nt1, obf], [obf]))
                    A(("g", 16, lambda g, msl=msl, sl=sl: g.dma_start(
                        out=outbf_d[msl, sl], in_=obf), [obf], []))

    return accum, finalize, normalize


def emit_scan(ctx, psA, psB, psC, d, BASE):
    pool, A = ctx.pool, ctx.A
    TC = 2048
    NTC = TT // TC
    NDG = DS // 128
    alogT = pool.tile([128, NDG * NSS])
    acol = pool.tile([128, NDG * NSS])
    dcolT = pool.tile([128, NDG])
    dt4 = [pool.tile([128, TC]) for _ in range(NDG)]
    e4 = [pool.tile([128, TC]) for _ in range(NDG)]
    yacc = [pool.tile([128, TC]) for _ in range(NDG)]
    ut = pool.tile([128, TC])
    sgt = pool.tile([128, TC])
    bbr = pool.tile([1, TC])
    cbr = pool.tile([1, TC])
    brep = pool.tile([128, TC])
    crep = pool.tile([128, TC])
    dA2 = [pool.tile([128, TC]) for _ in range(2)]
    dbu = pool.tile([128, TC])
    St = pool.tile([128, TC])
    z2 = [pool.tile([128, TC]) for _ in range(2)]
    tfin = dbu
    carry = pool.tile([128, NDG * NSS])
    ybf = pool.tile([128, TC], BF16)
    A(("y", 16, lambda y: y.dma_start(out=alogT, in_=d["alog"][:]), [], [alogT]))
    A(("y", 16, lambda y: y.dma_start(out=dcolT, in_=d["dcol"][:]), [], [dcolT]))
    A(("s", 1, lambda s: s.activation(out=acol, in_=alogT, func=AF.Exp),
      [alogT, acol], [acol]))
    A(("v", 1, lambda v: v.tensor_scalar_mul(acol, acol, -1.0), [acol], [acol]))

    for tc in range(NTC):
        t0 = tc * TC
        for dg in range(NDG):
            A(("y", 16, lambda y, dg=dg, t0=t0: y.dma_start(
                out=dt4[dg], in_=d["dt"][dg * 128:(dg + 1) * 128, t0:t0 + TC]),
              [dt4[dg]], [dt4[dg]]))
            A(("y", 16, lambda y, dg=dg, t0=t0: y.dma_start(
                out=ut, in_=d["u"][dg * 128:(dg + 1) * 128, t0:t0 + TC]),
              [ut], [ut]))
            A(("v", 1, lambda v, dg=dg: v.tensor_mul(e4[dg], dt4[dg], ut),
              [dt4[dg], ut, e4[dg]], [e4[dg]]))
        for n in range(NSS):
            A(("y", 16, lambda y, n=n, t0=t0: y.dma_start(
                out=bbr, in_=d["bc"][n:n + 1, t0:t0 + TC]), [bbr], [bbr]))
            A(("y", 16, lambda y, n=n, t0=t0: y.dma_start(
                out=cbr, in_=d["bc"][16 + n:17 + n, t0:t0 + TC]), [cbr], [cbr]))
            ones1s = d["ones1"]
            PSL = [psA, psB, psC, psA]
            for h, (src, dst) in enumerate([(bbr, brep), (cbr, crep)]):
                for c2 in range(2):
                    ps = PSL[h * 2 + c2]
                    o = c2 * 1024

                    def mmrep(t, src=src, ps=ps, o=o):
                        t.matmul(ps[:, 0:512], ones1s, src[:, o:o + 512],
                                 start=True, stop=True)
                        return t.matmul(ps[:, 512:1024], ones1s,
                                        src[:, o + 512:o + 1024],
                                        start=True, stop=True)
                    A(("t", 1, mmrep, [ones1s, src, ps], [ps]))
                    A(("s", 1, lambda s, ps=ps, dst=dst, o=o: s.activation(
                        out=dst[:, o:o + 1024], in_=ps[:, 0:1024], func=AF.Copy),
                      [ps, dst], [dst]))
            for dg in range(NDG):
                p2 = (n * NDG + dg) % 2
                col = dg * NSS + n
                A(("s", 1, lambda s, dg=dg, col=col, p2=p2: s.activation(
                    out=dA2[p2], in_=dt4[dg], func=AF.Exp,
                    scale=acol[:, col:col + 1]), [dt4[dg], acol, dA2[p2]], [dA2[p2]]))
                A(("v", 1, lambda v, dg=dg, p2=p2: v.tensor_mul(dbu, e4[dg], brep),
                  [e4[dg], brep, dbu], [dbu]))

                def scan_op(v, tc=tc, p2=p2, col=col):
                    if tc % 4 == 0:
                        return v.tensor_tensor_scan(
                            out=St, data0=dA2[p2], data1=dbu, initial=0.0,
                            op0=ALU.mult, op1=ALU.add)
                    return v.tensor_tensor_scan(
                        out=St, data0=dA2[p2], data1=dbu,
                        initial=carry[:, col:col + 1], op0=ALU.mult, op1=ALU.add)
                A(("v", 1, scan_op, [dA2[p2], dbu, carry, St], [St]))

                def zc(v, p2=p2, col=col):
                    v.tensor_copy(out=carry[:, col:col + 1], in_=St[:, TC - 1:TC])
                    return v.tensor_mul(z2[p2], St, crep)
                A(("v", 1, zc, [St, crep, carry, z2[p2]], [carry, z2[p2]]))
                if n == 0:
                    A(("g", 1, lambda g, dg=dg, p2=p2: g.tensor_copy(
                        out=yacc[dg], in_=z2[p2]), [z2[p2], yacc[dg]], [yacc[dg]]))
                else:
                    A(("g", 1, lambda g, dg=dg, p2=p2: g.tensor_add(
                        yacc[dg], yacc[dg], z2[p2]), [z2[p2], yacc[dg]], [yacc[dg]]))
        for dg in range(NDG):
            A(("y", 16, lambda y, dg=dg, t0=t0: y.dma_start(
                out=ut, in_=d["u"][dg * 128:(dg + 1) * 128, t0:t0 + TC]),
              [ut], [ut]))
            A(("y", 16, lambda y, dg=dg, t0=t0: y.dma_start(
                out=sgt, in_=d["sg"][dg * 128:(dg + 1) * 128, t0:t0 + TC]),
              [sgt], [sgt]))

            def fin_op(v, dg=dg):
                v.scalar_tensor_tensor(
                    out=tfin, in0=ut, scalar=dcolT[:, dg:dg + 1],
                    in1=yacc[dg], op0=ALU.mult, op1=ALU.add)
                return v.tensor_mul(ybf, tfin, sgt)
            A(("v", 1, fin_op, [ut, dcolT, yacc[dg], sgt, tfin, ybf],
              [tfin, ybf]))
            A(("g", 16, lambda g, dg=dg, t0=t0: g.dma_start(
                out=d["y"][dg * 128:(dg + 1) * 128, t0:t0 + TC], in_=ybf),
              [ybf], []))


def emit_tail(ctx, psA, psB, psC, psT, d, consts, BASE, G8):
    pool, A = ctx.pool, ctx.A
    ones128 = consts["ones128"]
    ones128b = consts["ones128b"]
    ones1 = consts["ones1"]
    eps_ln = consts["eps_ln"]
    P0, P1, P2 = psA[:, 0:512], psA[:, 512:1024], psB[:, 0:512]
    P3, P4, P5 = psB[:, 512:1024], psC[:, 0:512], psC[:, 512:1024]
    PROT6 = [P0, P1, P2, P3, P4, P5]
    PROT4 = [P0, P1, P2, P3]
    KT = DM // 128
    NN = SL // 512

    barrier(ctx)
    wop = pool.tile([128, DM])
    wopbf = [pool.tile([128, DM], BF16) for _ in range(4)]
    yk4 = [pool.tile([128, 4 * 512], BF16) for _ in range(2)]
    yfv = d["yfin"].rearrange("(k p) t -> p k t", p=128)
    ev = [pool.tile([128, 512]) for _ in range(4)]
    for k in range(4):
        A(("y", 16, lambda y, k=k: y.dma_start(
            out=wop, in_=d["opwc"][k * 128:(k + 1) * 128, :]), [wop], [wop]))
        A(("v", 1, lambda v, k=k: v.tensor_copy(out=wopbf[k], in_=wop),
          [wop, wopbf[k]], [wopbf[k]]))
    for r in range(NC):
        for nn in range(NN):
            jp = (r * NN + nn) % 2
            t0r = r * SL + nn * 512
            A(("y", 16, lambda y, t0r=t0r, jp=jp: y.dma_start(
                out=yk4[jp].rearrange("p (k c) -> p k c", k=4),
                in_=yfv[:, :, t0r:t0r + 512]), [yk4[jp]], [yk4[jp]]))
            for m in range(KT):
                mp = m % 4
                PS = PROT6[((r * NN + nn) * KT + m) % 6]
                emit_mm_group(ctx, PS, [
                    (lambda k=k, m=m: wopbf[k][:, m * 128:(m + 1) * 128],
                     lambda k=k, jp=jp: yk4[jp][:, k * 512:(k + 1) * 512])
                    for k in range(4)],
                    reads=wopbf + [yk4[jp], PS], writes=[PS])
                A(("v", 1, lambda v, mp=mp, PS=PS: v.tensor_copy(out=ev[mp], in_=PS),
                  [PS, ev[mp]], [ev[mp]]))
                A(("g", 16, lambda g, r=r, m=m, nn=nn, mp=mp: g.dma_start(
                    out=d["cb6"][r * DM + m * 128:r * DM + (m + 1) * 128,
                                 nn * 512:(nn + 1) * 512], in_=ev[mp]), [ev[mp]], []))
    A(("g", 1, lambda g: g.collective_compute(
        "ReduceScatter", ALU.add, G8, [d["cb6"]], [d["rs6"]])))
    pool.reset(BASE)
    accum, fin, norm = _fln_emit(ctx, "f1", d["lnscr"], d["x1"], d["x1bf"],
                                 ones128, ones1, eps_ln, psA, psB, psC)
    xt = [pool.tile([128, 512]) for _ in range(2)]
    raw = [pool.tile([128, 512]) for _ in range(2)]
    for m in range(KT):
        for nn in range(NN):
            jp = (m * NN + nn) % 2
            sl = slice(nn * 512, (nn + 1) * 512)
            msl = slice(m * 128, (m + 1) * 128)
            A(("y", 16, lambda y, msl=msl, sl=sl, jp=jp: y.dma_start(
                out=xt[jp], in_=d["rs6"][msl, sl]), [xt[jp]], [xt[jp]]))
            A(("y", 16, lambda y, msl=msl, sl=sl, jp=jp: y.dma_start(
                out=raw[jp], in_=d["xslab"][msl, sl]), [raw[jp]], [raw[jp]]))
            A(("v", 1, lambda v, jp=jp: v.tensor_add(raw[jp], raw[jp], xt[jp]),
              [raw[jp], xt[jp]], [raw[jp]]))
            accum(m, nn, raw[jp])
    fin()
    norm()
    pool.reset(BASE)

    barrier(ctx)
    HT = HID // 128
    sgc = pool.tile([128, HT])
    suc = pool.tile([128, HT])
    xk7 = [pool.tile([128, SL], BF16) for _ in range(KT)]
    wstage = [pool.tile([128, 11 * 128], I32) for _ in range(2)]
    wbf7 = [pool.tile([128, HT * 128], BF16) for _ in range(2)]
    upf = pool.tile([128, SL])
    gt = [pool.tile([128, 512]) for _ in range(2)]
    hb = [pool.tile([128, SL], BF16) for _ in range(2)]
    A(("y", 16, lambda y: y.dma_start(out=sgc, in_=d["sgc"][:]), [], [sgc]))
    A(("y", 16, lambda y: y.dma_start(out=suc, in_=d["suc"][:]), [], [suc]))

    wgv = d["wg"].rearrange("(k p) h -> p k h", p=128)
    wuv = d["wu"].rearrange("(k p) h -> p k h", p=128)
    wdv = d["wd"].rearrange("(k p) h -> p k h", p=128)
    wchunk = [0]
    wbuf7 = [0]

    def load_w7(view, nk, m, stages, bufs):
        wb = bufs[wbuf7[0] % 2]
        wbuf7[0] += 1
        k0 = 0
        while k0 < nk:
            kc = min(11, nk - k0)
            ws = stages[wchunk[0] % 2]
            wchunk[0] += 1
            A(("y", 16, lambda y, view=view, k0=k0, kc=kc, m=m, ws=ws: y.dma_start(
                out=ws[:, 0:kc * 128].rearrange("p (k c) -> p k c", k=kc),
                in_=view[:, k0:k0 + kc, m * 128:(m + 1) * 128]), [ws], [ws]))
            A(("s", 1, lambda s, k0=k0, kc=kc, ws=ws, wb=wb: s.activation(
                out=wb[:, k0 * 128:(k0 + kc) * 128], in_=ws[:, 0:kc * 128],
                func=AF.Copy, bias=-128.0), [ws, wb], [wb]))
            k0 += kc
        return wb

    for k in range(KT):
        A(("y", 16, lambda y, k=k: y.dma_start(
            out=xk7[k], in_=d["x1bf"][k * 128:(k + 1) * 128, :]),
          [xk7[k]], [xk7[k]]))
    rot7 = [0]
    for m in range(HT):
        wb = load_w7(wuv, KT, m, wstage, wbf7)
        for nn in range(NN):
            sl = slice(nn * 512, (nn + 1) * 512)
            PS = PROT4[rot7[0] % 4]
            rot7[0] += 1
            emit_mm_group(ctx, PS, [
                (lambda k=k, wb=wb: wb[:, k * 128:(k + 1) * 128],
                 lambda k=k, sl=sl: xk7[k][:, sl])
                for k in range(KT)], reads=[wb, PS] + xk7, writes=[PS])
            A(("s", 1, lambda s, m=m, sl=sl, PS=PS: s.activation(
                out=upf[:, sl], in_=PS, func=AF.Copy, scale=suc[:, m:m + 1]),
              [PS, suc, upf], [upf]))
        wb = load_w7(wgv, KT, m, wstage, wbf7)
        hbm = hb[m % 2]
        for nn in range(NN):
            sl = slice(nn * 512, (nn + 1) * 512)
            PS = PROT4[rot7[0] % 4]
            rot7[0] += 1
            gtb = gt[nn % 2]
            emit_mm_group(ctx, PS, [
                (lambda k=k, wb=wb: wb[:, k * 128:(k + 1) * 128],
                 lambda k=k, sl=sl: xk7[k][:, sl])
                for k in range(KT)], reads=[wb, PS] + xk7, writes=[PS])
            A(("s", 1, lambda s, m=m, PS=PS, gtb=gtb: s.activation(
                out=gtb, in_=PS, func=AF.Silu, scale=sgc[:, m:m + 1]),
              [PS, sgc, gtb], [gtb]))
            A(("v", 1, lambda v, sl=sl, gtb=gtb, hbm=hbm: v.tensor_mul(
                hbm[:, sl], gtb, upf[:, sl]), [gtb, upf, hbm], [hbm]))
        A(("g", 16, lambda g, m=m, hbm=hbm: g.dma_start(
            out=d["h7"][m * 128:(m + 1) * 128, :], in_=hbm), [hbm], []))
    barrier(ctx)
    pool.reset(BASE)
    sdc = pool.tile([128, KT])
    hpan = [pool.tile([128, 1024], BF16) for _ in range(HT)]
    wstage2 = [pool.tile([128, 11 * 128], I32) for _ in range(2)]
    wbf7b = [pool.tile([128, HT * 128], BF16) for _ in range(2)]
    xt7 = [pool.tile([128, 512]) for _ in range(2)]
    dnt = [pool.tile([128, 512]) for _ in range(2)]
    raw7 = [pool.tile([128, 512]) for _ in range(2)]
    A(("y", 16, lambda y: y.dma_start(out=sdc, in_=d["sdc"][:]), [], [sdc]))
    accum, fin, norm = _fln_emit(ctx, "f2", d["lnscr"], d["x2"], d["x2bf"],
                                 ones128, ones1, eps_ln, psA, psB, psC)
    for half in range(2):
        h0 = half * 1024
        for k in range(HT):
            A(("y", 16, lambda y, k=k, h0=h0: y.dma_start(
                out=hpan[k], in_=d["h7"][k * 128:(k + 1) * 128, h0:h0 + 1024]),
              [hpan[k]], [hpan[k]]))
        for m in range(KT):
            wb = load_w7(wdv, HT, m, wstage2, wbf7b)
            for nn in range(2):
                sl = slice(nn * 512, (nn + 1) * 512)
                gnn = half * 2 + nn
                gsl = slice(h0 + nn * 512, h0 + (nn + 1) * 512)
                jp = (m * 2 + nn) % 2
                PS = PROT4[rot7[0] % 4]
                rot7[0] += 1
                dntb = dnt[jp]
                emit_mm_group(ctx, PS, [
                    (lambda k=k, wb=wb: wb[:, k * 128:(k + 1) * 128],
                     lambda k=k, sl=sl: hpan[k][:, sl])
                    for k in range(HT)], reads=[wb, PS] + hpan, writes=[PS])
                A(("s", 1, lambda s, m=m, PS=PS, dntb=dntb: s.activation(
                    out=dntb, in_=PS, func=AF.Copy, scale=sdc[:, m:m + 1]),
                  [PS, sdc, dntb], [dntb]))
                A(("y", 16, lambda y, m=m, gsl=gsl, jp=jp: y.dma_start(
                    out=xt7[jp], in_=d["x1"][m * 128:(m + 1) * 128, gsl]),
                  [xt7[jp]], [xt7[jp]]))
                A(("v", 1, lambda v, jp=jp, dntb=dntb: v.tensor_add(
                    raw7[jp], xt7[jp], dntb), [xt7[jp], dntb, raw7[jp]], [raw7[jp]]))
                accum(m, gnn, raw7[jp])
    fin()
    norm()
    pool.reset(BASE)

    barrier(ctx)
    QB = 770
    xk8 = [pool.tile([128, SL], BF16) for _ in range(KT)]
    wst8 = [pool.tile([128, KT * 128]) for _ in range(2)]
    wbf8 = [pool.tile([128, KT * 128], BF16) for _ in range(2)]
    wqkvv = d["wqkv"].rearrange("(k p) m -> p k m", p=128)
    wgst = pool.tile([128, NH])
    wgbf = [pool.tile([128, NH], BF16) for _ in range(KT)]
    ob8 = [pool.tile([128, 512], BF16) for _ in range(4)]
    gb8 = pool.tile([NH, 512])
    gbbf = pool.tile([NH, 512], BF16)
    for k in range(KT):
        A(("y", 16, lambda y, k=k: y.dma_start(
            out=xk8[k], in_=d["x2bf"][k * 128:(k + 1) * 128, :]), [xk8[k]], [xk8[k]]))
        A(("y", 16, lambda y, k=k: y.dma_start(
            out=wgst, in_=d["wgates"][k * 128:(k + 1) * 128, :]), [wgst], [wgst]))
        A(("v", 1, lambda v, k=k: v.tensor_copy(out=wgbf[k], in_=wgst),
          [wgst, wgbf[k]], [wgbf[k]]))
    for nn in range(NN):
        sl = slice(nn * 512, (nn + 1) * 512)
        emit_mm_group(ctx, P0[0:NH, :], [
            (lambda k=k: wgbf[k], lambda k=k, sl=sl: xk8[k][:, sl])
            for k in range(KT)])
        A(("s", 1, lambda s: s.activation(out=gb8, in_=P0[0:NH, :], func=AF.Sigmoid)))
        A(("v", 1, lambda v: v.tensor_copy(out=gbbf, in_=gb8)))
        for h in range(NH):
            blk = h // 2
            ro = QB * blk + 768 + (h % 2)
            A(("g", 16, lambda g, h=h, ro=ro, sl=sl: g.dma_start(
                out=d["cb3"][ro:ro + 1, sl], in_=gbbf[h:h + 1, :])))
    rot8 = [0]
    for m in range(3 * KT):
        part = m // 16
        h = m % 16
        blk = h // 2
        ro = QB * blk + part * 256 + (h % 2) * 128
        mp = m % 2
        ws8, wb8 = wst8[mp], wbf8[mp]
        A(("y", 16, lambda y, m=m, ws8=ws8: y.dma_start(
            out=ws8.rearrange("p (k c) -> p k c", k=KT),
            in_=wqkvv[:, :, m * 128:(m + 1) * 128]), [ws8], [ws8]))
        A(("s", 1, lambda s, ws8=ws8, wb8=wb8: s.activation(
            out=wb8, in_=ws8, func=AF.Copy), [ws8, wb8], [wb8]))
        for nn in range(NN):
            sl = slice(nn * 512, (nn + 1) * 512)
            PS = PROT6[rot8[0] % 6]
            rot8[0] += 1
            ob = ob8[(m * NN + nn) % 4]
            emit_mm_group(ctx, PS, [
                (lambda k=k, wb8=wb8: wb8[:, k * 128:(k + 1) * 128],
                 lambda k=k, sl=sl: xk8[k][:, sl])
                for k in range(KT)], reads=[wb8, PS] + xk8, writes=[PS])
            A(("v", 1, lambda v, ob=ob, PS=PS: v.tensor_copy(out=ob, in_=PS),
              [PS, ob], [ob]))
            A(("g", 16, lambda g, ro=ro, sl=sl, ob=ob: g.dma_start(
                out=d["cb3"][ro:ro + 128, sl], in_=ob), [ob], []))
    def a2a(g):
        from concourse.aot_env import allow_ici_all_to_all
        with allow_ici_all_to_all():
            return g.collective_compute("AllToAll", ALU.bypass, G8,
                                        [d["cb3"]], [d["rs3"]])
    A(("g", 1, a2a))
    pool.reset(BASE)

    barrier(ctx)
    QB = 770
    QT = 1024
    identb = pool.tile([128, 128], BF16)
    mask = pool.tile([128, 2048])
    attc = pool.tile([128, 3])
    qh = pool.tile([128, LB], BF16)
    kh = pool.tile([128, LB], BF16)
    vh = pool.tile([128, LB], BF16)
    vt = pool.tile([128, LB], BF16)
    gl = pool.tile([1, LB])
    z9 = [pool.tile([128, QT]) for _ in range(2)]
    pbf9 = [pool.tile([128, QT], BF16) for _ in range(2)]
    linv = pool.tile([1, QT])
    coef = pool.tile([1, QT])
    cbc = pool.tile([128, QT])
    obf9 = pool.tile([128, QT], BF16)
    A(("y", 16, lambda y: y.dma_start(out=identb, in_=d["identb"][:]), [], [identb]))
    A(("y", 16, lambda y: y.dma_start(out=mask, in_=d["mask"][:]), [], [mask]))
    A(("y", 16, lambda y: y.dma_start(out=attc, in_=d["attc"][:]), [], [attc]))
    for i in range(2):
        for b in range(2):
            for r in range(4):
                rr = 4 * b + r
                csl = slice(r * SL, (r + 1) * SL)
                A(("y", 16, lambda y, rr=rr, i=i, csl=csl: y.dma_start(
                    out=qh[:, csl], in_=d["rs3"][rr * QB + i * 128:rr * QB + (i + 1) * 128, :]),
                  [qh], [qh]))
                A(("y", 16, lambda y, rr=rr, i=i, csl=csl: y.dma_start(
                    out=kh[:, csl], in_=d["rs3"][rr * QB + 256 + i * 128:rr * QB + 256 + (i + 1) * 128, :]),
                  [kh], [kh]))
                A(("y", 16, lambda y, rr=rr, i=i, csl=csl: y.dma_start(
                    out=vh[:, csl], in_=d["rs3"][rr * QB + 512 + i * 128:rr * QB + 512 + (i + 1) * 128, :]),
                  [vh], [vh]))
                A(("g", 16, lambda g, rr=rr, i=i, csl=csl: g.dma_start(
                    out=gl[:, csl], in_=d["rs3"][rr * QB + 768 + i:rr * QB + 768 + i + 1, :]),
                  [gl], [gl]))
            for kt in range(LB // 128):
                ksl = slice(kt * 128, (kt + 1) * 128)
                A(("t", 1, lambda t, ksl=ksl: t.transpose(
                    psT[:, 0:128], vh[:, ksl], identb), [vh, identb, psT], [psT]))
                A(("v", 1, lambda v, ksl=ksl: v.tensor_copy(
                    out=vt[:, ksl], in_=psT[:, 0:128]), [psT, vt], [vt]))

            def emit_mms(kt, q0):
                ksl = slice(kt * 128, (kt + 1) * 128)
                A(("t", 1, lambda t, ksl=ksl, q0=q0: t.matmul(
                    psA[:, 0:512], kh[:, ksl], qh[:, q0:q0 + 512],
                    start=True, stop=True), [kh, qh, P0], [P0]))
                A(("t", 1, lambda t, ksl=ksl, q0=q0: t.matmul(
                    psA[:, 512:1024], kh[:, ksl], qh[:, q0 + 512:q0 + 1024],
                    start=True, stop=True), [kh, qh, P1], [P1]))

            for qt in range(LB // QT):
                q0 = qt * QT
                nkt = (qt + 1) * (QT // 128)
                emit_mms(0, q0)
                for kt in range(nkt):
                    ksl = slice(kt * 128, (kt + 1) * 128)
                    kp = kt % 2
                    zb9, pb9 = z9[kp], pbf9[kp]
                    rrel = kt - (nkt - QT // 128)
                    A(("s", 1, lambda s, zb9=zb9: s.activation(
                        out=zb9, in_=psA[:, 0:1024], func=AF.Tanh,
                        scale=attc[:, 0:1]), [P0, P1, attc, zb9], [zb9]))
                    if kt + 1 < nkt:
                        emit_mms(kt + 1, q0)
                    if rrel >= 0:
                        msl = slice(QT - 128 * rrel, 2 * QT - 128 * rrel)
                        A(("v", 1, lambda v, msl=msl, zb9=zb9: v.tensor_add(
                            zb9, zb9, mask[:, msl]), [zb9, mask], [zb9]))
                    A(("s", 1, lambda s, zb9=zb9, pb9=pb9: s.activation(
                        out=pb9, in_=zb9, func=AF.Exp, scale=attc[:, 1:2],
                        bias=attc[:, 2:3]), [zb9, attc, pb9], [pb9]))

                    def mmlo(t, kt=kt, ksl=ksl, nkt=nkt, pb9=pb9):
                        t.matmul(psC[0:1, 0:512], ones128b, pb9[:, 0:512],
                                 start=(kt == 0), stop=(kt == nkt - 1))
                        t.matmul(psC[0:1, 512:1024], ones128b, pb9[:, 512:1024],
                                 start=(kt == 0), stop=(kt == nkt - 1))
                        t.matmul(psB[:, 0:512], vt[:, ksl], pb9[:, 0:512],
                                 start=(kt == 0), stop=(kt == nkt - 1))
                        return t.matmul(psB[:, 512:1024], vt[:, ksl], pb9[:, 512:1024],
                                        start=(kt == 0), stop=(kt == nkt - 1))
                    A(("t", 1, mmlo, [ones128b, vt, pb9, psB, psC], [psB, psC]))

                def lchain(s):
                    s.activation(out=linv, in_=psC[0:1, 0:1024], func=AF.Ln)
                    return s.activation(out=linv, in_=linv, func=AF.Exp, scale=-1.0)
                A(("s", 1, lchain, [psC, linv], [linv]))
                A(("v", 1, lambda v, q0=q0: v.tensor_mul(
                    coef, linv, gl[:, q0:q0 + QT]), [linv, gl, coef], [coef]))

                def mmc(t):
                    t.matmul(psC[:, 0:512], ones1, coef[:, 0:512], start=True, stop=True)
                    return t.matmul(psC[:, 512:1024], ones1, coef[:, 512:1024],
                                    start=True, stop=True)
                A(("t", 1, mmc, [ones1, coef, psC], [psC]))

                def ochain(v):
                    v.tensor_copy(out=cbc, in_=psC[:, 0:1024])
                    return v.tensor_mul(obf9, psB[:, 0:1024], cbc)
                A(("v", 1, ochain, [psC, psB, cbc, obf9], [cbc, obf9]))
                A(("g", 16, lambda g, i=i, b=b, q0=q0: g.dma_start(
                    out=d["o"][i * 128:(i + 1) * 128, b * LB + q0:b * LB + q0 + QT],
                    in_=obf9), [obf9], []))
    pool.reset(BASE)

    barrier(ctx)
    wo = pool.tile([128, DM])
    wobf = [pool.tile([128, DM], BF16) for _ in range(2)]
    ok10 = [pool.tile([128, 2 * 512], BF16) for _ in range(2)]
    ofv = d["o"].rearrange("(k p) t -> p k t", p=128)
    ev10 = [pool.tile([128, 512]) for _ in range(2)]
    for k in range(2):
        A(("y", 16, lambda y, k=k: y.dma_start(
            out=wo, in_=d["wouth"][k * 128:(k + 1) * 128, :]), [wo], [wo]))
        A(("v", 1, lambda v, k=k: v.tensor_copy(out=wobf[k], in_=wo),
          [wo, wobf[k]], [wobf[k]]))
    for r in range(NC):
        for nn in range(NN):
            jp = (r * NN + nn) % 2
            t0r = r * SL + nn * 512
            A(("y", 16, lambda y, t0r=t0r, jp=jp: y.dma_start(
                out=ok10[jp].rearrange("p (k c) -> p k c", k=2),
                in_=ofv[:, :, t0r:t0r + 512]), [ok10[jp]], [ok10[jp]]))
            for m in range(KT):
                mp = m % 2
                emit_mm_group(ctx, P0, [
                    (lambda k=k, m=m: wobf[k][:, m * 128:(m + 1) * 128],
                     lambda k=k, jp=jp: ok10[jp][:, k * 512:(k + 1) * 512])
                    for k in range(2)],
                    reads=wobf + [ok10[jp], psA], writes=[psA])
                A(("v", 1, lambda v, mp=mp: v.tensor_copy(out=ev10[mp], in_=P0),
                  [psA, ev10[mp]], [ev10[mp]]))
                A(("g", 16, lambda g, r=r, m=m, nn=nn, mp=mp: g.dma_start(
                    out=d["cb10"][r * DM + m * 128:r * DM + (m + 1) * 128,
                                  nn * 512:(nn + 1) * 512], in_=ev10[mp]), [ev10[mp]], []))
    A(("g", 1, lambda g: g.collective_compute(
        "ReduceScatter", ALU.add, G8, [d["cb10"]], [d["rs10"]])))
    pool.reset(BASE)
    accum, fin, norm = _fln_emit(ctx, "f3", d["lnscr"], d["x3"], d["x3bf"],
                                 ones128, ones1, eps_ln, psA, psB, psC)
    xt10 = [pool.tile([128, 512]) for _ in range(2)]
    raw10 = [pool.tile([128, 512]) for _ in range(2)]
    for m in range(KT):
        for nn in range(NN):
            jp = (m * NN + nn) % 2
            sl = slice(nn * 512, (nn + 1) * 512)
            msl = slice(m * 128, (m + 1) * 128)
            A(("y", 16, lambda y, msl=msl, sl=sl, jp=jp: y.dma_start(
                out=xt10[jp], in_=d["rs10"][msl, sl]), [xt10[jp]], [xt10[jp]]))
            A(("y", 16, lambda y, msl=msl, sl=sl, jp=jp: y.dma_start(
                out=raw10[jp], in_=d["x2"][msl, sl]), [raw10[jp]], [raw10[jp]]))
            A(("v", 1, lambda v, jp=jp: v.tensor_add(raw10[jp], raw10[jp], xt10[jp]),
              [raw10[jp], xt10[jp]], [raw10[jp]]))
            accum(m, nn, raw10[jp])
    fin()
    norm()
    barrier(ctx)
    rwst = pool.tile([128, NEXP])
    rwbf = [pool.tile([128, NEXP], BF16) for _ in range(KT)]
    rbs = pool.tile([1, NEXP])
    rbc = pool.tile([128, NEXP])
    x3p = [pool.tile([128, KT * 128], BF16) for _ in range(2)]
    x3bv = d["x3bf"].rearrange("(k p) t -> p k t", p=128)
    rt10 = [pool.tile([128, NEXP]) for _ in range(2)]
    mx = pool.tile([128, 1])
    sm = pool.tile([128, 1])
    smr = pool.tile([128, 1])
    st10 = [pool.tile([128, NEXP]) for _ in range(2)]
    A(("y", 16, lambda y: y.dma_start(out=rbs, in_=d["rb"][:]), [], [rbs]))
    for k in range(KT):
        A(("y", 16, lambda y, k=k: y.dma_start(
            out=rwst, in_=d["rw"][k * 128:(k + 1) * 128, :]), [rwst], [rwst]))
        A(("v", 1, lambda v, k=k: v.tensor_copy(out=rwbf[k], in_=rwst),
          [rwst, rwbf[k]], [rwbf[k]]))
    A(("t", 1, lambda t: t.matmul(P1[:, 0:NEXP], ones1, rbs, start=True, stop=True),
      [ones1, rbs, psA], [psA]))
    A(("v", 1, lambda v: v.tensor_copy(out=rbc, in_=P1[:, 0:NEXP]), [psA, rbc], [rbc]))
    for tt in range(SL // 128):
        tp10 = tt % 2
        A(("y", 16, lambda y, tt=tt, tp10=tp10: y.dma_start(
            out=x3p[tp10].rearrange("p (k c) -> p k c", k=KT),
            in_=x3bv[:, :, tt * 128:(tt + 1) * 128]), [x3p[tp10]], [x3p[tp10]]))

        def mmr(t, tp10=tp10):
            inst = None
            for k in range(KT):
                inst = t.matmul(P0[:, 0:NEXP], x3p[tp10][:, k * 128:(k + 1) * 128],
                                rwbf[k], start=(k == 0), stop=(k == KT - 1))
            return inst
        A(("t", 1, mmr, [x3p[tp10], psA] + rwbf, [psA]))
        A(("v", 1, lambda v, tp10=tp10: v.tensor_add(rt10[tp10], P0[:, 0:NEXP], rbc),
          [psA, rbc, rt10[tp10]], [rt10[tp10]]))
        A(("v", 1, lambda v, tp10=tp10: v.reduce_max(mx, rt10[tp10], axis=AX.X),
          [rt10[tp10], mx], [mx]))
        A(("v", 1, lambda v, tp10=tp10: v.tensor_sub(
            rt10[tp10], rt10[tp10], mx.broadcast_to([128, NEXP])),
          [rt10[tp10], mx], [rt10[tp10]]))
        A(("s", 1, lambda s, tp10=tp10: s.activation(
            out=rt10[tp10], in_=rt10[tp10], func=AF.Exp), [rt10[tp10]], [rt10[tp10]]))
        A(("v", 1, lambda v, tp10=tp10: v.reduce_sum(sm, rt10[tp10], axis=AX.X),
          [rt10[tp10], sm], [sm]))

        def smchain(s):
            s.activation(out=smr, in_=sm, func=AF.Ln)
            return s.activation(out=smr, in_=smr, func=AF.Exp, scale=-1.0)
        A(("s", 1, smchain, [sm, smr], [smr]))
        A(("v", 1, lambda v, tp10=tp10: v.tensor_mul(
            st10[tp10], rt10[tp10], smr.broadcast_to([128, NEXP])),
          [rt10[tp10], smr, st10[tp10]], [st10[tp10]]))
        A(("g", 16, lambda g, tt=tt, tp10=tp10: g.dma_start(
            out=d["sslab"][tt * 128:(tt + 1) * 128, :], in_=st10[tp10]),
          [st10[tp10]], []))
    A(("g", 1, lambda g: g.collective_compute(
        "AllGather", ALU.bypass, G8, [d["sslab"]], [d["sgath"]])))
    pool.reset(BASE)

    barrier(ctx)
    SW = TT * NEXP // 128
    SWL = SL * NEXP // 128
    NBIS = 35
    kval = pool.tile([1, NEXP])
    ssb = pool.tile([128, SW])
    msk = pool.tile([128, SW])
    thr = pool.tile([1, SW])
    cacc = pool.tile([1, SW])
    lo = pool.tile([1, NEXP])
    hi = pool.tile([1, NEXP])
    mid = pool.tile([1, NEXP])
    cnt = pool.tile([1, NEXP])
    cge = pool.tile([1, NEXP])
    tmp16 = pool.tile([1, NEXP])
    sslab = pool.tile([128, SWL])
    thr2 = pool.tile([1, SWL])
    msk2 = pool.tile([128, SWL])
    wtm = pool.tile([128, 16])
    wrow = pool.tile([1, SL])
    rhs11 = [pool.tile([128, SL], BF16) for _ in range(KT)]
    wst11 = pool.tile([128, KT * 128])
    wbf11v = pool.tile([128, KT * 128], BF16)
    ewv = d["ew"].rearrange("(k p) m -> p k m", p=128)
    ebc = pool.tile([128, KT])
    zb = pool.tile([128, 512])
    xt11 = pool.tile([128, 512])
    raw11 = pool.tile([128, 512])
    wbcs = pool.tile([128, 512])
    A(("y", 16, lambda y: y.dma_start(out=kval, in_=d["kval"][:])))
    A(("y", 16, lambda y: y.dma_start(
        out=ssb, in_=d["sgath"].rearrange("(p w) e -> p (w e)", p=128))))
    A(("y", 16, lambda y: y.dma_start(
        out=sslab, in_=d["sslab"].rearrange("(p w) e -> p (w e)", p=128))))
    A(("y", 16, lambda y: y.dma_start(out=ebc, in_=d["ebc"][:])))
    A(("y", 16, lambda y: y.dma_start(out=lo, in_=d["lohi"][:, 0:NEXP])))
    A(("y", 16, lambda y: y.dma_start(out=hi, in_=d["lohi"][:, NEXP:2 * NEXP])))
    for k in range(KT):
        A(("y", 16, lambda y, k=k: y.dma_start(
            out=rhs11[k], in_=d["x3bf"][k * 128:(k + 1) * 128, :])))

    def count_pass(thr_src):
        A(("v", 1, lambda v: v.tensor_copy(
            out=thr.rearrange("o (t e) -> o t e", e=NEXP),
            in_=thr_src.unsqueeze(1).broadcast_to([1, SW // NEXP, NEXP]))))
        for ch in range(SW // 512):
            sl = slice(ch * 512, (ch + 1) * 512)
            A(("t", 1, lambda t, sl=sl: t.matmul(
                P2, ones1, thr[:, sl], start=True, stop=True)))
            A(("v", 1, lambda v, sl=sl: v.tensor_tensor(
                out=msk[:, sl], in0=ssb[:, sl], in1=P2, op=ALU.is_ge)))
            A(("t", 1, lambda t, sl=sl: t.matmul(
                P3[0:1, :], ones128, msk[:, sl], start=True, stop=True)))
            A(("v", 1, lambda v, sl=sl: v.tensor_copy(out=cacc[:, sl], in_=P3[0:1, :])))
        A(("v", 1, lambda v: v.reduce_sum(
            cnt, cacc.rearrange("o (t e) -> o e t", e=NEXP), axis=AX.X)))

    for it in range(NBIS):
        A(("v", 1, lambda v: v.tensor_add(mid, lo, hi)))
        A(("v", 1, lambda v: v.tensor_scalar_mul(mid, mid, 0.5)))
        count_pass(mid)
        A(("v", 1, lambda v: v.tensor_tensor(out=cge, in0=cnt, in1=kval, op=ALU.is_ge)))
        A(("v", 1, lambda v: v.tensor_sub(tmp16, mid, lo)))
        A(("v", 1, lambda v: v.tensor_mul(tmp16, tmp16, cge)))
        A(("v", 1, lambda v: v.tensor_add(lo, lo, tmp16)))
        A(("v", 1, lambda v: v.tensor_sub(tmp16, hi, mid)))
        A(("v", 1, lambda v: v.tensor_mul(tmp16, tmp16, cge)))
        A(("v", 1, lambda v: v.tensor_add(hi, mid, tmp16)))
    A(("v", 1, lambda v: v.tensor_copy(
        out=thr2.rearrange("o (t e) -> o t e", e=NEXP),
        in_=lo.unsqueeze(1).broadcast_to([1, SWL // NEXP, NEXP]))))
    A(("t", 1, lambda t: t.matmul(P2[:, 0:256], ones1, thr2, start=True, stop=True)))
    A(("v", 1, lambda v: v.tensor_tensor(
        out=msk2, in0=sslab, in1=P2[:, 0:256], op=ALU.is_ge)))
    A(("v", 1, lambda v: v.tensor_mul(msk2, msk2, sslab)))
    A(("v", 1, lambda v: v.reduce_sum(
        wtm, msk2.rearrange("p (t e) -> p t e", e=NEXP), axis=AX.X)))
    A(("g", 16, lambda g: g.dma_start(
        out=d["w"].rearrange("o (p t) -> p (o t)", p=128), in_=wtm)))
    A(("y", 16, lambda y: y.dma_start(out=wrow, in_=d["w"][0:1, :])))
    accum, fin, norm = _fln_emit(ctx, "f4", d["lnscr"], d["x4"], None,
                                 ones128, ones1, eps_ln, psA, psB, psC)
    for m in range(KT):
        A(("y", 16, lambda y, m=m: y.dma_start(
            out=wst11.rearrange("p (k c) -> p k c", k=KT),
            in_=ewv[:, :, m * 128:(m + 1) * 128])))
        A(("v", 1, lambda v: v.tensor_copy(out=wbf11v, in_=wst11)))
        for nn in range(NN):
            sl = slice(nn * 512, (nn + 1) * 512)
            emit_mm_group(ctx, P0, [
                (lambda k=k: wbf11v[:, k * 128:(k + 1) * 128],
                 lambda k=k, sl=sl: rhs11[k][:, sl])
                for k in range(KT)])
            A(("v", 1, lambda v, m=m: v.tensor_scalar_add(zb, P0, ebc[:, m:m + 1])))
            A(("t", 1, lambda t, sl=sl: t.matmul(
                P4, ones1, wrow[:, sl], start=True, stop=True)))
            A(("v", 1, lambda v: v.tensor_copy(out=wbcs, in_=P4)))
            A(("v", 1, lambda v: v.tensor_mul(zb, zb, wbcs)))
            A(("y", 16, lambda y, m=m, sl=sl: y.dma_start(
                out=xt11, in_=d["x3"][m * 128:(m + 1) * 128, sl])))
            A(("v", 1, lambda v: v.tensor_add(raw11, xt11, zb)))
            accum(m, nn, raw11)
    fin()
    norm()


def build_fused():
    nc = _bass()
    f = F32
    def di(name, shape, dt=F32):
        return nc.dram_tensor(name, shape, dt, kind="ExternalInput").ap()

    xT_d = di("xT", [DM, TT])
    win_d = di("win", [DM, 2 * DS])
    convw_d = di("convw", [128, 16])
    convb_d = di("convb", [128, 4])
    xpw_d = di("xpw", [DS, 160])
    dpwc_d = di("dpwc", [128, DS])
    dpbc_d = di("dpbc", [128, DS // 128])
    alog_d = di("alog", [128, 64])
    dcol_d = di("dcol", [128, 4])
    opwc_d = di("opwc", [DS, DM])
    xslab_d = di("xslab", [DM, SL])
    wg_d = di("wgT", [DM, HID], I32)
    wu_d = di("wuT", [DM, HID], I32)
    wd_d = di("wdT", [HID, DM], I32)
    sgc_d = di("sgcol", [128, HID // 128])
    suc_d = di("sucol", [128, HID // 128])
    sdc_d = di("sdcol", [128, DM // 128])
    wqkv_d = di("wqkv", [DM, 3 * DM])
    wgates_d = di("wgates", [DM, NH])
    wouth_d = di("wouth", [256, DM])
    rw_d = di("rw", [DM, NEXP])
    rb_d = di("rb", [1, NEXP])
    ew_d = di("ew", [DM, DM])
    ebc_d = di("ebcol", [128, DM // 128])
    kval_d = di("kval", [1, NEXP])
    lohi_d = di("lohi", [1, 2 * NEXP])
    ones128_d = di("ones128", [128, 1])
    ones128b_d = di("ones128b", [128, 1], BF16)
    ones1_d = di("ones1", [1, 128])
    epsv_d = di("epsv", [128, 2])
    attc_d = di("attc", [128, 3])
    mask_d = di("mask", [128, 2048])
    identb_d = di("identb", [128, 128], BF16)
    onesbc_d = di("onesbc", [32, 2])
    sel2_d2 = di("sel2t", [2, 32])
    x4_d = nc.dram_tensor("x4", [DM, SL], F32, kind="ExternalOutput").ap()

    def dn(name, shape, dt=F32):
        return nc.dram_tensor(name, shape, dt, kind="Internal").ap()

    hid_d = dn("hid", [DS, TT])
    sg_d = dn("sg", [DS, TT])
    u_d = dn("u", [DS, TT])
    ubf_d = dn("ubf", [DS, TT], BF16)
    pssm_d = dn("pssm", [160, TT])
    ssm_d = dn("ssm", [160, TT])
    dt_d = dn("dtf", [DS, TT])
    bcrow_d = dn("bcrow", [32, TT])
    yfin_d = dn("yfinf", [DS, TT], BF16)
    cb6_d = dn("cb6", [NC * DM, SL])
    rs6_d = dn("rs6", [DM, SL])
    lnscr_d = dn("lnscr", [DM, SL])
    x1_d = dn("x1f", [DM, SL])
    x1bf_d = dn("x1bff", [DM, SL], BF16)
    x2_d = dn("x2f", [DM, SL])
    x2bf_d = dn("x2bff", [DM, SL], BF16)
    h7_d = dn("h7f", [HID, SL], BF16)
    QB = 770
    cb3_d = dn("cb3", [8 * QB, SL], BF16)
    rs3_d = dn("rs3", [8 * QB, SL], BF16)
    o_dd = dn("Of", [256, TT], BF16)
    cb10_d = dn("cb10", [8 * DM, SL])
    rs10_d = dn("rs10", [DM, SL])
    x3_d = dn("x3f", [DM, SL])
    x3bf_d = dn("x3bff", [DM, SL], BF16)
    sslab_d = dn("sslabf", [SL, NEXP])
    sgath_d = dn("sgath", [TT, NEXP])
    w_d = dn("wtokf", [1, SL])

    G8 = [list(range(8))]
    GB = [[0, 1, 2, 3], [4, 5, 6, 7]]

    with ExitStack() as es:
        e = es.enter_context
        pool = Pool(nc, es, kb_per_partition=204)
        psA = e(nc.psum_tensor("psA", [128, 1024], F32))
        psB = e(nc.psum_tensor("psB", [128, 1024], F32))
        psC = e(nc.psum_tensor("psC", [128, 1024], F32))
        psT = e(nc.psum_tensor("psT", [128, 1024], BF16))
        P32 = [psA[:, 0:512], psA[:, 512:1024], psB[:, 0:512],
               psB[:, 512:1024], psC[:, 0:512], psC[:, 512:1024]]
        ops = []
        ctx = Ctx(nc, es, ops, pool, None)
        A = ops.append

        ones128 = pool.tile([128, 1])
        ones128b = pool.tile([128, 1], BF16)
        ones1 = pool.tile([1, 128])
        epsv = pool.tile([128, 2])
        A(("y", 16, lambda y: y.dma_start(out=ones128, in_=ones128_d[:])))
        A(("y", 16, lambda y: y.dma_start(out=ones128b, in_=ones128b_d[:])))
        A(("y", 16, lambda y: y.dma_start(out=ones1, in_=ones1_d[:])))
        A(("y", 16, lambda y: y.dma_start(out=epsv, in_=epsv_d[:])))
        BASE = pool.mark()
        eps_rms = epsv[0:1, 0:1]
        eps_ln = epsv[0:1, 1:2]

        KT, MT8, NO, NN4 = DM // 128, 8, 8, 4
        wst = [pool.tile([128, 2 * DS]) for _ in range(2)]
        wbf = [pool.tile([128, 2 * DS], BF16) for _ in range(KT)]
        xst = [pool.tile([128, 2048]) for _ in range(2)]
        xbf = [pool.tile([128, 2048], BF16) for _ in range(KT)]
        ev = [pool.tile([128, 1024]) for _ in range(3)]
        PS3 = [psA, psB, psC]
        for k in range(KT):
            ws1 = wst[k % 2]
            A(("y", 16, lambda y, k=k, ws1=ws1: y.dma_start(
                out=ws1, in_=win_d[k * 128:(k + 1) * 128, :]), [ws1], [ws1]))
            A(("v", 1, lambda v, k=k, ws1=ws1: v.tensor_copy(out=wbf[k], in_=ws1),
              [ws1, wbf[k]], [wbf[k]]))
        rot1 = [0]
        for no in range(NO):
            for k in range(KT):
                xs1 = xst[k % 2]
                A(("y", 16, lambda y, k=k, no=no, xs1=xs1: y.dma_start(
                    out=xs1, in_=xT_d[k * 128:(k + 1) * 128, no * 2048:(no + 1) * 2048]),
                  [xs1], [xs1]))
                A(("v", 1, lambda v, k=k, xs1=xs1: v.tensor_copy(out=xbf[k], in_=xs1),
                  [xs1, xbf[k]], [xbf[k]]))
            for m in range(MT8):
                for n2 in range(2):
                    ps = PS3[rot1[0] % 3]
                    evt = ev[rot1[0] % 3]
                    rot1[0] += 1

                    def mmg(t, m=m, n2=n2, ps=ps):
                        inst = None
                        for half in range(2):
                            for k in range(KT):
                                inst = t.matmul(
                                    ps[:, half * 512:(half + 1) * 512],
                                    wbf[k][:, m * 128:(m + 1) * 128],
                                    xbf[k][:, n2 * 1024 + half * 512:n2 * 1024 + (half + 1) * 512],
                                    start=(k == 0), stop=(k == KT - 1))
                        return inst
                    A(("t", 1, mmg, wbf + xbf + [ps], [ps]))
                    func = AF.Copy if m < 4 else AF.Silu
                    A(("s", 1, lambda s, func=func, ps=ps, evt=evt: s.activation(
                        out=evt, in_=ps[:, 0:1024], func=func), [ps, evt], [evt]))
                    if m < 4:
                        dst = hid_d[m * 128:(m + 1) * 128,
                                    no * 2048 + n2 * 1024: no * 2048 + (n2 + 1) * 1024]
                    else:
                        dst = sg_d[(m - 4) * 128:(m - 3) * 128,
                                   no * 2048 + n2 * 1024: no * 2048 + (n2 + 1) * 1024]
                    A(("g", 16, lambda g, dst=dst, evt=evt: g.dma_start(
                        out=dst, in_=evt), [evt], []))
        barrier(ctx)
        pool.reset(BASE)

        CH = 4096
        cjobs = [(d, b, hf) for d in range(4) for b in range(2) for hf in range(2)]
        cw = pool.tile([128, 16])
        cb = pool.tile([128, 4])
        hh = pool.tile([128, CH + 4])
        acc = pool.tile([128, CH])
        uo = pool.tile([128, CH])
        ub = pool.tile([128, CH], BF16)
        A(("y", 16, lambda y: y.dma_start(out=cw, in_=convw_d[:])))
        A(("y", 16, lambda y: y.dma_start(out=cb, in_=convb_d[:])))
        for (d, b, hf) in cjobs:
            t0 = b * LB + hf * CH
            if hf == 0:
                A(("g", 1, lambda g: g.memset(hh[:, 0:4], 0.0)))
                A(("y", 16, lambda y, d=d, t0=t0: y.dma_start(
                    out=hh[:, 4:4 + CH], in_=hid_d[d * 128:(d + 1) * 128, t0:t0 + CH])))
            else:
                A(("y", 16, lambda y, d=d, t0=t0: y.dma_start(
                    out=hh[:, 1:4 + CH], in_=hid_d[d * 128:(d + 1) * 128, t0 - 3:t0 + CH])))
            def conv_fn(v, d=d):
                v.tensor_scalar_mul(acc, hh[:, 1:1 + CH], cw[:, d * 4 + 0: d * 4 + 1])
                inst = None
                for tap in range(1, KC):
                    inst = v.scalar_tensor_tensor(
                        out=acc, in0=hh[:, 1 + tap:1 + tap + CH],
                        scalar=cw[:, d * 4 + tap: d * 4 + tap + 1],
                        in1=acc, op0=ALU.mult, op1=ALU.add)
                return inst
            A(("v", 1, conv_fn))
            A(("s", 1, lambda s, d=d: s.activation(out=uo, in_=acc, func=AF.Silu,
                                                   bias=cb[:, d:d + 1])))
            A(("g", 16, lambda g, d=d, t0=t0: g.dma_start(
                out=u_d[d * 128:(d + 1) * 128, t0:t0 + CH], in_=uo)))
            A(("v", 1, lambda v: v.tensor_copy(out=ub, in_=uo)))
            A(("g", 16, lambda g, d=d, t0=t0: g.dma_start(
                out=ubf_d[d * 128:(d + 1) * 128, t0:t0 + CH], in_=ub)))
        pool.reset(BASE)

        KT4 = 4
        xwst = pool.tile([128, 160])
        xwbf = [pool.tile([128, 160], BF16) for _ in range(KT4)]
        ust = [pool.tile([128, 512], BF16) for _ in range(KT4)]
        ev1 = pool.tile([128, 512])
        ev2 = pool.tile([32, 512])
        for k in range(KT4):
            A(("y", 16, lambda y, k=k: y.dma_start(out=xwst, in_=xpw_d[k * 128:(k + 1) * 128, :])))
            A(("v", 1, lambda v, k=k: v.tensor_copy(out=xwbf[k], in_=xwst)))
        for nn in range(TT // 512):
            for k in range(KT4):
                A(("y", 16, lambda y, k=k, nn=nn: y.dma_start(
                    out=ust[k], in_=ubf_d[k * 128:(k + 1) * 128, nn * 512:(nn + 1) * 512])))
            emit_mm_group(ctx, P32[0], [
                (lambda k=k: xwbf[k][:, 0:128], lambda k=k: ust[k]) for k in range(KT4)])
            emit_mm_group(ctx, P32[1][0:32, :], [
                (lambda k=k: xwbf[k][:, 128:160], lambda k=k: ust[k]) for k in range(KT4)])
            A(("v", 1, lambda v: v.tensor_copy(out=ev1, in_=P32[0])))
            A(("v", 1, lambda v: v.tensor_copy(out=ev2, in_=P32[1][0:32, :])))
            A(("g", 16, lambda g, nn=nn: g.dma_start(
                out=pssm_d[0:128, nn * 512:(nn + 1) * 512], in_=ev1)))
            A(("g", 16, lambda g, nn=nn: g.dma_start(
                out=pssm_d[128:160, nn * 512:(nn + 1) * 512], in_=ev2)))
        A(("g", 1, lambda g: g.collective_compute(
            "AllReduce", ALU.add, G8, [pssm_d], [ssm_d])))
        pool.reset(BASE)

        dtnbf = pool.tile([128, TT], BF16)
        bcn = pool.tile([32, 2048])
        sdt = pool.tile([128, 2048])
        sbc = pool.tile([32, 2048])
        sq4 = pool.tile([128, 2048])
        sqb = pool.tile([32, 2048])
        mrow = pool.tile([1, 2048])
        rrow = pool.tile([1, 2048])
        m2b = pool.tile([2, 2048])
        onesbc = pool.tile([32, 2])
        sel2t = pool.tile([2, 32])
        A(("y", 16, lambda y: y.dma_start(out=onesbc, in_=onesbc_d[:])))
        A(("y", 16, lambda y: y.dma_start(out=sel2t, in_=sel2_d2[:])))
        for tc in range(TT // 2048):
            t0 = tc * 2048
            A(("y", 16, lambda y, t0=t0: y.dma_start(out=sdt, in_=ssm_d[0:128, t0:t0 + 2048])))
            A(("y", 16, lambda y, t0=t0: y.dma_start(out=sbc, in_=ssm_d[128:160, t0:t0 + 2048])))
            A(("s", 1, lambda s: s.activation(out=sq4, in_=sdt, func=AF.Square)))
            for ch in range(4):
                sl = slice(ch * 512, (ch + 1) * 512)
                A(("t", 1, lambda t, sl=sl: t.matmul(P32[0][0:1, :], ones128, sq4[:, sl], start=True, stop=True)))
                A(("v", 1, lambda v, sl=sl: v.tensor_copy(out=mrow[:, sl], in_=P32[0][0:1, :])))
            A(("s", 1, lambda s: s.activation(out=mrow, in_=mrow, func=AF.Ln, scale=1.0 / 128, bias=eps_rms)))
            A(("s", 1, lambda s: s.activation(out=rrow, in_=mrow, func=AF.Exp, scale=-0.5)))
            for ch in range(4):
                sl = slice(ch * 512, (ch + 1) * 512)
                A(("t", 1, lambda t, sl=sl: t.matmul(P32[1], ones1, rrow[:, sl], start=True, stop=True)))
                A(("v", 1, lambda v, sl=sl, t0=t0: v.tensor_tensor(
                    out=dtnbf[:, t0 + sl.start:t0 + sl.stop], in0=sdt[:, sl], in1=P32[1], op=ALU.mult)))
            A(("s", 1, lambda s: s.activation(out=sqb, in_=sbc, func=AF.Square)))
            for ch in range(4):
                sl = slice(ch * 512, (ch + 1) * 512)
                A(("t", 1, lambda t, sl=sl: t.matmul(P32[2][0:2, :], onesbc, sqb[:, sl], start=True, stop=True)))
                A(("v", 1, lambda v, sl=sl: v.tensor_copy(out=m2b[:, sl], in_=P32[2][0:2, :])))
            A(("s", 1, lambda s: s.activation(out=m2b, in_=m2b, func=AF.Ln, scale=1.0 / NSS, bias=epsv[0:2, 0:1])))
            A(("s", 1, lambda s: s.activation(out=m2b, in_=m2b, func=AF.Exp, scale=-0.5)))
            for ch in range(4):
                sl = slice(ch * 512, (ch + 1) * 512)
                A(("t", 1, lambda t, sl=sl: t.matmul(P32[3][0:32, :], sel2t, m2b[:, sl], start=True, stop=True)))
                A(("v", 1, lambda v, sl=sl: v.tensor_mul(bcn[:, sl], sbc[:, sl], P32[3][0:32, :])))
            A(("g", 16, lambda g, t0=t0: g.dma_start(out=bcrow_d[:, t0:t0 + 2048], in_=bcn)))
        dwbf = pool.tile([128, DS], BF16)
        dwst = pool.tile([128, DS])
        dpb4 = pool.tile([128, DS // 128])
        zr = pool.tile([128, 1024])
        za = pool.tile([128, 1024])
        dstg = pool.tile([128, 1024])
        A(("y", 16, lambda y: y.dma_start(out=dwst, in_=dpwc_d[:])))
        A(("v", 1, lambda v: v.tensor_copy(out=dwbf, in_=dwst)))
        A(("y", 16, lambda y: y.dma_start(out=dpb4, in_=dpbc_d[:])))
        for m in range(DS // 128):
            for nn in range(TT // 1024):
                sl = slice(nn * 1024, (nn + 1) * 1024)

                def mmdp(t, m=m, sl=sl):
                    t.matmul(psA[:, 0:512], dwbf[:, m * 128:(m + 1) * 128],
                             dtnbf[:, sl.start:sl.start + 512], start=True, stop=True)
                    return t.matmul(psA[:, 512:1024], dwbf[:, m * 128:(m + 1) * 128],
                                    dtnbf[:, sl.start + 512:sl.stop], start=True, stop=True)
                A(("t", 1, mmdp))
                A(("s", 1, lambda s, m=m: s.activation(
                    out=zr, in_=psA[:, 0:1024], func=AF.Relu, bias=dpb4[:, m:m + 1])))
                A(("s", 1, lambda s, m=m: s.activation(
                    out=za, in_=psA[:, 0:1024], func=AF.Abs, bias=dpb4[:, m:m + 1])))
                A(("s", 1, lambda s: s.activation(out=za, in_=za, func=AF.Exp, scale=-1.0)))
                A(("s", 1, lambda s: s.activation(out=za, in_=za, func=AF.Ln, bias=1.0)))
                A(("v", 1, lambda v: v.tensor_add(dstg, zr, za)))
                A(("g", 16, lambda g, m=m, sl=sl: g.dma_start(
                    out=dt_d[m * 128:(m + 1) * 128, sl], in_=dstg)))
        pool.reset(BASE)
        emit_scan(ctx, psA, psB, psC, dict(
            dt=dt_d, u=u_d, sg=sg_d, bc=bcrow_d, alog=alog_d, dcol=dcol_d,
            y=yfin_d, ones1=ones1), BASE)
        pool.reset(BASE)
        emit_tail(ctx, psA, psB, psC, psT, dict(
            yfin=yfin_d, opwc=opwc_d, cb6=cb6_d, rs6=rs6_d, xslab=xslab_d,
            lnscr=lnscr_d, x1=x1_d, x1bf=x1bf_d,
            wg=wg_d, wu=wu_d, wd=wd_d, sgc=sgc_d, suc=suc_d, sdc=sdc_d, h7=h7_d,
            x2=x2_d, x2bf=x2bf_d, wqkv=wqkv_d, wgates=wgates_d,
            cb3=cb3_d, rs3=rs3_d, o=o_dd, identb=identb_d, attc=attc_d,
            mask=mask_d, wouth=wouth_d, cb10=cb10_d, rs10=rs10_d,
            x3=x3_d, x3bf=x3bf_d, rw=rw_d, rb=rb_d, sslab=sslab_d,
            sgath=sgath_d, ew=ew_d, ebc=ebc_d, kval=kval_d, lohi=lohi_d,
            w=w_d, x4=x4_d),
            dict(ones128=ones128, ones128b=ones128b, ones1=ones1,
                 eps_ln=eps_ln), BASE, G8)
        run_serial(nc, es, ops)
    return nc


def scan_host_consts(c, A_log, D_param):
    ds = c * DS
    alog = np.zeros((128, 64), np.float32)
    for dg in range(4):
        alog[:, dg * 16:(dg + 1) * 16] = A_log[ds + dg * 128: ds + (dg + 1) * 128, :]
    dcol = np.zeros((128, 4), np.float32)
    for dg in range(4):
        dcol[:, dg] = D_param[ds + dg * 128: ds + (dg + 1) * 128]
    return {"alog": alog, "dcol": dcol}


def _run(nc, in_maps, names):
    res = run_bass_kernel_spmd(nc, in_maps, core_ids=list(range(NC)))
    return [[np.asarray(res.results[c][n]) for n in names] for c in range(NC)]


def kernel_unfused(**inp):
    f32 = np.float32
    x = np.ascontiguousarray(inp["L0_attn_x"].astype(f32, copy=False))
    xT = np.ascontiguousarray(x.reshape(TT, DM).T)
    ones128 = np.ones((128, 1), f32)
    ones1 = np.ones((1, 128), f32)
    ones128b = np.ones((128, 1), BF16NP)
    epsr = np.full((128, 1), RMS_EPS, f32)
    epsl = np.full((128, 1), LN_EPS, f32)

    def colarr(s, nt):
        a = np.zeros((128, nt), f32)
        for m in range(nt):
            a[:, m] = s[m * 128:(m + 1) * 128]
        return a

    W = inp["L0_attn_in_proj_w"].astype(f32, copy=False)
    maps = []
    for c in range(NC):
        ds = c * DS
        win = np.ascontiguousarray(
            np.concatenate([W[:, ds:ds + DS], W[:, DI + ds:DI + ds + DS]], axis=1))
        maps.append({"xT": xT, "win": win})
    r1 = _run(build_inproj(), maps, ["hid", "sg"])

    cw = inp["L0_attn_conv_w"].astype(f32, copy=False).reshape(DI, KC)
    cb = inp["L0_attn_conv_b"].astype(f32, copy=False)
    maps = []
    for c in range(NC):
        ds = c * DS
        convw = np.zeros((128, 16), f32)
        convb = np.zeros((128, 4), f32)
        for d in range(4):
            convw[:, d * 4:(d + 1) * 4] = cw[ds + d * 128: ds + (d + 1) * 128]
            convb[:, d] = cb[ds + d * 128: ds + (d + 1) * 128]
        maps.append({"hid": r1[c][0], "convw": convw, "convb": convb})
    r2 = _run(build_conv(), maps, ["u", "ubf"])

    xpw = inp["L0_attn_x_proj_w"].astype(f32, copy=False)
    maps = [{"ubf": r2[c][1], "xpw": np.ascontiguousarray(xpw[c * DS:(c + 1) * DS])}
            for c in range(NC)]
    r3 = _run(build_xproj(), maps, ["pdt", "pbc"])

    dpw = inp["L0_attn_dt_proj_w"].astype(f32, copy=False)
    dpb = colarr(inp["L0_attn_dt_proj_b"].astype(f32, copy=False), DI // 128)
    onesbc = np.zeros((32, 2), f32); onesbc[:16, 0] = 1; onesbc[16:, 1] = 1
    sel2 = np.zeros((2, 32), f32); sel2[0, :16] = 1; sel2[1, 16:] = 1
    maps = []
    for c in range(NC):
        t0 = c * SL
        pa = np.stack([r3[i][0][:, t0:t0 + SL] for i in range(NC)])
        pb = np.stack([r3[i][1][:, t0:t0 + SL] for i in range(NC)])
        maps.append({"pdt_all": np.ascontiguousarray(pa), "pbc_all": np.ascontiguousarray(pb),
                     "dpw": dpw, "dpb": dpb, "ones128": ones128, "ones1": ones1,
                     "onesbc": onesbc, "sel2": sel2, "epsv": epsr})
    r4 = _run(build_ssmmid(), maps, ["dt", "bc"])
    del r3

    dt_full = np.concatenate([r4[c][0] for c in range(NC)], axis=1)
    bc_full = np.concatenate([r4[c][1] for c in range(NC)], axis=1)
    del r4
    A_log = inp["L0_attn_A_log"].astype(f32, copy=False)
    D_param = inp["L0_attn_D_param"].astype(f32, copy=False)
    brow = np.ascontiguousarray(bc_full[:16])
    crow = np.ascontiguousarray(bc_full[16:])
    maps = []
    for c in range(NC):
        ds = c * DS
        m = scan_host_consts(c, A_log, D_param)
        m.update({"dt_c": np.ascontiguousarray(dt_full[ds:ds + DS]),
                  "u_c": r2[c][0], "sg_c": r1[c][1],
                  "brow": brow, "crow": crow})
        maps.append(m)
    r5 = _run(build_scan(), maps, ["yfin"])
    del dt_full, bc_full, r1, r2

    yfin = np.concatenate([r5[c][0] for c in range(NC)], axis=0)
    del r5
    opw = inp["L0_attn_out_proj_w"].astype(f32, copy=False)
    maps = []
    for c in range(NC):
        t0 = c * SL
        maps.append({"yslab": np.ascontiguousarray(yfin[:, t0:t0 + SL]),
                     "opw": opw, "xslab": np.ascontiguousarray(xT[:, t0:t0 + SL]),
                     "ones128": ones128, "ones1": ones1, "epsv": epsl})
    r6 = _run(build_outproj(), maps, ["x1", "x1bf"])
    del yfin

    wgT = np.ascontiguousarray(inp["L0_ffn_wq_gate"].T)
    wuT = np.ascontiguousarray(inp["L0_ffn_wq_up"].T)
    wdT = np.ascontiguousarray(inp["L0_ffn_wq_down"].T)
    maps = []
    for c in range(NC):
        maps.append({"x1": r6[c][0], "x1bf": r6[c][1],
                     "wgT": wgT, "wuT": wuT, "wdT": wdT,
                     "sgcol": colarr(inp["L0_ffn_s_gate"].astype(f32, copy=False), HID // 128),
                     "sucol": colarr(inp["L0_ffn_s_up"].astype(f32, copy=False), HID // 128),
                     "sdcol": colarr(inp["L0_ffn_s_down"].astype(f32, copy=False), DM // 128),
                     "ones128": ones128, "ones1": ones1, "epsv": epsl})
    r7 = _run(build_ffn0(), maps, ["x2", "x2bf"])
    del r6, wgT, wuT, wdT

    maps = [{"x2bf": r7[c][1],
             "wqkv": inp["L1_attn_w_qkv"].astype(f32, copy=False),
             "wg": inp["L1_attn_w_gates"].astype(f32, copy=False)}
            for c in range(NC)]
    r8 = _run(build_qkv(), maps, ["qkv", "g"])

    qkv = np.concatenate([r8[c][0] for c in range(NC)], axis=1)
    g = np.concatenate([r8[c][1] for c in range(NC)], axis=1)
    del r8
    mask = np.zeros((128, 2048), f32)
    jj = np.arange(-1024, 1024)
    kk = np.arange(128)
    mask[kk[:, None] > jj[None, :]] = -1e9
    attc = np.tile(np.array([[(DH ** -0.5) / 50.0, 50.0, -50.0]], f32), (128, 1))
    maps = []
    for c in range(NC):
        b, hg = c // 4, c % 4
        t0 = b * LB
        hs = [4 * hg + i for i in range(4)]
        Q = np.concatenate([qkv[h * 128:(h + 1) * 128, t0:t0 + LB] for h in hs], axis=0)
        Kf = np.concatenate([qkv[DM + h * 128:DM + (h + 1) * 128, t0:t0 + LB] for h in hs], axis=0)
        V = np.concatenate([qkv[2 * DM + h * 128:2 * DM + (h + 1) * 128, t0:t0 + LB] for h in hs], axis=0)
        maps.append({"Q": np.ascontiguousarray(Q), "Kf": np.ascontiguousarray(Kf),
                     "Vt": np.ascontiguousarray(V.T),
                     "grows": np.ascontiguousarray(g[4 * hg:4 * hg + 4, t0:t0 + LB]),
                     "mask": mask, "ones128b": ones128b, "ones1": ones1, "attc": attc})
    r9 = _run(build_attn(), maps, ["O"])
    del qkv, g

    O_full = np.zeros((DM, TT), BF16NP)
    for c in range(NC):
        b, hg = c // 4, c % 4
        t0 = b * LB
        for i in range(4):
            h = 4 * hg + i
            O_full[h * 128:(h + 1) * 128, t0:t0 + LB] = r9[c][0][i * 128:(i + 1) * 128]
    del r9
    maps = []
    for c in range(NC):
        t0 = c * SL
        maps.append({"Oslab": np.ascontiguousarray(O_full[:, t0:t0 + SL]),
                     "wout": inp["L1_attn_w_out"].astype(f32, copy=False),
                     "x2": r7[c][0],
                     "rw": inp["L1_ffn_router_w"].astype(f32, copy=False),
                     "rb": inp["L1_ffn_router_b"].astype(f32, copy=False).reshape(1, NEXP),
                     "ones128": ones128, "ones1": ones1, "epsv": epsl})
    r10 = _run(build_attnout(), maps, ["x3", "x3bf", "S"])
    del O_full, r7

    S_full = np.concatenate([r10[c][2] for c in range(NC)], axis=0)
    ebcol = colarr(inp["L1_ffn_expert_b"].astype(f32, copy=False), DM // 128)
    lohi = np.concatenate([np.zeros((1, NEXP), f32), np.ones((1, NEXP), f32)], axis=1)
    Sall = np.ascontiguousarray(S_full.reshape(128, TT * NEXP // 128))
    maps = []
    for c in range(NC):
        t0 = c * SL
        maps.append({"Sall": Sall,
                     "Sslab": np.ascontiguousarray(S_full[t0:t0 + SL].reshape(128, SL * NEXP // 128)),
                     "x3": r10[c][0], "x3bf": r10[c][1],
                     "ew": inp["L1_ffn_expert_w"].astype(f32, copy=False),
                     "ebcol": ebcol,
                     "kval": np.full((1, NEXP), float(KTOP), f32),
                     "lohi": lohi,
                     "ones128": ones128, "ones1": ones1, "epsv": epsl})
    r11 = _run(build_moe(), maps, ["x4"])
    del r10

    x4 = np.concatenate([r11[c][0] for c in range(NC)], axis=1)
    return np.ascontiguousarray(x4.T.reshape(2, LB, DM)).astype(np.float32)


def kernel(**inp):
    f32 = np.float32
    x = np.ascontiguousarray(inp["L0_attn_x"].astype(f32, copy=False))
    xT = np.ascontiguousarray(x.reshape(TT, DM).T)

    def colarr(s, nt):
        a = np.zeros((128, nt), f32)
        for m in range(nt):
            a[:, m] = s[m * 128:(m + 1) * 128]
        return a

    W = inp["L0_attn_in_proj_w"].astype(f32, copy=False)
    cw = inp["L0_attn_conv_w"].astype(f32, copy=False).reshape(DI, KC)
    cbv = inp["L0_attn_conv_b"].astype(f32, copy=False)
    xpw = inp["L0_attn_x_proj_w"].astype(f32, copy=False)
    dpw = inp["L0_attn_dt_proj_w"].astype(f32, copy=False)
    dpb = inp["L0_attn_dt_proj_b"].astype(f32, copy=False)
    opw = inp["L0_attn_out_proj_w"].astype(f32, copy=False)
    A_log = inp["L0_attn_A_log"].astype(f32, copy=False)
    D_param = inp["L0_attn_D_param"].astype(f32, copy=False)
    wgT = np.ascontiguousarray(inp["L0_ffn_wq_gate"].T)
    wuT = np.ascontiguousarray(inp["L0_ffn_wq_up"].T)
    wdT = np.ascontiguousarray(inp["L0_ffn_wq_down"].T)
    wqkv = inp["L1_attn_w_qkv"].astype(f32, copy=False)
    wgates = inp["L1_attn_w_gates"].astype(f32, copy=False)
    wout = inp["L1_attn_w_out"].astype(f32, copy=False)
    rw = inp["L1_ffn_router_w"].astype(f32, copy=False)
    rb = inp["L1_ffn_router_b"].astype(f32, copy=False).reshape(1, NEXP)
    ew = inp["L1_ffn_expert_w"].astype(f32, copy=False)
    ebcol = colarr(inp["L1_ffn_expert_b"].astype(f32, copy=False), DM // 128)

    epsv = np.zeros((128, 2), f32)
    epsv[:, 0] = RMS_EPS
    epsv[:, 1] = LN_EPS
    mask = np.zeros((128, 2048), f32)
    jj = np.arange(-1024, 1024)
    kk = np.arange(128)
    mask[kk[:, None] > jj[None, :]] = -1e9
    attc = np.tile(np.array([[(DH ** -0.5) / 50.0, 50.0, -50.0]], f32), (128, 1))
    onesbc = np.zeros((32, 2), f32); onesbc[:16, 0] = 1; onesbc[16:, 1] = 1
    sel2t = np.zeros((2, 32), f32); sel2t[0, :16] = 1; sel2t[1, 16:] = 1
    lohi = np.concatenate([np.zeros((1, NEXP), f32), np.ones((1, NEXP), f32)], axis=1)

    nc = build_fused()
    maps = []
    for c in range(NC):
        ds = c * DS
        t0 = c * SL
        m = scan_host_consts(c, A_log, D_param)
        win = np.ascontiguousarray(
            np.concatenate([W[:, ds:ds + DS], W[:, DI + ds:DI + ds + DS]], axis=1))
        convw = np.zeros((128, 16), f32)
        convb = np.zeros((128, 4), f32)
        for dd in range(4):
            convw[:, dd * 4:(dd + 1) * 4] = cw[ds + dd * 128: ds + (dd + 1) * 128]
            convb[:, dd] = cbv[ds + dd * 128: ds + (dd + 1) * 128]
        m.update({
            "xT": xT, "win": win, "convw": convw, "convb": convb,
            "xpw": np.ascontiguousarray(xpw[ds:ds + DS]),
            "dpwc": np.ascontiguousarray(dpw[:, ds:ds + DS]),
            "dpbc": colarr(dpb[ds:ds + DS], DS // 128),
            "opwc": np.ascontiguousarray(opw[ds:ds + DS]),
            "xslab": np.ascontiguousarray(xT[:, t0:t0 + SL]),
            "wgT": wgT, "wuT": wuT, "wdT": wdT,
            "sgcol": colarr(inp["L0_ffn_s_gate"].astype(f32, copy=False), HID // 128),
            "sucol": colarr(inp["L0_ffn_s_up"].astype(f32, copy=False), HID // 128),
            "sdcol": colarr(inp["L0_ffn_s_down"].astype(f32, copy=False), DM // 128),
            "wqkv": wqkv, "wgates": wgates,
            "wouth": np.ascontiguousarray(wout[2 * c * 128:2 * c * 128 + 256, :]),
            "rw": rw, "rb": rb, "ew": ew, "ebcol": ebcol,
            "kval": np.full((1, NEXP), float(KTOP), f32), "lohi": lohi,
            "ones128": np.ones((128, 1), f32),
            "ones128b": np.ones((128, 1), BF16NP),
            "ones1": np.ones((1, 128), f32),
            "epsv": epsv, "attc": attc, "mask": mask,
            "identb": np.eye(128).astype(BF16NP),
            "onesbc": onesbc, "sel2t": sel2t,
        })
        maps.append(m)
    res = run_bass_kernel_spmd(nc, maps, core_ids=list(range(NC)))
    x4 = np.concatenate([np.asarray(res.results[c]["x4"]) for c in range(NC)], axis=1)
    return np.ascontiguousarray(x4.T.reshape(2, LB, DM)).astype(np.float32)
